# version 11
# baseline (speedup 1.0000x reference)
import math
import numpy as np
from contextlib import ExitStack

import concourse.bass as bass
import concourse.mybir as mybir
from concourse.bass_utils import run_bass_kernel_spmd
from concourse.bass import IndirectOffsetOnAxis

F32 = mybir.dt.float32
BF16 = mybir.dt.bfloat16
I32 = mybir.dt.int32
AF = mybir.ActivationFunctionType
ALU = mybir.AluOpType

B, T, D = 4, 4096, 2048
TPC = 2048
HALO = 3
NDC = D // 128
NTG = TPC // 512
CH = 128
NCH = TPC // CH
GM = 64
FH = 5632
NHC = FH // 128
NH, HD = 16, 128
SCALE_GLA = 1.0 / math.sqrt(GM)
SCALEB = 1.0 / math.sqrt(HD)
KV_ROWS = 2 * TPC + 1
NSLOT = sum(20 + 4 * g for g in range(4))
LN_EPS = 1e-5
HHC = NHC // 2
_DBG_STOP = [0]



class _Eng:
    def __init__(self, name):
        self.name = name
        self.ops = []
        self.n_dma = 0
        self.n_ms = 0
        self.seq = 0
        self.last_ms_seq = -1
        self.ms_list = []
        self.waited = {}


class _Op:
    __slots__ = ("eng", "fn", "kind", "seq", "phase", "waits", "incs",
                 "ms_val", "dma_val", "tiny")

    def __init__(self, eng, fn, kind, seq, phase):
        self.eng, self.fn, self.kind = eng, fn, kind
        self.seq, self.phase = seq, phase
        self.waits = []
        self.incs = []
        self.ms_val = None
        self.dma_val = None
        self.tiny = False


class _Buf:
    __slots__ = ("writer", "readers")

    def __init__(self):
        self.writer = None
        self.readers = []


class Prog:
    ENG2SECT = {"pe": "tensor", "act": "scalar", "dve": "vector",
                "sp": "sync", "g": "gpsimd"}

    def __init__(self, nc, ctx):
        self.nc = nc
        self.engs = {n: _Eng(n) for n in ("pe", "act", "dve", "sp", "g")}
        self.sems = {}
        for n in ("pe_s", "act_s", "dve_s", "g_s", "sp_s", "sp_d", "g_d", "cc_s"):
            self.sems[n] = ctx.enter_context(nc.semaphore(n))
        self.phase = 0
        self.n_cc = 0
        self._bufs = {}

    def buf(self, obj):
        key = id(obj)
        b = self._bufs.get(key)
        if b is None:
            b = self._bufs[key] = _Buf()
        return b

    def _milestone(self, op):
        e = op.eng
        if op.kind == "dma":
            return (e.name + "_d", 16 * e.n_dma)
        if op.kind == "cc":
            return ("cc_s", self.n_cc)
        if op.ms_val is None:
            if op.seq <= e.last_ms_seq:
                import bisect
                i = bisect.bisect_left(e.ms_list, (op.seq, -1))
                return (e.name + "_s", e.ms_list[i][1])
            e.n_ms += 1
            e.last_ms_seq = op.seq
            op.ms_val = e.n_ms
            op.incs.append((e.name + "_s", 1))
            e.ms_list.append((op.seq, op.ms_val))
        return (e.name + "_s", op.ms_val)

    def _wait(self, cons, prod):
        if prod is None:
            return
        if prod.kind == "exec":
            if prod.eng is cons.eng:
                if not prod.tiny:
                    return
            elif prod.phase < self.phase:
                return
        sem, val = self._milestone(prod)
        w = cons.eng.waited
        if w.get(sem, -1) >= val:
            return
        w[sem] = val
        cons.waits.append((sem, val))

    def emit(self, eng_name, fn, reads=(), writes=(), kind="exec", tiny=False):
        e = self.engs[eng_name]
        op = _Op(e, fn, kind, e.seq, self.phase)
        op.tiny = tiny
        e.seq += 1
        for r in reads:
            b = self.buf(r)
            self._wait(op, b.writer)
            b.readers.append(op)
        for wt in writes:
            b = self.buf(wt)
            self._wait(op, b.writer)
            for r in b.readers:
                self._wait(op, r)
            b.writer = op
            b.readers = []
        if kind == "dma":
            e.n_dma += 1
            op.dma_val = 16 * e.n_dma
            op.incs.append((e.name + "_d", 16))
        elif kind == "cc":
            self.n_cc += 1
            op.dma_val = self.n_cc
            op.incs.append(("cc_s", 1))
        e.ops.append(op)
        return op

    def flush(self):
        nc, sems = self.nc, self.sems
        if any(e.ops for e in self.engs.values()):
            with nc.Block() as blk:
                for name, e in self.engs.items():
                    if not e.ops:
                        continue
                    ops, e.ops = e.ops, []

                    def runner(engine, ops=ops):
                        for op in ops:
                            for sem, val in op.waits:
                                engine.wait_ge(sems[sem], val)
                            inst = op.fn(engine)
                            for sem, amt in op.incs:
                                inst.then_inc(sems[sem], amt)
                    getattr(blk, self.ENG2SECT[name])(runner)
        self.phase += 1

    def pe(self, fn, r=(), w=()):
        return self.emit("pe", fn, r, w)

    def act(self, fn, r=(), w=()):
        return self.emit("act", fn, r, w)

    def dve(self, fn, r=(), w=()):
        return self.emit("dve", fn, r, w)

    def dve_t(self, fn, r=(), w=()):
        return self.emit("dve", fn, r, w, tiny=True)

    def act_t(self, fn, r=(), w=()):
        return self.emit("act", fn, r, w, tiny=True)

    def dma(self, q, fn, r=(), w=()):
        return self.emit(q, fn, r, w, kind="dma")

    def gx(self, fn, r=(), w=()):
        return self.emit("g", fn, r, w)

    def cc(self, fn, r=(), w=()):
        return self.emit("g", fn, r, w, kind="cc")



def _t4(wT, n_in_tiles, n_out_tiles):
    a = wT.reshape(n_in_tiles, 128, n_out_tiles, 128)
    return np.ascontiguousarray(
        a.transpose(2, 0, 1, 3).reshape(n_out_tiles * n_in_tiles * 128, 128),
        dtype=np.float32)


def _host_inputs(inp):
    x = np.asarray(inp["L0_attn_a_hidden_states"])
    A = lambda a: np.ascontiguousarray(a, dtype=np.float32)

    perm = np.concatenate([h * HD + np.concatenate([np.arange(0, HD, 2),
                                                    np.arange(1, HD, 2)])
                           for h in range(NH)])
    wqkvT = np.asarray(inp["L0_attn_b_w_qkv"], dtype=np.float32).T.copy()
    wqkvT[:, 0:D] = wqkvT[:, 0:D][:, perm]
    wqkvT[:, D:2 * D] = wqkvT[:, D:2 * D][:, perm]
    bqkv = np.asarray(inp["L0_attn_b_b_qkv"], dtype=np.float32).copy()
    bqkv[0:D] = bqkv[0:D][perm]
    bqkv[D:2 * D] = bqkv[D:2 * D][perm]

    w1T = np.asarray(inp["L0_ffn_w1"], dtype=np.float32).T.copy()
    w2T = np.asarray(inp["L0_ffn_w2"], dtype=np.float32).T.copy()
    w3T = np.asarray(inp["L0_ffn_w3"], dtype=np.float32).T.copy()

    half = HD // 2
    thetas = 10000.0 ** (-np.arange(half, dtype=np.float64) / half)
    rs = math.sqrt(SCALEB)

    tri = (np.arange(512)[:, None] <= np.arange(512)[None, :]).astype(np.float32)
    glam = (np.arange(128)[:, None] <= np.arange(128)[None, :]).astype(np.float32)

    pcol = lambda v: A(np.asarray(v).reshape(-1, 128).T)
    memgb = np.asarray(inp["L0_attn_a_mem_gate_b"], dtype=np.float32)

    shared = {
        "convw": A(np.asarray(inp["L0_attn_a_conv_w"]).reshape(NDC, 128, 4)
                   .transpose(1, 0, 2).reshape(128, NDC * 4)),
        "convb": pcol(inp["L0_attn_a_conv_b"]),
        "resw": pcol(inp["L0_attn_a_residual_weight"]),
        "inprojT": A(np.asarray(inp["L0_attn_a_in_proj_w"]).T),
        "memgT": A(np.asarray(inp["L0_attn_a_mem_gate_w"]).T),
        "memgb2": A(np.stack([memgb[:GM], memgb[GM:]], axis=1)),
        "chw0T": A(np.asarray(inp["L0_attn_a_ch_gate_w0"]).T),
        "chw1T": A(np.asarray(inp["L0_attn_a_ch_gate_w1"]).T),
        "chb1": A(np.asarray(inp["L0_attn_a_ch_gate_b1"]).reshape(1, D)),
        "wqkT": A(wqkvT[:, 0:2 * D]),
        "wvT": A(wqkvT[:, 2 * D:]),
        "bqkrow": A(bqkv[0:2 * D].reshape(1, 2 * D)),
        "bvrow": A(bqkv[2 * D:].reshape(1, D)),
        "wout4": _t4(np.asarray(inp["L0_attn_b_w_out"]).T, NDC, NDC),
        "bout": pcol(inp["L0_attn_b_b_out"]),
        "w14": _t4(w1T, NDC, NHC),
        "w24": _t4(w2T, NDC, NHC),
        "w34": _t4(w3T, NHC, NDC),
        "triband": tri, "glamask": glam,
        "ident": np.eye(128, dtype=np.float32),
    }

    in_maps = []
    for c in range(8):
        b, hlf = c // 2, c % 2
        t0 = hlf * TPC
        xs = x[b, t0:t0 + TPC, :].T
        if hlf == 0:
            haloc = np.zeros((D, HALO), np.float32)
        else:
            haloc = x[b, t0 - HALO:t0, :].T
        xT = A(np.concatenate([haloc, xs], axis=1))

        pos = np.arange(t0, t0 + TPC, dtype=np.float64)
        ang = pos[:, None] * thetas[None, :]
        cosP = np.tile((np.cos(ang) * rs).astype(np.float32), (1, 4))
        sinP = np.tile((np.sin(ang) * rs).astype(np.float32), (1, 4))

        moff = np.zeros((128, NSLOT), np.float32)
        col = 0
        for g_ in range(4):
            for kt in range(20 + 4 * g_):
                if hlf == 0 and kt < 16:
                    moff[:, col] = -30000.0
                col += 1

        smask = np.zeros((GM, 8), np.float32)
        if hlf == 1:
            smask[:, c - 1] = 1.0

        ra = c - 1 if hlf == 1 else c
        grp = np.arange(16)
        p = np.arange(128)
        idxk = (ra * KV_ROWS + grp[None, :] * 128 + p[:, None]).astype(np.int32)
        idxv = (ra * KV_ROWS + TPC + grp[None, :] * 128 + p[:, None]).astype(np.int32)

        m = dict(shared)
        m.update({
            "xT": xT, "cosP": A(cosP), "sinP": A(sinP), "moff": moff,
            "smask": smask,
            "idxk": np.ascontiguousarray(idxk),
            "idxv": np.ascontiguousarray(idxv),
            "g1": A(w1T[c * 256:(c + 1) * 256, :]),
            "g2": A(w2T[c * 256:(c + 1) * 256, :]),
            "g3": A(w3T[c * 704:(c + 1) * 704, :]),
        })
        in_maps.append(m)
    return in_maps


def _dump_x(nc, tr, g, y):
    bb = tr.buf
    for dc in range(NDC):
        tr.dma("g", lambda e, dc=dc: e.dma_start(
            out=y[dc * 128:(dc + 1) * 128, :], in_=g["xb"][dc][:]),
            r=[bb(g["xb"][dc])], w=[bb(y)])
    tr.flush()


def build():
    _LN_N[0] = 0
    _FFN_N[0] = 0
    nc = bass.Bass("TRN2", target_bir_lowering=False, debug=False)

    ein = {}

    def EIN(name, shape, dtype=F32):
        ein[name] = nc.dram_tensor(name, list(shape), dtype,
                                   kind="ExternalInput").ap()

    EIN("xT", (D, TPC + HALO))
    EIN("convw", (128, NDC * 4)); EIN("convb", (128, NDC)); EIN("resw", (128, NDC))
    EIN("inprojT", (D, 2 * GM)); EIN("memgT", (D, 2 * GM)); EIN("memgb2", (GM, 2))
    EIN("chw0T", (D, 128)); EIN("chw1T", (128, D)); EIN("chb1", (1, D))
    EIN("wqkT", (D, 2 * D)); EIN("wvT", (D, D))
    EIN("bqkrow", (1, 2 * D)); EIN("bvrow", (1, D))
    EIN("wout4", (NDC * D, 128)); EIN("bout", (128, NDC))
    EIN("w14", (NHC * D, 128)); EIN("w24", (NHC * D, 128)); EIN("w34", (NDC * FH, 128))
    EIN("triband", (512, 512)); EIN("glamask", (128, 128)); EIN("ident", (128, 128))
    EIN("cosP", (TPC, 256)); EIN("sinP", (TPC, 256))
    EIN("moff", (128, NSLOT)); EIN("smask", (GM, 8))
    EIN("idxk", (128, 16), I32); EIN("idxv", (128, 16), I32)
    EIN("g1", (256, FH)); EIN("g2", (256, FH)); EIN("g3", (704, D))

    y = nc.dram_tensor("y", [D, TPC], F32, kind="ExternalOutput").ap()

    u_dram = nc.dram_tensor("u_dram", [TPC, D], BF16).ap()
    q_dram = nc.dram_tensor("q_dram", [TPC, D], BF16).ap()
    kv_stage = nc.dram_tensor("kv_stage", [KV_ROWS, D], BF16).ap()
    kv_gath = nc.dram_tensor("kv_gath", [8 * KV_ROWS, D], BF16,
                             addr_space="Shared").ap()
    s_stage = nc.dram_tensor("s_stage", [GM + 1, D], F32).ap()
    s_gath = nc.dram_tensor("s_gath", [8 * (GM + 1), D], F32,
                            addr_space="Shared").ap()
    kbuf1 = nc.dram_tensor("kbuf1", [TPC, D], BF16).ap()
    vbuf1 = nc.dram_tensor("vbuf1", [TPC, D], BF16).ap()
    w12qc = nc.dram_tensor("w12qc", [NHC * 2 * D, 128], BF16).ap()
    w3qc = nc.dram_tensor("w3qc", [NDC * FH, 128], BF16).ap()

    with ExitStack() as ctx:
        E = ctx.enter_context
        tr = Prog(nc, ctx)

        g = {}
        g["xb"] = [E(nc.sbuf_tensor(f"x{i}", [128, TPC], BF16)) for i in range(NDC)]
        g["ln"] = [E(nc.sbuf_tensor(f"ln{i}", [128, TPC + HALO], BF16))
                   for i in range(NDC)]
        for nm, shp, dt in (
            ("ones_bf", [128, 1], BF16), ("ones_f", [128, 1], F32),
            ("onesr_bf", [1, 128], BF16), ("onesr_f", [1, 128], F32),
            ("row_m", [1, TPC + HALO], F32),
            ("row_r", [1, TPC + HALO], F32),
            ("convb_s", [128, NDC], F32), ("resw_s", [128, NDC], F32),
            ("convw_s", [128, NDC * 4], F32), ("memgb_s", [GM, 2], F32),
            ("gpart", [1, 16], F32), ("thr_s", [128, 3], F32),
            ("nthr_s", [128, 3], F32),
            ("moff_sb", [128, NSLOT], F32), ("smask_sb", [GM, 8], F32),
            ("idxk_sb", [128, 16], I32), ("idxv_sb", [128, 16], I32),
            ("bout_s", [128, NDC], F32),
        ):
            g[nm] = E(nc.sbuf_tensor(nm, shp, dt))

        _p0(nc, tr, g, ein)
        tr.flush()
        if _DBG_STOP[0] == 1:
            _dump_x(nc, tr, g, y)
            return _finish(nc, tr)
        with ExitStack() as gla:
            Eg = gla.enter_context
            for nm, shp, dt in (
                ("r_bf", [GM, TPC], BF16), ("kh_bf", [GM, TPC], BF16),
                ("d_f", [GM, TPC], F32), ("S_f", [GM, D], F32),
                ("S_bf", [GM, D], BF16), ("eD_f", [GM, NCH], F32),
                ("ident_bf", [128, 128], BF16), ("glam_f", [128, 128], F32),
                ("ones64", [GM, 128], F32),
            ):
                g[nm] = Eg(nc.sbuf_tensor(nm, shp, dt))
            tr.gx(lambda e: e.memset(g["ones64"][:], 1.0), w=[tr.buf(g["ones64"])])
            tr.dma("g", lambda e: e.dma_start(out=g["ident_bf"][:], in_=ein["ident"]),
                   w=[tr.buf(g["ident_bf"])])
            tr.dma("sp", lambda e: e.dma_start(out=g["glam_f"][:], in_=ein["glamask"]),
                   w=[tr.buf(g["glam_f"])])
            _p1(nc, tr, g, ein)
            tr.flush()
            _p2(nc, tr, g, ein, u_dram)
            tr.flush()
            _p3(nc, tr, g, ein, u_dram, s_stage, s_gath)
            tr.flush()
            if _DBG_STOP[0] not in (0, 2, 3, 4, 5):
                pass
            _p4(nc, tr, g, ein, s_gath, w12qc, w3qc)
            tr.flush()
        if _DBG_STOP[0] == 2:
            _dump_x(nc, tr, g, y)
            return _finish(nc, tr)
        if _DBG_STOP[0] == 6:
            tr.dma("g", lambda e: e.dma_start(
                out=y[0:128, :], in_=w12qc[0:D, :].rearrange("(i p) c -> p i c", p=128)),
                r=[tr.buf(w12qc)], w=[tr.buf(y)])
            tr.dma("g", lambda e: e.dma_start(
                out=y[128:256, :], in_=w3qc[0:D, :].rearrange("(i p) c -> p i c", p=128)),
                r=[tr.buf(w3qc)], w=[tr.buf(y)])
            tr.dma("g", lambda e: e.dma_start(
                out=y[256:384, 0:3], in_=g["thr_s"][0:128, 0:3]),
                r=[tr.buf(g["thr_s"])], w=[tr.buf(y)])
            tr.dma("g", lambda e: e.dma_start(
                out=y[384:385, 0:16], in_=g["gpart"][:]),
                r=[tr.buf(g["gpart"])], w=[tr.buf(y)])
            tr.dma("g", lambda e: e.dma_start(
                out=y[385:386, 0:24], in_=g["_dbg_grows"][:]),
                r=[tr.buf(g["_dbg_grows"])], w=[tr.buf(y)])
            tr.dma("g", lambda e: e.dma_start(
                out=y[386:387, 0:16], in_=g["_dbg_sgath"][GM:GM+1, 0:16]),
                r=[tr.buf(g["_dbg_sgath"])], w=[tr.buf(y)])
            tr.dma("g", lambda e: e.dma_start(
                out=y[387:388, 0:16], in_=g["_dbg_sgath"][2*GM+1:2*GM+2, 0:16]),
                r=[tr.buf(g["_dbg_sgath"])], w=[tr.buf(y)])
            tr.dma("g", lambda e: e.dma_start(
                out=y[388:389, 0:3], in_=g["_dbg_gsum"][0:1, 4:7]),
                r=[tr.buf(g["_dbg_gsum"])], w=[tr.buf(y)])
            gr = g["_dbg_grows"]
            tr.dve(lambda e: e.tensor_reduce(
                gr[0:1, 20:21], gr[0:1, 0:24:3], mybir.AxisListType.X, ALU.add),
                r=[tr.buf(gr)], w=[tr.buf(gr)])
            tr.dma("g", lambda e: e.dma_start(
                out=y[389:390, 0:1], in_=gr[0:1, 20:21]),
                r=[tr.buf(gr)], w=[tr.buf(y)])
            tr.flush()
            return _finish(nc, tr)
        _ln_x(nc, tr, g)
        tr.flush()
        if _DBG_STOP[0] == 5:
            bb = tr.buf
            for dc in range(NDC):
                tr.dma("g", lambda e, dc=dc: e.dma_start(
                    out=y[dc * 128:(dc + 1) * 128, :], in_=g["ln"][dc][:, 0:TPC]),
                    r=[bb(g["ln"][dc])], w=[bb(y)])
            tr.flush()
            return _finish(nc, tr)
        _p_ffn(nc, tr, g, w12qc, w3qc, y=None)
        tr.flush()
        if _DBG_STOP[0] == 3:
            _dump_x(nc, tr, g, y)
            return _finish(nc, tr)
        _ln_x(nc, tr, g)
        tr.flush()
        _p6(nc, tr, g, ein, q_dram, kv_stage, kv_gath)
        tr.flush()
        _p7(nc, tr, g, ein, kv_stage, kv_gath, kbuf1, vbuf1, q_dram)
        tr.flush()
        if _DBG_STOP[0] == 4:
            _dump_x(nc, tr, g, y)
            return _finish(nc, tr)
        _ln_x(nc, tr, g)
        tr.flush()
        _p_ffn(nc, tr, g, w12qc, w3qc, y=y)
        tr.flush()

        _finish(nc, tr)
    return nc


def _finish(nc, tr):
    with nc.Block() as blk:
        sems, engs = tr.sems, tr.engs

        @blk.sync
        def _(sy):
            sy.wait_ge(sems["sp_d"], 16 * engs["sp"].n_dma)

        @blk.gpsimd
        def _(gq):
            gq.wait_ge(sems["g_d"], 16 * engs["g"].n_dma)
    return nc



def _p0(nc, tr, g, ein):
    bb = tr.buf
    xb, ln = g["xb"], g["ln"]
    with ExitStack() as px:
        E = px.enter_context
        xin = [E(nc.sbuf_tensor(f"p0xin{i}", [128, TPC + HALO], F32)) for i in range(2)]
        sqt = [E(nc.sbuf_tensor(f"p0sq{i}", [128, TPC + HALO], BF16)) for i in range(2)]
        hx = E(nc.sbuf_tensor("p0hx", [128, NDC * HALO], F32))
        hq = E(nc.sbuf_tensor("p0hq", [128, NDC * HALO], F32))
        gtile = [E(nc.sbuf_tensor(f"p0gt{i}", [128, 2048], F32)) for i in range(2)]
        gacc = E(nc.sbuf_tensor("p0gacc", [128, 32], F32))
        gcol = E(nc.sbuf_tensor("p0gcol", [128, 1], F32))
        scr = E(nc.sbuf_tensor("p0scr", [128, 512 + HALO], F32))
        g["row_t"] = E(nc.sbuf_tensor("p0rowt", [1, TPC + HALO], F32))

        seg1 = ExitStack()
        E1 = seg1.enter_context
        psx = [E1(nc.psum_tensor(f"p0psx{i}", [1, 512], F32)) for i in range(4)]
        psq = [E1(nc.psum_tensor(f"p0psq{i}", [1, 512], F32)) for i in range(4)]

        tr.gx(lambda e: e.memset(g["ones_bf"][:], 1.0), w=[bb(g["ones_bf"])])
        tr.gx(lambda e: e.memset(g["ones_f"][:], 1.0), w=[bb(g["ones_f"])])
        tr.gx(lambda e: e.memset(g["onesr_bf"][:], 1.0), w=[bb(g["onesr_bf"])])
        tr.gx(lambda e: e.memset(g["onesr_f"][:], 1.0), w=[bb(g["onesr_f"])])
        for nm, src in (("convb_s", "convb"), ("resw_s", "resw"),
                        ("convw_s", "convw"), ("memgb_s", "memgb2"),
                        ("moff_sb", "moff"), ("smask_sb", "smask"),
                        ("idxk_sb", "idxk"), ("idxv_sb", "idxv"),
                        ("bout_s", "bout")):
            tr.dma("sp", lambda e, nm=nm, src=src: e.dma_start(
                out=g[nm][:], in_=ein[src]), w=[bb(g[nm])])

        for dc in range(NDC):
            xt = xin[dc % 2]
            sq = sqt[dc % 2]
            tr.dma("sp", lambda e, dc=dc, xt=xt: e.dma_start(
                out=xt[:], in_=ein["xT"][dc * 128:(dc + 1) * 128, :]),
                w=[bb(xt)])
            tr.dve(lambda e, dc=dc, xt=xt: e.tensor_copy(ln[dc][:], xt[:]),
                   r=[bb(xt)], w=[bb(ln[dc])])
            tr.dve(lambda e, dc=dc: e.tensor_copy(
                xb[dc][:], ln[dc][:, HALO:]), r=[bb(ln[dc])], w=[bb(xb[dc])])
            tr.act(lambda e, xt=xt, sq=sq: e.activation(
                sq[:], xt[:], AF.Square), r=[bb(xt)], w=[bb(sq)])
            for tg in range(4):
                c0 = HALO + tg * 512
                tr.pe(lambda e, dc=dc, tg=tg, c0=c0: e.matmul(
                    psx[tg][:], g["ones_bf"][:], ln[dc][:, c0:c0 + 512],
                    start=(dc == 0), stop=(dc == NDC - 1)),
                    r=[bb(ln[dc]), bb(g["ones_bf"])], w=[bb(psx[tg])])
                tr.pe(lambda e, dc=dc, tg=tg, c0=c0, sq=sq: e.matmul(
                    psq[tg][:], g["ones_bf"][:], sq[:, c0:c0 + 512],
                    start=(dc == 0), stop=(dc == NDC - 1)),
                    r=[bb(sq), bb(g["ones_bf"])], w=[bb(psq[tg])])
        for tg in range(4):
            tr.dve(lambda e, tg=tg: e.tensor_copy(
                g["row_m"][:, HALO + tg * 512:HALO + (tg + 1) * 512],
                psx[tg][:]), r=[bb(psx[tg])], w=[bb(g["row_m"])])
            tr.dve(lambda e, tg=tg: e.tensor_copy(
                g["row_r"][:, HALO + tg * 512:HALO + (tg + 1) * 512],
                psq[tg][:]), r=[bb(psq[tg])], w=[bb(g["row_r"])])
        seg1.close()
        tr.flush()

        psh = [E(nc.psum_tensor(f"p0psh{i}", [1, HALO], F32)) for i in range(2)]
        psb = [E(nc.psum_tensor(f"p0psb{i}", [128, 512], F32)) for i in range(2)]
        psg = E(nc.psum_tensor("p0psg", [1, 1], F32))

        for dc in range(NDC):
            tr.dma("sp", lambda e, dc=dc: e.dma_start(
                out=hx[:, dc * HALO:(dc + 1) * HALO],
                in_=ein["xT"][dc * 128:(dc + 1) * 128, 0:HALO]), w=[bb(hx)])
        tr.act(lambda e: e.activation(hq[:], hx[:], AF.Square),
               r=[bb(hx)], w=[bb(hq)])
        for dc in range(NDC):
            tr.pe(lambda e, dc=dc: e.matmul(
                psh[0][:], g["ones_f"][:], hx[:, dc * HALO:(dc + 1) * HALO],
                start=(dc == 0), stop=(dc == NDC - 1)),
                r=[bb(hx), bb(g["ones_f"])], w=[bb(psh[0])])
            tr.pe(lambda e, dc=dc: e.matmul(
                psh[1][:], g["ones_f"][:], hq[:, dc * HALO:(dc + 1) * HALO],
                start=(dc == 0), stop=(dc == NDC - 1)),
                r=[bb(hq), bb(g["ones_f"])], w=[bb(psh[1])])
        tr.act(lambda e: e.copy(g["row_m"][:, 0:HALO], psh[0][:]),
               r=[bb(psh[0])], w=[bb(g["row_m"])])
        tr.act(lambda e: e.copy(g["row_r"][:, 0:HALO], psh[1][:]),
               r=[bb(psh[1])], w=[bb(g["row_r"])])

        _stats_pipeline(nc, tr, g, n=TPC + HALO)

        groups = [(0, HALO)] + [(HALO + t * 512, 512) for t in range(4)]
        for (c0, n) in groups:
            tr.pe(lambda e, c0=c0, n=n: e.matmul(
                psb[0][:, 0:n], g["onesr_f"][:], g["row_m"][0:1, c0:c0 + n],
                start=True, stop=True),
                r=[bb(g["row_m"]), bb(g["onesr_f"])], w=[bb(psb[0])])
            tr.pe(lambda e, c0=c0, n=n: e.matmul(
                psb[1][:, 0:n], g["onesr_f"][:], g["row_r"][0:1, c0:c0 + n],
                start=True, stop=True),
                r=[bb(g["row_r"]), bb(g["onesr_f"])], w=[bb(psb[1])])
            for dc in range(NDC):
                tr.dve(lambda e, dc=dc, c0=c0, n=n: e.tensor_tensor(
                    scr[:, 0:n], ln[dc][:, c0:c0 + n], psb[0][:, 0:n],
                    ALU.subtract), r=[bb(ln[dc]), bb(psb[0])], w=[bb(scr)])
                if n < 128:
                    tr.act(lambda e, n=n: e.copy(scr[:, 512:512 + n], scr[:, 0:n]),
                           r=[bb(scr)], w=[bb(scr)])
                    tr.dve(lambda e, dc=dc, c0=c0, n=n: e.tensor_tensor(
                        ln[dc][:, c0:c0 + n], scr[:, 512:512 + n], psb[1][:, 0:n],
                        ALU.mult), r=[bb(scr), bb(psb[1])], w=[bb(ln[dc])])
                else:
                    tr.dve(lambda e, dc=dc, c0=c0, n=n: e.tensor_tensor(
                        ln[dc][:, c0:c0 + n], scr[:, 0:n], psb[1][:, 0:n],
                        ALU.mult), r=[bb(scr), bb(psb[1])], w=[bb(ln[dc])])

        tr.gx(lambda e: e.memset(gacc[:], 0.0), w=[bb(gacc)])
        col = 0
        spans = []
        for nm, rows, cols in (("g1", 256, FH), ("g2", 256, FH), ("g3", 704, D)):
            start_col = col
            r0 = 0
            while r0 < rows:
                pr = min(128, rows - r0)
                c0 = 0
                while c0 < cols:
                    pc = min(2048, cols - c0)
                    gt = gtile[col % 2]
                    tr.dma("sp", lambda e, nm=nm, r0=r0, pr=pr, c0=c0, pc=pc, gt=gt:
                           e.dma_start(out=gt[0:pr, 0:pc],
                                       in_=ein[nm][r0:r0 + pr, c0:c0 + pc]),
                           w=[bb(gt)])
                    tr.dve(lambda e, gt=gt, pr=pr, pc=pc, col=col: e.tensor_reduce(
                        gacc[0:pr, col:col + 1], gt[0:pr, 0:pc],
                        mybir.AxisListType.X, ALU.add, apply_absolute_value=True),
                        r=[bb(gt)], w=[bb(gacc)])
                    col += 1
                    c0 += pc
                r0 += pr
            spans.append((start_col, col))
        for j, (a, bcol) in enumerate(spans):
            tr.dve(lambda e, a=a, bcol=bcol: e.tensor_reduce(
                gcol[:, 0:1], gacc[:, a:bcol], mybir.AxisListType.X, ALU.add),
                r=[bb(gacc)], w=[bb(gcol)])
            tr.pe(lambda e: e.matmul(psg[:], g["ones_f"][:], gcol[:, 0:1],
                                     start=True, stop=True),
                  r=[bb(gcol), bb(g["ones_f"])], w=[bb(psg)])
            tr.dve(lambda e, j=j: e.tensor_copy(g["gpart"][:, j:j + 1], psg[:]),
                   r=[bb(psg)], w=[bb(g["gpart"])])


def _stats_pipeline(nc, tr, g, n):
    bb = tr.buf
    m, r, t = g["row_m"], g["row_r"], g["row_t"]
    tr.dve(lambda e: e.tensor_scalar_mul(m[:, 0:n], m[:, 0:n], 1.0 / D),
           r=[], w=[bb(m)])
    tr.dve(lambda e: e.tensor_tensor(t[:, 0:n], m[:, 0:n], m[:, 0:n], ALU.mult),
           r=[bb(m)], w=[bb(t)])
    tr.dve(lambda e: e.scalar_tensor_tensor(
        r[:, 0:n], r[:, 0:n], 1.0 / D, t[:, 0:n], ALU.mult, ALU.subtract),
        r=[bb(t)], w=[bb(r)])
    tr.dve(lambda e: e.tensor_scalar_add(r[:, 0:n], r[:, 0:n], LN_EPS),
           r=[], w=[bb(r)])
    tr.act(lambda e: e.activation(t[:, 0:n], r[:, 0:n], AF.Ln),
           r=[bb(r)], w=[bb(t)])
    tr.dve(lambda e: e.tensor_scalar_mul(t[:, 0:n], t[:, 0:n], -0.5),
           r=[], w=[bb(t)])
    tr.act(lambda e: e.activation(r[:, 0:n], t[:, 0:n], AF.Exp),
           r=[bb(t)], w=[bb(r)])


def _p1(nc, tr, g, ein):
    bb = tr.buf
    xb, ln = g["xb"], g["ln"]
    with ExitStack() as px:
        E = px.enter_context
        inpj = E(nc.sbuf_tensor("p1inpj", [128, NDC * 128], BF16))
        memg = E(nc.sbuf_tensor("p1memg", [128, NDC * 128], BF16))
        shf = [E(nc.sbuf_tensor(f"p1s{i}", [128, 512], BF16)) for i in range(2)]
        acc = E(nc.sbuf_tensor("p1acc", [128, 512], F32))
        g_f = E(nc.sbuf_tensor("p1gf", [GM, 512], F32))
        sel = E(nc.sbuf_tensor("p1sel", [GM, 512], F32))
        tau = E(nc.sbuf_tensor("p1tau", [GM, 512], F32))
        aa = E(nc.sbuf_tensor("p1aa", [GM, 512], F32))
        mm_ = E(nc.sbuf_tensor("p1mm", [GM, 512], F32))
        kpre = E(nc.sbuf_tensor("p1kpre", [GM, 512], F32))
        k2 = E(nc.sbuf_tensor("p1k2", [GM, 512], F32))
        lns = aa
        itg = mm_
        kq = k2
        nr = E(nc.sbuf_tensor("p1nr", [1, 512], F32))
        pr_ = E(nc.psum_tensor("p1pr", [GM, 512], F32))
        pk = E(nc.psum_tensor("p1pk", [GM, 512], F32))
        pse = E(nc.psum_tensor("p1pse", [GM, 512], F32))
        pta = E(nc.psum_tensor("p1pta", [GM, 512], F32))
        pn2 = E(nc.psum_tensor("p1pn2", [1, 512], F32))
        pnb = E(nc.psum_tensor("p1pnb", [GM, 512], F32))

        tr.dma("g", lambda e: e.dma_start(
            out=inpj[:], in_=ein["inprojT"].rearrange("(i p) c -> p i c", p=128)),
            w=[bb(inpj)])
        tr.dma("g", lambda e: e.dma_start(
            out=memg[:], in_=ein["memgT"].rearrange("(i p) c -> p i c", p=128)),
            w=[bb(memg)])

        for tg in range(NTG):
            t0 = tg * 512
            for dc in range(NDC):
                sh = shf[dc % 2]
                tr.dve(lambda e, dc=dc, t0=t0: e.tensor_scalar_mul(
                    acc[:], ln[dc][:, t0:t0 + 512],
                    g["convw_s"][:, dc * 4:dc * 4 + 1]),
                    r=[bb(ln[dc]), bb(g["convw_s"])], w=[bb(acc)])
                for i in range(1, 4):
                    tr.dve(lambda e, dc=dc, t0=t0, i=i: e.scalar_tensor_tensor(
                        acc[:], ln[dc][:, t0 + i:t0 + i + 512],
                        g["convw_s"][:, dc * 4 + i:dc * 4 + i + 1], acc[:],
                        ALU.mult, ALU.add),
                        r=[bb(ln[dc]), bb(g["convw_s"])], w=[bb(acc)])
                tr.act(lambda e, dc=dc, sh=sh: e.activation(
                    sh[:], acc[:], AF.Silu, bias=g["convb_s"][:, dc:dc + 1]),
                    r=[bb(acc), bb(g["convb_s"])], w=[bb(sh)])
                tr.dve(lambda e, dc=dc, t0=t0, sh=sh: e.scalar_tensor_tensor(
                    xb[dc][:, t0:t0 + 512], sh[:],
                    g["resw_s"][:, dc:dc + 1], xb[dc][:, t0:t0 + 512],
                    ALU.mult, ALU.add),
                    r=[bb(sh), bb(g["resw_s"])], w=[bb(xb[dc])])
                st, sp_ = (dc == 0), (dc == NDC - 1)
                tr.pe(lambda e, dc=dc, sh=sh, st=st, sp_=sp_: e.matmul(
                    pr_[:], inpj[:, dc * 128:dc * 128 + GM], sh[:],
                    start=st, stop=sp_), r=[bb(inpj), bb(sh)], w=[bb(pr_)])
                tr.pe(lambda e, dc=dc, sh=sh, st=st, sp_=sp_: e.matmul(
                    pk[:], inpj[:, dc * 128 + GM:dc * 128 + 128], sh[:],
                    start=st, stop=sp_), r=[bb(inpj), bb(sh)], w=[bb(pk)])
                tr.pe(lambda e, dc=dc, sh=sh, st=st, sp_=sp_: e.matmul(
                    pse[:], memg[:, dc * 128:dc * 128 + GM], sh[:],
                    start=st, stop=sp_), r=[bb(memg), bb(sh)], w=[bb(pse)])
                tr.pe(lambda e, dc=dc, sh=sh, st=st, sp_=sp_: e.matmul(
                    pta[:], memg[:, dc * 128 + GM:dc * 128 + 128], sh[:],
                    start=st, stop=sp_), r=[bb(memg), bb(sh)], w=[bb(pta)])
            tr.dve(lambda e, t0=t0: e.tensor_copy(
                g["r_bf"][:, t0:t0 + 512], pr_[:]),
                r=[bb(pr_)], w=[bb(g["r_bf"])])
            tr.dve(lambda e: e.tensor_copy(kpre[:], pk[:]),
                   r=[bb(pk)], w=[bb(kpre)])
            tr.act(lambda e: e.activation(tau[:], pta[:], AF.Sigmoid,
                                          bias=g["memgb_s"][:, 1:2]),
                   r=[bb(pta), bb(g["memgb_s"])], w=[bb(tau)])
            tr.dve(lambda e: e.tensor_scalar_add(mm_[:], pse[:],
                                                 g["memgb_s"][:, 0:1]),
                   r=[bb(pse), bb(g["memgb_s"])], w=[bb(mm_)])
            tr.act(lambda e: e.activation(aa[:], mm_[:], AF.Abs),
                   r=[bb(mm_)], w=[bb(aa)])
            tr.act(lambda e: e.activation(aa[:], aa[:], AF.Exp, scale=-1.0),
                   r=[], w=[bb(aa)])
            tr.act(lambda e: e.activation(aa[:], aa[:], AF.Ln, bias=1.0),
                   r=[], w=[bb(aa)])
            tr.dve(lambda e: e.tensor_scalar_max(mm_[:], mm_[:], 0.0),
                   r=[], w=[bb(mm_)])
            tr.dve(lambda e: e.tensor_add(sel[:], mm_[:], aa[:]),
                   r=[bb(mm_), bb(aa)], w=[bb(sel)])
            tr.dve(lambda e: e.tensor_scalar_max(sel[:], sel[:], 1e-38),
                   r=[], w=[bb(sel)])
            tr.act(lambda e: e.activation(lns[:], sel[:], AF.Ln),
                   r=[bb(sel)], w=[bb(lns)])
            tr.dve(lambda e: e.tensor_mul(lns[:], tau[:], lns[:]),
                   r=[bb(tau)], w=[bb(lns)])
            tr.act(lambda e: e.activation(itg[:], lns[:], AF.Exp),
                   r=[bb(lns)], w=[bb(itg)])
            tr.dve(lambda e: e.scalar_tensor_tensor(
                g_f[:], sel[:], -1.0, tau[:],
                ALU.mult, ALU.mult),
                r=[bb(sel), bb(tau)], w=[bb(g_f)])
            for cc_ in range(4):
                c = tg * 4 + cc_
                tr.dve(lambda e, c=c, cc_=cc_: e.tensor_tensor_scan(
                    g["d_f"][:, c * CH:(c + 1) * CH], g["ones64"][:, 0:CH],
                    g_f[:, cc_ * CH:(cc_ + 1) * CH], 0.0, ALU.mult, ALU.add),
                    r=[bb(g["ones64"]), bb(g_f)], w=[bb(g["d_f"])])
            tr.act(lambda e: e.activation(k2[:], kpre[:], AF.Square),
                   r=[bb(kpre)], w=[bb(k2)])
            tr.pe(lambda e: e.matmul(pn2[:], g["ones_f"][0:GM, 0:1], k2[:],
                                     start=True, stop=True),
                  r=[bb(k2), bb(g["ones_f"])], w=[bb(pn2)])
            tr.act(lambda e: e.activation(nr[:], pn2[:], AF.Sqrt),
                   r=[bb(pn2)], w=[bb(nr)])
            tr.dve(lambda e: e.tensor_scalar_max(nr[:], nr[:], 1e-12),
                   r=[], w=[bb(nr)])
            tr.dve(lambda e: e.reciprocal(nr[:], nr[:]), r=[], w=[bb(nr)])
            tr.pe(lambda e: e.matmul(pnb[:], g["onesr_f"][0:1, 0:GM], nr[:],
                                     start=True, stop=True),
                  r=[bb(nr), bb(g["onesr_f"])], w=[bb(pnb)])
            tr.dve(lambda e: e.tensor_tensor(kq[:], kpre[:], pnb[:], ALU.mult),
                   r=[bb(kpre), bb(pnb)], w=[bb(kq)])
            tr.dve(lambda e, t0=t0: e.tensor_tensor(
                g["kh_bf"][:, t0:t0 + 512], kq[:], itg[:], ALU.mult),
                r=[bb(kq), bb(itg)], w=[bb(g["kh_bf"])])



def _p2(nc, tr, g, ein, u_dram):
    bb = tr.buf
    ln = g["ln"]
    with ExitStack() as px:
        E = px.enter_context
        chw0_sb = E(nc.sbuf_tensor("p2w0", [128, NDC * 128], BF16))
        chw1_sb = E(nc.sbuf_tensor("p2w1", [128, D], BF16))
        chb1_sb = E(nc.sbuf_tensor("p2b1", [1, D], BF16))
        h0 = E(nc.sbuf_tensor("p2h0", [128, TPC], BF16))
        lntm = E(nc.sbuf_tensor("p2lntm", [128, D], BF16))
        chs = E(nc.sbuf_tensor("p2chs", [128, 512], BF16))
        ut = E(nc.sbuf_tensor("p2ut", [128, D], BF16))
        ph = [E(nc.psum_tensor(f"p2ph{i}", [128, 512], F32)) for i in range(2)]
        ptr = [E(nc.psum_tensor(f"p2ptr{i}", [128, 128], BF16)) for i in range(2)]
        pch = [E(nc.psum_tensor(f"p2pch{i}", [128, 512], F32)) for i in range(2)]

        tr.dma("g", lambda e: e.dma_start(
            out=chw0_sb[:], in_=ein["chw0T"].rearrange("(i p) c -> p i c", p=128)),
            w=[bb(chw0_sb)])
        tr.dma("g", lambda e: e.dma_start(out=chw1_sb[:], in_=ein["chw1T"]),
               w=[bb(chw1_sb)])
        tr.dma("g", lambda e: e.dma_start(out=chb1_sb[:], in_=ein["chb1"]),
               w=[bb(chb1_sb)])

        for tg in range(NTG):
            p = ph[tg % 2]
            for dc in range(NDC):
                tr.pe(lambda e, dc=dc, tg=tg, p=p: e.matmul(
                    p[:], chw0_sb[:, dc * 128:(dc + 1) * 128],
                    ln[dc][:, HALO + tg * 512:HALO + (tg + 1) * 512],
                    start=(dc == 0), stop=(dc == NDC - 1)),
                    r=[bb(chw0_sb), bb(ln[dc])], w=[bb(p)])
            tr.dve(lambda e, tg=tg, p=p: e.tensor_copy(
                h0[:, tg * 512:(tg + 1) * 512], p[:]), r=[bb(p)], w=[bb(h0)])

        for tc in range(NCH):
            c0 = HALO + tc * 128
            for dc in range(NDC):
                pt = ptr[dc % 2]
                tr.pe(lambda e, dc=dc, c0=c0, pt=pt: e.transpose(
                    pt[:], ln[dc][:, c0:c0 + 128], g["ident_bf"][:]),
                    r=[bb(ln[dc]), bb(g["ident_bf"])], w=[bb(pt)])
                tr.dve(lambda e, dc=dc, pt=pt: e.tensor_copy(
                    lntm[:, dc * 128:(dc + 1) * 128], pt[:]),
                    r=[bb(pt)], w=[bb(lntm)])
            for dg in range(4):
                p = pch[dg % 2]
                tr.pe(lambda e, tc=tc, dg=dg, p=p: e.matmul(
                    p[:], h0[:, tc * 128:(tc + 1) * 128],
                    chw1_sb[:, dg * 512:(dg + 1) * 512],
                    start=True, stop=False),
                    r=[bb(h0), bb(chw1_sb)], w=[bb(p)])
                tr.pe(lambda e, dg=dg, p=p: e.matmul(
                    p[:], g["onesr_bf"][:], chb1_sb[0:1, dg * 512:(dg + 1) * 512],
                    start=False, stop=True),
                    r=[bb(g["onesr_bf"]), bb(chb1_sb)], w=[bb(p)])
                tr.act(lambda e, p=p: e.activation(chs[:], p[:], AF.Sigmoid),
                       r=[bb(p)], w=[bb(chs)])
                tr.dve(lambda e, dg=dg: e.tensor_tensor(
                    ut[:, dg * 512:(dg + 1) * 512], chs[:],
                    lntm[:, dg * 512:(dg + 1) * 512], ALU.mult),
                    r=[bb(chs), bb(lntm)], w=[bb(ut)])
            tr.dma("sp", lambda e, tc=tc: e.dma_start(
                out=u_dram[tc * 128:(tc + 1) * 128, :], in_=ut[:]),
                r=[bb(ut)], w=[bb(u_dram)])



def _p3(nc, tr, g, ein, u_dram, s_stage, s_gath):
    bb = tr.buf
    xb = g["xb"]
    with ExitStack() as px:
        E = px.enter_context
        Uc = [E(nc.sbuf_tensor(f"p3u{i}", [128, D], BF16)) for i in range(2)]
        nd = E(nc.sbuf_tensor("p3nd", [GM, CH], F32))
        e0 = E(nc.sbuf_tensor("p3e0", [GM, CH], BF16))
        q0 = E(nc.sbuf_tensor("p3q0", [GM, CH], BF16))
        eq = E(nc.sbuf_tensor("p3eq", [GM, 32], BF16))
        ek = E(nc.sbuf_tensor("p3ek", [GM, CH], BF16))
        qj = E(nc.sbuf_tensor("p3qj", [GM, 32], BF16))
        kj = E(nc.sbuf_tensor("p3kj", [GM, CH], BF16))
        es = E(nc.sbuf_tensor("p3es", [GM, CH], BF16))
        kS = E(nc.sbuf_tensor("p3kS", [GM, CH], BF16))
        kT = E(nc.sbuf_tensor("p3kT", [128, GM], BF16))
        Pt = E(nc.sbuf_tensor("p3Pt", [128, CH], BF16))
        pP = E(nc.psum_tensor("p3pP", [128, CH], F32))
        pT = E(nc.psum_tensor("p3pT", [128, GM], BF16))
        pI = E(nc.psum_tensor("p3pI", [GM, 512], F32))
        po = [E(nc.psum_tensor(f"p3po{i}", [128, CH], F32)) for i in range(5)]

        tr.gx(lambda e: e.memset(g["S_f"][:], 0.0), w=[bb(g["S_f"])])
        dbuf = bb(g["d_f"])

        for c in range(NCH):
            u = Uc[c % 2]
            tr.dma("sp", lambda e, c=c, u=u: e.dma_start(
                out=u[:], in_=u_dram[c * 128:(c + 1) * 128, :]),
                r=[bb(u_dram)], w=[bb(u)])
            dsl = g["d_f"][:, c * CH:(c + 1) * CH]
            tr.dve(lambda e, dsl=dsl: e.tensor_scalar_mul(nd[:], dsl, -1.0),
                   r=[dbuf], w=[bb(nd)])
            tr.act(lambda e, c=c, dsl=dsl: e.activation(
                g["eD_f"][:, c:c + 1], dsl[:, CH - 1:CH], AF.Exp),
                r=[dbuf], w=[bb(g["eD_f"])])
            tr.dve(lambda e: e.tensor_copy(g["S_bf"][:], g["S_f"][:]),
                   r=[bb(g["S_f"])], w=[bb(g["S_bf"])])
            tr.act(lambda e, dsl=dsl: e.activation(e0[:], dsl, AF.Exp),
                   r=[dbuf], w=[bb(e0)])
            tr.dve(lambda e, c=c: e.tensor_tensor(
                q0[:], g["r_bf"][:, c * CH:(c + 1) * CH], e0[:], ALU.mult),
                r=[bb(g["r_bf"]), bb(e0)], w=[bb(q0)])
            tr.dve(lambda e: e.memset(pP[:], 0.0), w=[bb(pP)])
            for J in range(4):
                w_ = 32 * J + 32
                mid = 32 * J + 16
                tr.act(lambda e, dsl=dsl, J=J, mid=mid, w_=w_: e.activation(
                    eq[:], dsl[:, 32 * J:w_], AF.Exp,
                    bias=nd[:, mid:mid + 1]),
                    r=[dbuf, bb(nd)], w=[bb(eq)])
                tr.dve(lambda e, c=c, J=J, w_=w_: e.tensor_tensor(
                    qj[:], g["r_bf"][:, c * CH + 32 * J:c * CH + w_],
                    eq[:], ALU.mult), r=[bb(g["r_bf"]), bb(eq)], w=[bb(qj)])
                tr.act(lambda e, dsl=dsl, w_=w_, mid=mid: e.activation(
                    ek[:, 0:w_], nd[:, 0:w_], AF.Exp,
                    bias=dsl[:, mid:mid + 1]),
                    r=[dbuf, bb(nd)], w=[bb(ek)])
                tr.dve(lambda e, c=c, w_=w_: e.tensor_tensor(
                    kj[:, 0:w_], g["kh_bf"][:, c * CH:c * CH + w_],
                    ek[:, 0:w_], ALU.mult),
                    r=[bb(g["kh_bf"]), bb(ek)], w=[bb(kj)])
                tr.pe(lambda e, J=J, w_=w_: e.matmul(
                    pP[0:w_, 32 * J:w_], kj[0:GM, 0:w_], qj[:],
                    start=True, stop=True),
                    r=[bb(kj), bb(qj)], w=[bb(pP)])
            tr.dve(lambda e: e.tensor_tensor(Pt[:], pP[:], g["glam_f"][:],
                                             ALU.mult),
                   r=[bb(pP), bb(g["glam_f"])], w=[bb(Pt)])
            for w0 in range(0, NDC, 5):
                dcs = range(w0, min(w0 + 5, NDC))
                for i, dc in enumerate(dcs):
                    tr.pe(lambda e, dc=dc, i=i, u=u: e.matmul(
                        po[i][:], u[:, dc * 128:(dc + 1) * 128], Pt[:],
                        start=True, stop=False),
                        r=[bb(u), bb(Pt)], w=[bb(po[i])])
                    tr.pe(lambda e, dc=dc, i=i: e.matmul(
                        po[i][:], g["S_bf"][:, dc * 128:(dc + 1) * 128], q0[:],
                        start=False, stop=True),
                        r=[bb(g["S_bf"]), bb(q0)], w=[bb(po[i])])
                for i, dc in enumerate(dcs):
                    tr.dve(lambda e, dc=dc, i=i, c=c: e.scalar_tensor_tensor(
                        xb[dc][:, c * CH:(c + 1) * CH], po[i][:], SCALE_GLA,
                        xb[dc][:, c * CH:(c + 1) * CH], ALU.mult, ALU.add),
                        r=[bb(po[i])], w=[bb(xb[dc])])
            tr.act(lambda e, dsl=dsl: e.activation(
                es[:], nd[:], AF.Exp, bias=dsl[:, CH - 1:CH]),
                r=[dbuf, bb(nd)], w=[bb(es)])
            tr.dve(lambda e, c=c: e.tensor_tensor(
                kS[:], g["kh_bf"][:, c * CH:(c + 1) * CH], es[:], ALU.mult),
                r=[bb(g["kh_bf"]), bb(es)], w=[bb(kS)])
            tr.pe(lambda e: e.transpose(pT[:], kS[:], g["ident_bf"][0:GM, 0:GM]),
                  r=[bb(kS), bb(g["ident_bf"])], w=[bb(pT)])
            tr.dve(lambda e: e.tensor_copy(kT[:], pT[:]), r=[bb(pT)], w=[bb(kT)])
            for dg in range(4):
                tr.pe(lambda e, dg=dg, u=u: e.matmul(
                    pI[:], kT[:, 0:GM], u[:, dg * 512:(dg + 1) * 512],
                    start=True, stop=True),
                    r=[bb(kT), bb(u)], w=[bb(pI)])
                tr.dve(lambda e, c=c, dg=dg: e.scalar_tensor_tensor(
                    g["S_f"][:, dg * 512:(dg + 1) * 512],
                    g["S_f"][:, dg * 512:(dg + 1) * 512],
                    g["eD_f"][:, c:c + 1], pI[:], ALU.mult, ALU.add),
                    r=[bb(g["eD_f"]), bb(pI)], w=[bb(g["S_f"])])

        tr.dma("sp", lambda e: e.dma_start(out=s_stage[0:GM, :], in_=g["S_f"][:]),
               r=[bb(g["S_f"])], w=[bb(s_stage)])
        tr.dma("sp", lambda e: e.dma_start(
            out=s_stage[GM:GM + 1, 0:16], in_=g["gpart"][:]),
            r=[bb(g["gpart"])], w=[bb(s_stage)])
        tr.cc(lambda e: e.collective_compute(
            "AllGather", ALU.bypass,
            replica_groups=[[0, 1, 2, 3, 4, 5, 6, 7]],
            ins=[s_stage.opt()], outs=[s_gath.opt()]),
            r=[bb(s_stage)], w=[bb(s_gath)])


def _p4(nc, tr, g, ein, s_gath, w12qc, w3qc):
    bb = tr.buf
    xb = g["xb"]
    with ExitStack() as px:
        E = px.enter_context
        sg_t = E(nc.sbuf_tensor("p4sg", [GM, D], F32))
        e0 = E(nc.sbuf_tensor("p4e0", [GM, CH], BF16))
        q0 = E(nc.sbuf_tensor("p4q0", [GM, CH], BF16))
        q0a = E(nc.sbuf_tensor("p4q0a", [GM, CH], BF16))
        Acum = E(nc.sbuf_tensor("p4A", [GM, NCH + 1], F32))
        Acs = E(nc.sbuf_tensor("p4Ac", [GM, NCH + 1], F32))
        grows = E(nc.sbuf_tensor("p4grows", [1, 24], F32))
        gsum = E(nc.sbuf_tensor("p4gsum", [1, 12], F32))
        g["_dbg_grows"] = grows
        g["_dbg_gsum"] = gsum
        g["_dbg_sgath"] = s_gath
        po = [E(nc.psum_tensor(f"p4po{i}", [128, CH], F32)) for i in range(6)]
        pb = E(nc.psum_tensor("p4pb", [128, 1], F32))

        tr.gx(lambda e: e.memset(g["S_f"][:], 0.0), w=[bb(g["S_f"])])
        for r in range(8):
            tr.dma("sp", lambda e, r=r: e.dma_start(
                out=sg_t[:], in_=s_gath[r * (GM + 1):r * (GM + 1) + GM, :]),
                r=[bb(s_gath)], w=[bb(sg_t)])
            tr.dve(lambda e, r=r: e.scalar_tensor_tensor(
                g["S_f"][:], sg_t[:], g["smask_sb"][:, r:r + 1], g["S_f"][:],
                ALU.mult, ALU.add),
                r=[bb(sg_t), bb(g["smask_sb"])], w=[bb(g["S_f"])])
        tr.dve(lambda e: e.tensor_copy(g["S_bf"][:], g["S_f"][:]),
               r=[bb(g["S_f"])], w=[bb(g["S_bf"])])
        tr.gx(lambda e: e.memset(Acum[:, 0:NCH + 1], 1.0), w=[bb(Acum)])
        tr.gx(lambda e: e.memset(Acs[:, 0:NCH + 1], 1.0), w=[bb(Acs)])
        for c in range(NCH):
            tr.dve(lambda e, c=c: e.tensor_tensor(
                Acum[:, c + 1:c + 2], Acs[:, c:c + 1], g["eD_f"][:, c:c + 1],
                ALU.mult), r=[bb(g["eD_f"]), bb(Acs)], w=[bb(Acum)])
            tr.act(lambda e, c=c: e.copy(Acs[:, c + 1:c + 2],
                                         Acum[:, c + 1:c + 2]),
                   r=[bb(Acum)], w=[bb(Acs)])

        dbuf = bb(g["d_f"])
        for c in range(NCH):
            dsl = g["d_f"][:, c * CH:(c + 1) * CH]
            tr.act(lambda e, dsl=dsl: e.activation(e0[:], dsl, AF.Exp),
                   r=[dbuf], w=[bb(e0)])
            tr.dve(lambda e, c=c: e.tensor_tensor(
                q0[:], g["r_bf"][:, c * CH:(c + 1) * CH], e0[:], ALU.mult),
                r=[bb(g["r_bf"]), bb(e0)], w=[bb(q0)])
            tr.dve(lambda e, c=c: e.tensor_scalar_mul(
                q0a[:], q0[:], Acs[:, c:c + 1]),
                r=[bb(q0), bb(Acs)], w=[bb(q0a)])
            for w0 in range(0, NDC, 6):
                dcs = range(w0, min(w0 + 6, NDC))
                for i, dc in enumerate(dcs):
                    tr.pe(lambda e, dc=dc, i=i: e.matmul(
                        po[i][:], g["S_bf"][:, dc * 128:(dc + 1) * 128], q0a[:],
                        start=True, stop=True),
                        r=[bb(g["S_bf"]), bb(q0a)], w=[bb(po[i])])
                for i, dc in enumerate(dcs):
                    tr.dve(lambda e, dc=dc, i=i, c=c: e.scalar_tensor_tensor(
                        xb[dc][:, c * CH:(c + 1) * CH], po[i][:], SCALE_GLA,
                        xb[dc][:, c * CH:(c + 1) * CH], ALU.mult, ALU.add),
                        r=[bb(po[i])], w=[bb(xb[dc])])

        for r in range(8):
            tr.dma("sp", lambda e, r=r: e.dma_start(
                out=grows[0:1, r * 3:(r + 1) * 3],
                in_=s_gath[r * (GM + 1) + GM:r * (GM + 1) + GM + 1, 0:3]),
                r=[bb(s_gath)], w=[bb(grows)])
        RW = dict(r=[bb(grows), bb(gsum)], w=[bb(grows), bb(gsum)])
        for p_ in range(4):
            tr.dve(lambda e, p_=p_: e.tensor_tensor(
                grows[0:1, p_ * 3:p_ * 3 + 3], grows[0:1, 2 * p_ * 3:2 * p_ * 3 + 3],
                grows[0:1, (2 * p_ + 1) * 3:(2 * p_ + 1) * 3 + 3], ALU.add),
                **RW)
        tr.act(lambda e: e.copy(gsum[:, 0:12], grows[0:1, 0:12]), **RW)
        for p_ in range(2):
            tr.dve(lambda e, p_=p_: e.tensor_tensor(
                grows[0:1, p_ * 3:p_ * 3 + 3], gsum[0:1, 2 * p_ * 3:2 * p_ * 3 + 3],
                gsum[0:1, (2 * p_ + 1) * 3:(2 * p_ + 1) * 3 + 3], ALU.add),
                **RW)
        tr.act(lambda e: e.copy(gsum[:, 0:6], grows[0:1, 0:6]), **RW)
        tr.dve(lambda e: e.tensor_tensor(
            grows[0:1, 0:3], gsum[0:1, 0:3], gsum[0:1, 3:6], ALU.add), **RW)
        tr.act(lambda e: e.copy(gsum[:, 0:3], grows[0:1, 0:3]), **RW)
        for j in range(3):
            tr.dve(lambda e, j=j: e.tensor_scalar(
                gsum[:, j:j + 1], gsum[:, j:j + 1], 0.5 / (FH * D), 0.5e-8,
                ALU.mult, ALU.add), **RW)
        tr.act(lambda e: e.copy(gsum[:, 4:7], gsum[:, 0:3]), **RW)
        for j in range(3):
            tr.pe(lambda e, j=j: e.matmul(
                pb[:], g["onesr_f"][:], gsum[0:1, 4 + j:5 + j], start=True,
                stop=True), r=[bb(gsum), bb(g["onesr_f"])], w=[bb(pb)])
            tr.dve(lambda e, j=j: e.tensor_copy(g["thr_s"][:, j:j + 1], pb[:]),
                   r=[bb(pb)], w=[bb(g["thr_s"])])
            tr.act(lambda e, j=j: e.mul(
                g["nthr_s"][:, j:j + 1], g["thr_s"][:, j:j + 1], -1.0),
                r=[bb(g["thr_s"])], w=[bb(g["nthr_s"])])

        wf_t = [E(nc.sbuf_tensor(f"p4wf{i}", [128, 1408], F32)) for i in range(2)]
        bl_t = [E(nc.sbuf_tensor(f"p4bl{i}", [128, 1408], BF16)) for i in range(2)]
        wq_t = [E(nc.sbuf_tensor(f"p4wq{i}", [128, 1408], BF16)) for i in range(2)]

        qjobs = []
        for hc in range(NHC):
            for rh in range(2):
                r0 = hc * D + rh * 1024
                c0 = hc * 2 * D + rh * 1024
                qjobs.append((ein["w14"][r0:r0 + 1024, :],
                              w12qc[c0:c0 + 1024, :], w12qc, 0, 1024))
                qjobs.append((ein["w24"][r0:r0 + 1024, :],
                              w12qc[c0 + D:c0 + D + 1024, :], w12qc, 1, 1024))
        for dc in range(NDC):
            for re4 in range(4):
                r0 = dc * FH + re4 * 1408
                qjobs.append((ein["w34"][r0:r0 + 1408, :],
                              w3qc[r0:r0 + 1408, :], w3qc, 2, 1408))
        for i, (src, dst, track, j, rows) in enumerate(qjobs):
            rows = rows
            wf = wf_t[i % 2]
            bl = bl_t[i % 2]
            wq = wq_t[i % 2]
            nc_ = rows
            em = tr.dve
            tr.dma("sp" if i % 2 == 0 else "g",
                   lambda e, src=src, wf=wf, nc_=nc_: e.dma_start(
                       out=wf[:, 0:nc_],
                       in_=src.rearrange("(i p) c -> p i c", p=128)),
                   w=[bb(wf)])
            em(lambda e, wf=wf, bl=bl, nc_=nc_, j=j: e.tensor_scalar(
                bl[:, 0:nc_], wf[:, 0:nc_], g["nthr_s"][:, j:j + 1], None,
                ALU.is_lt),
                r=[bb(wf), bb(g["nthr_s"])], w=[bb(bl)])
            em(lambda e, wf=wf, bl=bl, wq=wq, nc_=nc_, j=j: e.scalar_tensor_tensor(
                wq[:, 0:nc_], wf[:, 0:nc_], g["thr_s"][:, j:j + 1], bl[:, 0:nc_],
                ALU.is_gt, ALU.subtract),
                r=[bb(wf), bb(bl), bb(g["thr_s"])], w=[bb(wq)])
            tr.dma("g" if i % 2 == 0 else "sp",
                   lambda e, dst=dst, wq=wq, nc_=nc_: e.dma_start(
                       out=dst.rearrange("(i p) c -> p i c", p=128),
                       in_=wq[:, 0:nc_]),
                   r=[bb(wq)], w=[bb(track)])



_LN_N = [0]


def _ln_x(nc, tr, g):
    bb = tr.buf
    xb, ln = g["xb"], g["ln"]
    _LN_N[0] += 1
    u_ = f"_{_LN_N[0]}"
    with ExitStack() as px:
        E = px.enter_context
        sq = [E(nc.sbuf_tensor(f"lnsq{i}{u_}", [128, 512], BF16)) for i in range(2)]
        scr = E(nc.sbuf_tensor("lnscr" + u_, [128, 512], F32))
        g["row_t"] = E(nc.sbuf_tensor("lnrowt" + u_, [1, TPC], F32))
        psx = E(nc.psum_tensor("lnpsx" + u_, [1, 512], F32))
        psq = E(nc.psum_tensor("lnpsq" + u_, [1, 512], F32))
        psb = [E(nc.psum_tensor(f"lnpsb{i}{u_}", [128, 512], F32)) for i in range(2)]

        for tg in range(NTG):
            t0 = tg * 512
            for dc in range(NDC):
                s = sq[dc % 2]
                tr.act(lambda e, dc=dc, t0=t0, s=s: e.activation(
                    s[:], xb[dc][:, t0:t0 + 512], AF.Square),
                    r=[bb(xb[dc])], w=[bb(s)])
                tr.pe(lambda e, dc=dc, t0=t0: e.matmul(
                    psx[:], g["ones_bf"][:], xb[dc][:, t0:t0 + 512],
                    start=(dc == 0), stop=(dc == NDC - 1)),
                    r=[bb(xb[dc]), bb(g["ones_bf"])], w=[bb(psx)])
                tr.pe(lambda e, s=s, dc=dc: e.matmul(
                    psq[:], g["ones_bf"][:], s[:],
                    start=(dc == 0), stop=(dc == NDC - 1)),
                    r=[bb(s), bb(g["ones_bf"])], w=[bb(psq)])
            tr.dve(lambda e, t0=t0: e.tensor_copy(
                g["row_m"][:, t0:t0 + 512], psx[:]),
                r=[bb(psx)], w=[bb(g["row_m"])])
            tr.dve(lambda e, t0=t0: e.tensor_copy(
                g["row_r"][:, t0:t0 + 512], psq[:]),
                r=[bb(psq)], w=[bb(g["row_r"])])
        _stats_pipeline(nc, tr, g, n=TPC)
        for tg in range(NTG):
            t0 = tg * 512
            tr.pe(lambda e, t0=t0: e.matmul(
                psb[0][:], g["onesr_f"][:], g["row_m"][0:1, t0:t0 + 512],
                start=True, stop=True),
                r=[bb(g["row_m"]), bb(g["onesr_f"])], w=[bb(psb[0])])
            tr.pe(lambda e, t0=t0: e.matmul(
                psb[1][:], g["onesr_f"][:], g["row_r"][0:1, t0:t0 + 512],
                start=True, stop=True),
                r=[bb(g["row_r"]), bb(g["onesr_f"])], w=[bb(psb[1])])
            for dc in range(NDC):
                tr.dve(lambda e, dc=dc, t0=t0: e.tensor_tensor(
                    scr[:], xb[dc][:, t0:t0 + 512], psb[0][:], ALU.subtract),
                    r=[bb(xb[dc]), bb(psb[0])], w=[bb(scr)])
                tr.dve(lambda e, dc=dc, t0=t0: e.tensor_tensor(
                    ln[dc][:, t0:t0 + 512], scr[:], psb[1][:], ALU.mult),
                    r=[bb(scr), bb(psb[1])], w=[bb(ln[dc])])



_FFN_N = [0]


def _p_ffn(nc, tr, g, w12qc, w3qc, y):
    bb = tr.buf
    xb, ln = g["xb"], g["ln"]
    _FFN_N[0] += 1
    u_ = f"_{_FFN_N[0]}"
    with ExitStack() as px:
        E = px.enter_context
        hid = [E(nc.sbuf_tensor(f"fh{i}{u_}", [128, 512], BF16)) for i in range(HHC)]
        wq12 = [E(nc.sbuf_tensor(f"fwq12{i}{u_}", [128, 4096], BF16)) for i in range(2)]
        wq3 = [E(nc.sbuf_tensor(f"fwq3{i}{u_}", [128, HHC * 128], BF16))
               for i in range(2)]
        sg = E(nc.sbuf_tensor("fsg" + u_, [128, 512], BF16))
        outf = E(nc.sbuf_tensor("foutf" + u_, [128, 512], F32))
        psA = [E(nc.psum_tensor(f"fpsA{i}{u_}", [128, 512], F32)) for i in range(2)]
        psB = [E(nc.psum_tensor(f"fpsB{i}{u_}", [128, 512], F32)) for i in range(2)]
        ps3 = [E(nc.psum_tensor(f"fps3{i}{u_}", [128, 512], F32)) for i in range(2)]

        for tt in range(NTG):
            t0 = tt * 512
            for hh in range(2):
                for i in range(HHC):
                    hc = hh * HHC + i
                    b12 = wq12[i % 2]
                    b1 = b12[:, 0:2048]
                    b2 = b12[:, 2048:4096]
                    tr.dma("sp" if i % 2 == 0 else "g",
                           lambda e, hc=hc, b12=b12: e.dma_start(
                               out=b12[:], in_=w12qc[hc * 2 * D:(hc + 1) * 2 * D, :]
                               .rearrange("(i p) c -> p i c", p=128)),
                           r=[bb(w12qc)], w=[bb(b12)])
                    pA, pB = psA[i % 2], psB[i % 2]
                    b12b = bb(wq12[i % 2])
                    for ic in range(NDC):
                        tr.pe(lambda e, ic=ic, t0=t0, b1=b1, pA=pA: e.matmul(
                            pA[:], b1[:, ic * 128:(ic + 1) * 128],
                            ln[ic][:, t0:t0 + 512],
                            start=(ic == 0), stop=(ic == NDC - 1)),
                            r=[b12b, bb(ln[ic])], w=[bb(pA)])
                    for ic in range(NDC):
                        tr.pe(lambda e, ic=ic, t0=t0, b2=b2, pB=pB: e.matmul(
                            pB[:], b2[:, ic * 128:(ic + 1) * 128],
                            ln[ic][:, t0:t0 + 512],
                            start=(ic == 0), stop=(ic == NDC - 1)),
                            r=[b12b, bb(ln[ic])], w=[bb(pB)])
                    tr.act(lambda e, pA=pA: e.activation(sg[:], pA[:], AF.Sigmoid),
                           r=[bb(pA)], w=[bb(sg)])
                    tr.dve(lambda e, i=i, pB=pB: e.tensor_tensor(
                        hid[i][:], sg[:], pB[:], ALU.mult),
                        r=[bb(sg), bb(pB)], w=[bb(hid[i])])
                for dc in range(NDC):
                    b3 = wq3[dc % 2]
                    r0 = dc * FH + hh * (HHC * 128)
                    tr.dma("sp" if dc % 2 == 0 else "g",
                           lambda e, r0=r0, b3=b3: e.dma_start(
                               out=b3[:], in_=w3qc[r0:r0 + HHC * 128, :]
                               .rearrange("(i p) c -> p i c", p=128)),
                           r=[bb(w3qc)], w=[bb(b3)])
                    p3 = ps3[dc % 2]
                    for i in range(HHC):
                        tr.pe(lambda e, i=i, b3=b3, p3=p3: e.matmul(
                            p3[:], b3[:, i * 128:(i + 1) * 128], hid[i][:],
                            start=(i == 0), stop=(i == HHC - 1)),
                            r=[bb(b3), bb(hid[i])], w=[bb(p3)])
                    if y is None or hh == 0:
                        tr.dve(lambda e, dc=dc, t0=t0, p3=p3: e.scalar_tensor_tensor(
                            xb[dc][:, t0:t0 + 512], p3[:], 1.0,
                            xb[dc][:, t0:t0 + 512], ALU.mult, ALU.add),
                            r=[bb(p3)], w=[bb(xb[dc])])
                    else:
                        tr.dve(lambda e, dc=dc, t0=t0, p3=p3: e.scalar_tensor_tensor(
                            outf[:], p3[:], 1.0, xb[dc][:, t0:t0 + 512],
                            ALU.mult, ALU.add),
                            r=[bb(p3), bb(xb[dc])], w=[bb(outf)])
                        tr.dma("sp", lambda e, dc=dc, t0=t0: e.dma_start(
                            out=y[dc * 128:(dc + 1) * 128, t0:t0 + 512],
                            in_=outf[:]), r=[bb(outf)], w=[bb(y)])


def _p6(nc, tr, g, ein, q_dram, kv_stage, kv_gath):
    bb = tr.buf
    ln = g["ln"]
    with ExitStack() as px:
        E = px.enter_context
        wt = [E(nc.sbuf_tensor(f"p6w{i}", [128, NDC * 512], BF16)) for i in range(2)]
        cosc = E(nc.sbuf_tensor("p6cos", [128, NCH * 256], BF16))
        sinc = E(nc.sbuf_tensor("p6sin", [128, NCH * 256], BF16))
        brow = [E(nc.sbuf_tensor(f"p6brow{i}", [1, 512], BF16)) for i in range(2)]
        qk = E(nc.sbuf_tensor("p6qk", [128, 512], BF16))
        rq = E(nc.sbuf_tensor("p6rq", [128, 512], BF16))
        tA = E(nc.sbuf_tensor("p6tA", [128, 256], BF16))
        tB = E(nc.sbuf_tensor("p6tB", [128, 256], BF16))
        ksq = E(nc.sbuf_tensor("p6ksq", [128, 512], BF16))
        nscr = E(nc.sbuf_tensor("p6nscr", [128, 1], F32))
        nstore = E(nc.sbuf_tensor("p6nst", [128, 512], F32))
        nsum = E(nc.sbuf_tensor("p6nsum", [128, 32], F32))
        ntr = E(nc.sbuf_tensor("p6ntr", [32, 1], F32))
        nflat = E(nc.sbuf_tensor("p6nflat", [1, 32], F32))
        ex2 = E(nc.sbuf_tensor("p6ex2", [1, 2], F32))
        ps = [E(nc.psum_tensor(f"p6ps{i}", [128, 512], F32)) for i in range(2)]
        pnt = E(nc.psum_tensor("p6pnt", [32, 128], F32))

        tr.dma("g", lambda e: e.dma_start(
            out=cosc[:], in_=ein["cosP"].rearrange("(tc p) c -> p tc c", p=128)),
            w=[bb(cosc)])
        tr.dma("g", lambda e: e.dma_start(
            out=sinc[:], in_=ein["sinP"].rearrange("(tc p) c -> p tc c", p=128)),
            w=[bb(sinc)])
        tr.gx(lambda e: e.memset(nsum[:], 0.0), w=[bb(nsum)])
        tr.gx(lambda e: e.memset(nstore[:], 0.0), w=[bb(nstore)])

        for ng in range(12):
            w_ = wt[ng % 2]
            br = brow[ng % 2]
            if ng < 8:
                src = ein["wqkT"]
                co = ng * 512
                bsrc = ein["bqkrow"][0:1, ng * 512:(ng + 1) * 512]
            else:
                src = ein["wvT"]
                co = (ng - 8) * 512
                bsrc = ein["bvrow"][0:1, co:co + 512]
            tr.dma("g", lambda e, bsrc=bsrc, br=br: e.dma_start(
                out=br[:], in_=bsrc), w=[bb(br)])
            for ic in range(NDC):
                tr.dma("g", lambda e, ic=ic, co=co, src=src, w_=w_: e.dma_start(
                    out=w_[:, ic * 512:(ic + 1) * 512],
                    in_=src[ic * 128:(ic + 1) * 128, co:co + 512]), w=[bb(w_)])
            for tc in range(NCH):
                p = ps[tc % 2]
                for ic in range(NDC):
                    tr.pe(lambda e, ic=ic, tc=tc, p=p, w_=w_: e.matmul(
                        p[:], ln[ic][:, tc * 128:(tc + 1) * 128],
                        w_[:, ic * 512:(ic + 1) * 512],
                        start=(ic == 0), stop=False),
                        r=[bb(ln[ic]), bb(w_)], w=[bb(p)])
                tr.pe(lambda e, p=p, br=br: e.matmul(
                    p[:], g["onesr_bf"][:], br[0:1, :],
                    start=False, stop=True),
                    r=[bb(g["onesr_bf"]), bb(br)], w=[bb(p)])
                tr.dve(lambda e, p=p: e.tensor_copy(qk[:], p[:]),
                       r=[bb(p)], w=[bb(qk)])
                if ng >= 8:
                    tr.dma("sp", lambda e, tc=tc, co=co: e.dma_start(
                        out=kv_stage[TPC + tc * 128:TPC + (tc + 1) * 128,
                                     co:co + 512], in_=qk[:]),
                        r=[bb(qk)], w=[bb(kv_stage)])
                    continue
                cs4 = cosc[:, tc * 256:(tc + 1) * 256].rearrange(
                    "p (h c) -> p h c", c=GM)
                sn4 = sinc[:, tc * 256:(tc + 1) * 256].rearrange(
                    "p (h c) -> p h c", c=GM)
                tA4 = tA[:].rearrange("p (h c) -> p h c", c=GM)
                tB4 = tB[:].rearrange("p (h c) -> p h c", c=GM)
                qk4 = qk[:].rearrange("p (h c) -> p h c", c=128)
                rq4 = rq[:].rearrange("p (h c) -> p h c", c=128)
                qe, qo = qk4[:, :, 0:GM], qk4[:, :, GM:128]
                re_, ro = rq4[:, :, 0:GM], rq4[:, :, GM:128]
                csb, snb = bb(cosc), bb(sinc)
                tr.dve(lambda e, qe=qe, cs4=cs4, tA4=tA4: e.tensor_tensor(
                    tA4, qe, cs4, ALU.mult), r=[bb(qk), csb], w=[bb(tA)])
                tr.dve(lambda e, qo=qo, sn4=sn4, tB4=tB4: e.tensor_tensor(
                    tB4, qo, sn4, ALU.mult), r=[bb(qk), snb], w=[bb(tB)])
                tr.dve(lambda e, re_=re_, tA4=tA4, tB4=tB4: e.tensor_tensor(
                    re_, tA4, tB4, ALU.subtract),
                    r=[bb(tA), bb(tB)], w=[bb(rq)])
                tr.dve(lambda e, qe=qe, sn4=sn4, tA4=tA4: e.tensor_tensor(
                    tA4, qe, sn4, ALU.mult), r=[bb(qk), snb], w=[bb(tA)])
                tr.dve(lambda e, qo=qo, cs4=cs4, tB4=tB4: e.tensor_tensor(
                    tB4, qo, cs4, ALU.mult), r=[bb(qk), csb], w=[bb(tB)])
                tr.dve(lambda e, ro=ro, tA4=tA4, tB4=tB4: e.tensor_tensor(
                    ro, tA4, tB4, ALU.add),
                    r=[bb(tA), bb(tB)], w=[bb(rq)])
                if ng < 4:
                    tr.dma("sp", lambda e, tc=tc, ng=ng: e.dma_start(
                        out=q_dram[tc * 128:(tc + 1) * 128,
                                   ng * 512:(ng + 1) * 512], in_=rq[:]),
                        r=[bb(rq)], w=[bb(q_dram)])
                else:
                    tr.dma("sp", lambda e, tc=tc, ng=ng: e.dma_start(
                        out=kv_stage[tc * 128:(tc + 1) * 128,
                                     (ng - 4) * 512:(ng - 3) * 512], in_=rq[:]),
                        r=[bb(rq)], w=[bb(kv_stage)])
                tr.act(lambda e: e.activation(ksq[:], rq[:], AF.Square),
                       r=[bb(rq)], w=[bb(ksq)])
                for hh in range(4):
                    colid = ng * 4 + hh
                    tr.dve(lambda e, hh=hh, tc=tc, colid=colid: e.tensor_reduce(
                        nstore[:, tc * 32 + colid:tc * 32 + colid + 1],
                        ksq[:, hh * 128:(hh + 1) * 128],
                        mybir.AxisListType.X, ALU.add),
                        r=[bb(ksq)], w=[bb(nstore)])

        for colid in range(32):
            tr.dve(lambda e, colid=colid: e.tensor_reduce(
                nsum[:, colid:colid + 1],
                nstore[:, colid:512:32], mybir.AxisListType.X, ALU.max),
                r=[bb(nstore)], w=[bb(nsum)])
        ident_f = px.enter_context(nc.sbuf_tensor("p6idf", [128, 128], F32))
        ndram = nc.dram_tensor("p6ndram", [32, 1], F32).ap()
        tr.dma("sp", lambda e: e.dma_start(out=ident_f[:], in_=ein["ident"]),
               w=[bb(ident_f)])
        tr.pe(lambda e: e.transpose(pnt[:], nsum[:], ident_f[:]),
              r=[bb(nsum), bb(ident_f)], w=[bb(pnt)])
        tr.dve(lambda e: e.tensor_reduce(ntr[:, 0:1], pnt[:],
                                         mybir.AxisListType.X, ALU.max),
               r=[bb(pnt)], w=[bb(ntr)])
        tr.dma("g", lambda e: e.dma_start(out=ndram, in_=ntr[:]),
               r=[bb(ntr)], w=[bb(ndram)])
        tr.dma("g", lambda e: e.dma_start(
            out=nflat[:], in_=ndram.rearrange("r c -> c r")),
            r=[bb(ndram)], w=[bb(nflat)])
        tr.dve(lambda e: e.tensor_reduce(ex2[:, 0:1], nflat[0:1, 0:16],
                                         mybir.AxisListType.X, ALU.max),
               r=[bb(nflat)], w=[bb(ex2)])
        tr.dve(lambda e: e.tensor_reduce(ex2[:, 1:2], nflat[0:1, 16:32],
                                         mybir.AxisListType.X, ALU.max),
               r=[bb(nflat)], w=[bb(ex2)])
        tr.act(lambda e: e.mul(ex2[:, 0:2], ex2[:, 0:2], 1.05),
               r=[bb(ex2)], w=[bb(ex2)])
        tr.dma("g", lambda e: e.dma_start(
            out=kv_stage[2 * TPC:2 * TPC + 1, 0:2], in_=ex2[:]),
            r=[bb(ex2)], w=[bb(kv_stage)])
        tr.cc(lambda e: e.collective_compute(
            "AllGather", ALU.bypass,
            replica_groups=[[0, 1, 2, 3, 4, 5, 6, 7]],
            ins=[kv_stage.opt()], outs=[kv_gath.opt()]),
            r=[bb(kv_stage)], w=[bb(kv_gath)])



def _p7(nc, tr, g, ein, kv_stage, kv_gath, kbuf1, vbuf1, q_dram):
    bb = tr.buf
    xb = g["xb"]
    av = g["ln"]
    with ExitStack() as px:
        E = px.enter_context
        bounce = [E(nc.sbuf_tensor(f"p7b{i}", [128, D], BF16)) for i in range(2)]
        tri_bf = E(nc.sbuf_tensor("p7tri", [128, 4 * 512], BF16))
        mrow = E(nc.sbuf_tensor("p7mrow", [1, 16], BF16))
        mser = E(nc.sbuf_tensor("p7mser", [1, 4], F32))
        mser2 = E(nc.sbuf_tensor("p7mser2", [1, 4], F32))
        negM = E(nc.sbuf_tensor("p7negM", [128, 1], F32))
        bias_all = E(nc.sbuf_tensor("p7bias", [128, NSLOT], F32))
        Ksb = E(nc.sbuf_tensor("p7K", [128, 2 * TPC], BF16))
        Vsb = E(nc.sbuf_tensor("p7V", [128, 2 * TPC], BF16))
        Qsb = E(nc.sbuf_tensor("p7Q", [128, TPC], BF16))
        Pt = [E(nc.sbuf_tensor(f"p7P{i}", [128, 512], BF16)) for i in range(3)]
        srow = E(nc.sbuf_tensor("p7srow", [1, 512], F32))
        rbsb = E(nc.sbuf_tensor("p7rb", [128, 512], F32))
        wo = [E(nc.sbuf_tensor(f"p7wo{i}", [128, 2048], BF16)) for i in range(2)]
        psS = [E(nc.psum_tensor(f"p7psS{i}", [128, 512], F32)) for i in range(2)]
        psAV = E(nc.psum_tensor("p7psAV", [128, 512], F32))
        psSum = E(nc.psum_tensor("p7psSum", [1, 512], F32))
        psR = E(nc.psum_tensor("p7psR", [128, 512], F32))
        pso = [E(nc.psum_tensor(f"p7pso{i}", [128, 512], F32)) for i in range(2)]

        for i in range(4):
            tr.dma("g", lambda e, i=i: e.dma_start(
                out=tri_bf[:, i * 512:(i + 1) * 512],
                in_=ein["triband"][i * 128:(i + 1) * 128, :]), w=[bb(tri_bf)])

        for grp in range(16):
            bt = bounce[grp % 2]
            tr.dma("g", lambda e, grp=grp, bt=bt: e.indirect_dma_start(
                bt[:], None, kv_gath, IndirectOffsetOnAxis(
                    g["idxk_sb"][:, grp:grp + 1], 0)),
                r=[bb(kv_gath), bb(g["idxk_sb"])], w=[bb(bt)])
            tr.dma("g", lambda e, grp=grp, bt=bt: e.dma_start(
                out=kbuf1[grp * 128:(grp + 1) * 128, :], in_=bt[:]),
                r=[bb(bt)], w=[bb(kbuf1)])
        for grp in range(16):
            bt = bounce[grp % 2]
            tr.dma("g", lambda e, grp=grp, bt=bt: e.indirect_dma_start(
                bt[:], None, kv_gath, IndirectOffsetOnAxis(
                    g["idxv_sb"][:, grp:grp + 1], 0)),
                r=[bb(kv_gath), bb(g["idxv_sb"])], w=[bb(bt)])
            tr.dma("g", lambda e, grp=grp, bt=bt: e.dma_start(
                out=vbuf1[grp * 128:(grp + 1) * 128, :], in_=bt[:]),
                r=[bb(bt)], w=[bb(vbuf1)])

        for r in range(8):
            tr.dma("sp", lambda e, r=r: e.dma_start(
                out=mrow[0:1, r * 2:(r + 1) * 2],
                in_=kv_gath[r * KV_ROWS + 2 * TPC:r * KV_ROWS + 2 * TPC + 1, 0:2]),
                r=[bb(kv_gath)], w=[bb(mrow)])
        tr.dve(lambda e: e.tensor_reduce(mser[:, 0:1], mrow[0:1, 0:16:2],
                                         mybir.AxisListType.X, ALU.max),
               r=[bb(mrow)], w=[bb(mser)])
        tr.dve(lambda e: e.tensor_reduce(mser[:, 1:2], mrow[0:1, 1:16:2],
                                         mybir.AxisListType.X, ALU.max),
               r=[bb(mrow)], w=[bb(mser)])
        tr.act(lambda e: e.copy(mser2[:, 0:2], mser[:, 0:2]),
               r=[bb(mser)], w=[bb(mser2)])
        tr.dve(lambda e: e.tensor_tensor(mser[:, 2:3], mser2[:, 0:1],
                                         mser2[:, 1:2], ALU.mult),
               r=[bb(mser2)], w=[bb(mser)])
        tr.act(lambda e: e.activation(mser[:, 3:4], mser[:, 2:3], AF.Sqrt),
               r=[], w=[bb(mser)])
        tr.dve(lambda e: e.tensor_scalar_mul(mser[:, 3:4], mser[:, 3:4], -1.0),
               r=[], w=[bb(mser)])
        tr.pe(lambda e: e.matmul(psR[:, 0:1], g["onesr_f"][:], mser[0:1, 3:4],
                                 start=True, stop=True),
              r=[bb(mser), bb(g["onesr_f"])], w=[bb(psR)])
        tr.dve(lambda e: e.tensor_copy(negM[:], psR[:, 0:1]),
               r=[bb(psR)], w=[bb(negM)])
        tr.act(lambda e: e.activation(bias_all[:], g["moff_sb"][:],
                                      AF.Identity, bias=negM[:, 0:1]),
               r=[bb(g["moff_sb"]), bb(negM)], w=[bb(bias_all)])

        for h in range(NH):
            for q4 in range(4):
                tr.dma("sp", lambda e, h=h, q4=q4: e.dma_start(
                    out=Ksb[:, q4 * 512:(q4 + 1) * 512],
                    in_=kbuf1[q4 * 512:(q4 + 1) * 512, h * 128:(h + 1) * 128],
                    transpose=True), r=[bb(kbuf1)], w=[bb(Ksb)])
                tr.dma("sp", lambda e, h=h, q4=q4: e.dma_start(
                    out=Ksb[:, TPC + q4 * 512:TPC + (q4 + 1) * 512],
                    in_=kv_stage[q4 * 512:(q4 + 1) * 512, h * 128:(h + 1) * 128],
                    transpose=True), r=[bb(kv_stage)], w=[bb(Ksb)])
                tr.dma("sp", lambda e, h=h, q4=q4: e.dma_start(
                    out=Qsb[:, q4 * 512:(q4 + 1) * 512],
                    in_=q_dram[q4 * 512:(q4 + 1) * 512, h * 128:(h + 1) * 128],
                    transpose=True), r=[bb(q_dram)], w=[bb(Qsb)])
                tr.dma("sp", lambda e, h=h, q4=q4: e.dma_start(
                    out=Vsb[:, q4 * 512:(q4 + 1) * 512],
                    in_=vbuf1[q4 * 512:(q4 + 1) * 512, h * 128:(h + 1) * 128]
                    .rearrange("(kt p) c -> p kt c", p=128)),
                    r=[bb(vbuf1)], w=[bb(Vsb)])
                tr.dma("sp", lambda e, h=h, q4=q4: e.dma_start(
                    out=Vsb[:, TPC + q4 * 512:TPC + (q4 + 1) * 512],
                    in_=kv_stage[TPC + q4 * 512:TPC + (q4 + 1) * 512,
                                 h * 128:(h + 1) * 128]
                    .rearrange("(kt p) c -> p kt c", p=128)),
                    r=[bb(kv_stage)], w=[bb(Vsb)])
            slot = 0
            for gq in range(4):
                nkt = 20 + 4 * gq
                for kt in range(nkt):
                    pS = psS[kt % 2]
                    P = Pt[kt % 3]
                    tr.pe(lambda e, kt=kt, gq=gq, pS=pS: e.matmul(
                        pS[:], Ksb[:, kt * 128:(kt + 1) * 128],
                        Qsb[:, gq * 512:(gq + 1) * 512],
                        start=True, stop=True),
                        r=[bb(Ksb), bb(Qsb)], w=[bb(pS)])
                    tr.act(lambda e, pS=pS, P=P, slot=slot: e.activation(
                        P[:], pS[:], AF.Exp, bias=bias_all[:, slot:slot + 1]),
                        r=[bb(pS), bb(bias_all)], w=[bb(P)])
                    if kt >= 16 + 4 * gq:
                        ti = kt - 16 - 4 * gq
                        tr.dve(lambda e, P=P, ti=ti: e.tensor_tensor(
                            P[:], P[:], tri_bf[:, ti * 512:(ti + 1) * 512],
                            ALU.mult), r=[bb(tri_bf)], w=[bb(P)])
                    tr.pe(lambda e, kt=kt, P=P, nkt=nkt: e.matmul(
                        psAV[:], Vsb[:, kt * 128:(kt + 1) * 128], P[:],
                        start=(kt == 0), stop=(kt == nkt - 1)),
                        r=[bb(Vsb), bb(P)], w=[bb(psAV)])
                    tr.pe(lambda e, kt=kt, P=P, nkt=nkt: e.matmul(
                        psSum[:], g["ones_bf"][:], P[:],
                        start=(kt == 0), stop=(kt == nkt - 1)),
                        r=[bb(g["ones_bf"]), bb(P)], w=[bb(psSum)])
                    slot += 1
                tr.act(lambda e: e.copy(srow[:], psSum[:]),
                       r=[bb(psSum)], w=[bb(srow)])
                tr.dve(lambda e: e.reciprocal(srow[:], srow[:]),
                       r=[bb(srow)], w=[bb(srow)])
                tr.pe(lambda e: e.matmul(psR[:], g["onesr_f"][:], srow[:],
                                         start=True, stop=True),
                      r=[bb(srow), bb(g["onesr_f"])], w=[bb(psR)])
                tr.dve(lambda e: e.tensor_copy(rbsb[:], psR[:]),
                       r=[bb(psR)], w=[bb(rbsb)])
                tr.dve(lambda e, h=h, gq=gq: e.tensor_tensor(
                    av[h][:, gq * 512:(gq + 1) * 512], psAV[:], rbsb[:],
                    ALU.mult), r=[bb(psAV), bb(rbsb)], w=[bb(av[h])])

        for oc in range(NDC):
            w_ = wo[oc % 2]
            tr.dma("g", lambda e, oc=oc, w_=w_: e.dma_start(
                out=w_[:], in_=ein["wout4"][oc * D:(oc + 1) * D, :]
                .rearrange("(i p) c -> p i c", p=128)), w=[bb(w_)])
            for tg in range(NTG):
                p = pso[tg % 2]
                for ic in range(NDC):
                    tr.pe(lambda e, ic=ic, tg=tg, w_=w_, p=p: e.matmul(
                        p[:], w_[:, ic * 128:(ic + 1) * 128],
                        av[ic][:, tg * 512:(tg + 1) * 512],
                        start=(ic == 0), stop=(ic == NDC - 1)),
                        r=[bb(w_), bb(av[ic])], w=[bb(p)])
                tr.dve(lambda e, oc=oc, tg=tg, p=p: e.scalar_tensor_tensor(
                    xb[oc][:, tg * 512:(tg + 1) * 512], p[:],
                    g["bout_s"][:, oc:oc + 1],
                    xb[oc][:, tg * 512:(tg + 1) * 512], ALU.add, ALU.add),
                    r=[bb(p), bb(g["bout_s"])], w=[bb(xb[oc])])



def kernel(**inputs):
    in_maps = _host_inputs(inputs)
    nc = build()
    res = run_bass_kernel_spmd(nc, in_maps, core_ids=list(range(8)))
    out = np.empty((B, T, D), np.float32)
    for c in range(8):
        b, hlf = c // 2, c % 2
        out[b, hlf * TPC:(hlf + 1) * TPC, :] = np.asarray(res.results[c]["y"]).T
    return out


# version 14
# speedup vs baseline: 1.0377x; 1.0377x over previous
import math
import numpy as np
from contextlib import ExitStack

import concourse.bass as bass
import concourse.mybir as mybir
from concourse.bass_utils import run_bass_kernel_spmd
from concourse.bass import IndirectOffsetOnAxis

F32 = mybir.dt.float32
BF16 = mybir.dt.bfloat16
I32 = mybir.dt.int32
AF = mybir.ActivationFunctionType
ALU = mybir.AluOpType

B, T, D = 4, 4096, 2048
TPC = 2048
HALO = 3
NDC = D // 128
NTG = TPC // 512
CH = 128
NCH = TPC // CH
GM = 64
FH = 5632
NHC = FH // 128
NH, HD = 16, 128
SCALE_GLA = 1.0 / math.sqrt(GM)
SCALEB = 1.0 / math.sqrt(HD)
KV_ROWS = 2 * TPC + 1
NSLOT = sum(20 + 4 * g for g in range(4))
LN_EPS = 1e-5
HHC = NHC // 2
_DBG_STOP = [0]



class _Eng:
    def __init__(self, name):
        self.name = name
        self.ops = []
        self.n_dma = 0
        self.n_ms = 0
        self.seq = 0
        self.last_ms_seq = -1
        self.ms_list = []
        self.waited = {}


class _Op:
    __slots__ = ("eng", "fn", "kind", "seq", "phase", "waits", "incs",
                 "ms_val", "dma_val", "tiny")

    def __init__(self, eng, fn, kind, seq, phase):
        self.eng, self.fn, self.kind = eng, fn, kind
        self.seq, self.phase = seq, phase
        self.waits = []
        self.incs = []
        self.ms_val = None
        self.dma_val = None
        self.tiny = False


class _Buf:
    __slots__ = ("writer", "readers")

    def __init__(self):
        self.writer = None
        self.readers = []


class Prog:
    ENG2SECT = {"pe": "tensor", "act": "scalar", "dve": "vector",
                "sp": "sync", "g": "gpsimd"}

    def __init__(self, nc, ctx):
        self.nc = nc
        self.engs = {n: _Eng(n) for n in ("pe", "act", "dve", "sp", "g")}
        self.sems = {}
        for n in ("pe_s", "act_s", "dve_s", "g_s", "sp_s", "sp_d", "g_d", "cc_s"):
            self.sems[n] = ctx.enter_context(nc.semaphore(n))
        self.phase = 0
        self.n_cc = 0
        self._bufs = {}

    def buf(self, obj):
        key = id(obj)
        b = self._bufs.get(key)
        if b is None:
            b = self._bufs[key] = _Buf()
        return b

    def _milestone(self, op):
        e = op.eng
        if op.kind == "dma":
            return (e.name + "_d", 16 * e.n_dma)
        if op.kind == "cc":
            return ("cc_s", self.n_cc)
        if op.ms_val is None:
            if op.seq <= e.last_ms_seq:
                import bisect
                i = bisect.bisect_left(e.ms_list, (op.seq, -1))
                return (e.name + "_s", e.ms_list[i][1])
            e.n_ms += 1
            e.last_ms_seq = op.seq
            op.ms_val = e.n_ms
            op.incs.append((e.name + "_s", 1))
            e.ms_list.append((op.seq, op.ms_val))
        return (e.name + "_s", op.ms_val)

    def _wait(self, cons, prod):
        if prod is None:
            return
        if prod.kind == "exec":
            if prod.eng is cons.eng:
                if not prod.tiny:
                    return
            elif prod.phase < self.phase:
                return
        sem, val = self._milestone(prod)
        w = cons.eng.waited
        if w.get(sem, -1) >= val:
            return
        w[sem] = val
        cons.waits.append((sem, val))

    def emit(self, eng_name, fn, reads=(), writes=(), kind="exec", tiny=False):
        e = self.engs[eng_name]
        op = _Op(e, fn, kind, e.seq, self.phase)
        op.tiny = tiny
        e.seq += 1
        for r in reads:
            b = self.buf(r)
            self._wait(op, b.writer)
            b.readers.append(op)
        for wt in writes:
            b = self.buf(wt)
            self._wait(op, b.writer)
            for r in b.readers:
                self._wait(op, r)
            b.writer = op
            b.readers = []
        if kind == "dma":
            e.n_dma += 1
            op.dma_val = 16 * e.n_dma
            op.incs.append((e.name + "_d", 16))
        elif kind == "cc":
            self.n_cc += 1
            op.dma_val = self.n_cc
            op.incs.append(("cc_s", 1))
        e.ops.append(op)
        return op

    def flush(self):
        nc, sems = self.nc, self.sems
        if any(e.ops for e in self.engs.values()):
            with nc.Block() as blk:
                for name, e in self.engs.items():
                    if not e.ops:
                        continue
                    ops, e.ops = e.ops, []

                    def runner(engine, ops=ops):
                        for op in ops:
                            for sem, val in op.waits:
                                engine.wait_ge(sems[sem], val)
                            inst = op.fn(engine)
                            for sem, amt in op.incs:
                                inst.then_inc(sems[sem], amt)
                    getattr(blk, self.ENG2SECT[name])(runner)
        self.phase += 1

    def pe(self, fn, r=(), w=()):
        return self.emit("pe", fn, r, w)

    def act(self, fn, r=(), w=()):
        return self.emit("act", fn, r, w)

    def dve(self, fn, r=(), w=()):
        return self.emit("dve", fn, r, w)

    def dve_t(self, fn, r=(), w=()):
        return self.emit("dve", fn, r, w, tiny=True)

    def act_t(self, fn, r=(), w=()):
        return self.emit("act", fn, r, w, tiny=True)

    def dma(self, q, fn, r=(), w=()):
        return self.emit(q, fn, r, w, kind="dma")

    def gx(self, fn, r=(), w=()):
        return self.emit("g", fn, r, w)

    def cc(self, fn, r=(), w=()):
        return self.emit("g", fn, r, w, kind="cc")



def _t4(wT, n_in_tiles, n_out_tiles):
    a = wT.reshape(n_in_tiles, 128, n_out_tiles, 128)
    return np.ascontiguousarray(
        a.transpose(2, 0, 1, 3).reshape(n_out_tiles * n_in_tiles * 128, 128),
        dtype=np.float32)


def _host_inputs(inp):
    x = np.asarray(inp["L0_attn_a_hidden_states"])
    A = lambda a: np.ascontiguousarray(a, dtype=np.float32)

    perm = np.concatenate([h * HD + np.concatenate([np.arange(0, HD, 2),
                                                    np.arange(1, HD, 2)])
                           for h in range(NH)])
    wqkvT = np.asarray(inp["L0_attn_b_w_qkv"], dtype=np.float32).T.copy()
    wqkvT[:, 0:D] = wqkvT[:, 0:D][:, perm]
    wqkvT[:, D:2 * D] = wqkvT[:, D:2 * D][:, perm]
    bqkv = np.asarray(inp["L0_attn_b_b_qkv"], dtype=np.float32).copy()
    bqkv[0:D] = bqkv[0:D][perm]
    bqkv[D:2 * D] = bqkv[D:2 * D][perm]

    w1T = np.asarray(inp["L0_ffn_w1"], dtype=np.float32).T.copy()
    w2T = np.asarray(inp["L0_ffn_w2"], dtype=np.float32).T.copy()
    w3T = np.asarray(inp["L0_ffn_w3"], dtype=np.float32).T.copy()

    half = HD // 2
    thetas = 10000.0 ** (-np.arange(half, dtype=np.float64) / half)
    rs = math.sqrt(SCALEB)

    tri = (np.arange(512)[:, None] <= np.arange(512)[None, :]).astype(np.float32)
    glam = (np.arange(128)[:, None] <= np.arange(128)[None, :]).astype(np.float32)

    pcol = lambda v: A(np.asarray(v).reshape(-1, 128).T)
    memgb = np.asarray(inp["L0_attn_a_mem_gate_b"], dtype=np.float32)

    shared = {
        "convw": A(np.asarray(inp["L0_attn_a_conv_w"]).reshape(NDC, 128, 4)
                   .transpose(1, 0, 2).reshape(128, NDC * 4)),
        "convb": pcol(inp["L0_attn_a_conv_b"]),
        "resw": pcol(inp["L0_attn_a_residual_weight"]),
        "inprojT": A(np.asarray(inp["L0_attn_a_in_proj_w"]).T),
        "memgT": A(np.asarray(inp["L0_attn_a_mem_gate_w"]).T),
        "memgb2": A(np.stack([memgb[:GM], memgb[GM:]], axis=1)),
        "chw0T": A(np.asarray(inp["L0_attn_a_ch_gate_w0"]).T),
        "chw1T": A(np.asarray(inp["L0_attn_a_ch_gate_w1"]).T),
        "chb1": A(np.asarray(inp["L0_attn_a_ch_gate_b1"]).reshape(1, D)),
        "wqkT": A(wqkvT[:, 0:2 * D]),
        "wvT": A(wqkvT[:, 2 * D:]),
        "bqkrow": A(bqkv[0:2 * D].reshape(1, 2 * D)),
        "bvrow": A(bqkv[2 * D:].reshape(1, D)),
        "wout4": _t4(np.asarray(inp["L0_attn_b_w_out"]).T, NDC, NDC),
        "bout": pcol(inp["L0_attn_b_b_out"]),
        "w14": _t4(w1T, NDC, NHC),
        "w24": _t4(w2T, NDC, NHC),
        "w34": _t4(w3T, NHC, NDC),
        "triband": tri, "glamask": glam,
        "ident": np.eye(128, dtype=np.float32),
    }

    in_maps = []
    for c in range(8):
        b, hlf = c // 2, c % 2
        t0 = hlf * TPC
        xs = x[b, t0:t0 + TPC, :].T
        if hlf == 0:
            haloc = np.zeros((D, HALO), np.float32)
        else:
            haloc = x[b, t0 - HALO:t0, :].T
        xT = A(np.concatenate([haloc, xs], axis=1))

        pos = np.arange(t0, t0 + TPC, dtype=np.float64)
        ang = pos[:, None] * thetas[None, :]
        cosP = np.tile((np.cos(ang) * rs).astype(np.float32), (1, 4))
        sinP = np.tile((np.sin(ang) * rs).astype(np.float32), (1, 4))

        moff = np.zeros((128, NSLOT), np.float32)
        col = 0
        for g_ in range(4):
            for kt in range(20 + 4 * g_):
                if hlf == 0 and kt < 16:
                    moff[:, col] = -30000.0
                col += 1

        smask = np.zeros((GM, 8), np.float32)
        if hlf == 1:
            smask[:, c - 1] = 1.0

        ra = c - 1 if hlf == 1 else c
        grp = np.arange(16)
        p = np.arange(128)
        idxk = (ra * KV_ROWS + grp[None, :] * 128 + p[:, None]).astype(np.int32)
        idxv = (ra * KV_ROWS + TPC + grp[None, :] * 128 + p[:, None]).astype(np.int32)

        m = dict(shared)
        m.update({
            "xT": xT, "cosP": A(cosP), "sinP": A(sinP), "moff": moff,
            "smask": smask,
            "idxk": np.ascontiguousarray(idxk),
            "idxv": np.ascontiguousarray(idxv),
            "g1": A(w1T[c * 256:(c + 1) * 256, :]),
            "g2": A(w2T[c * 256:(c + 1) * 256, :]),
            "g3": A(w3T[c * 704:(c + 1) * 704, :]),
        })
        in_maps.append(m)
    return in_maps


def _dump_x(nc, tr, g, y):
    bb = tr.buf
    for dc in range(NDC):
        tr.dma("g", lambda e, dc=dc: e.dma_start(
            out=y[dc * 128:(dc + 1) * 128, :], in_=g["xb"][dc][:]),
            r=[bb(g["xb"][dc])], w=[bb(y)])
    tr.flush()


def build():
    _LN_N[0] = 0
    _FFN_N[0] = 0
    nc = bass.Bass("TRN2", target_bir_lowering=False, debug=False)

    ein = {}

    def EIN(name, shape, dtype=F32):
        ein[name] = nc.dram_tensor(name, list(shape), dtype,
                                   kind="ExternalInput").ap()

    EIN("xT", (D, TPC + HALO))
    EIN("convw", (128, NDC * 4)); EIN("convb", (128, NDC)); EIN("resw", (128, NDC))
    EIN("inprojT", (D, 2 * GM)); EIN("memgT", (D, 2 * GM)); EIN("memgb2", (GM, 2))
    EIN("chw0T", (D, 128)); EIN("chw1T", (128, D)); EIN("chb1", (1, D))
    EIN("wqkT", (D, 2 * D)); EIN("wvT", (D, D))
    EIN("bqkrow", (1, 2 * D)); EIN("bvrow", (1, D))
    EIN("wout4", (NDC * D, 128)); EIN("bout", (128, NDC))
    EIN("w14", (NHC * D, 128)); EIN("w24", (NHC * D, 128)); EIN("w34", (NDC * FH, 128))
    EIN("triband", (512, 512)); EIN("glamask", (128, 128)); EIN("ident", (128, 128))
    EIN("cosP", (TPC, 256)); EIN("sinP", (TPC, 256))
    EIN("moff", (128, NSLOT)); EIN("smask", (GM, 8))
    EIN("idxk", (128, 16), I32); EIN("idxv", (128, 16), I32)
    EIN("g1", (256, FH)); EIN("g2", (256, FH)); EIN("g3", (704, D))

    y = nc.dram_tensor("y", [D, TPC], F32, kind="ExternalOutput").ap()

    u_dram = nc.dram_tensor("u_dram", [TPC, D], BF16).ap()
    q_dram = nc.dram_tensor("q_dram", [TPC, D], BF16).ap()
    kv_stage = nc.dram_tensor("kv_stage", [KV_ROWS, D], BF16).ap()
    kv_gath = nc.dram_tensor("kv_gath", [8 * KV_ROWS, D], BF16,
                             addr_space="Shared").ap()
    s_stage = nc.dram_tensor("s_stage", [GM + 1, D], F32).ap()
    s_gath = nc.dram_tensor("s_gath", [8 * (GM + 1), D], F32,
                            addr_space="Shared").ap()
    kbuf1 = nc.dram_tensor("kbuf1", [TPC, D], BF16).ap()
    vbuf1 = nc.dram_tensor("vbuf1", [TPC, D], BF16).ap()
    w12qc = nc.dram_tensor("w12qc", [NHC * 2 * D, 128], BF16).ap()
    w3qc = nc.dram_tensor("w3qc", [NDC * FH, 128], BF16).ap()

    with ExitStack() as ctx:
        E = ctx.enter_context
        tr = Prog(nc, ctx)

        g = {}
        g["xb"] = [E(nc.sbuf_tensor(f"x{i}", [128, TPC], BF16)) for i in range(NDC)]
        g["ln"] = [E(nc.sbuf_tensor(f"ln{i}", [128, TPC + HALO], BF16))
                   for i in range(NDC)]
        for nm, shp, dt in (
            ("ones_bf", [128, 1], BF16), ("ones_f", [128, 1], F32),
            ("onesr_bf", [1, 128], BF16), ("onesr_f", [1, 128], F32),
            ("row_m", [1, TPC + HALO], F32),
            ("row_r", [1, TPC + HALO], F32),
            ("convb_s", [128, NDC], F32), ("resw_s", [128, NDC], F32),
            ("convw_s", [128, NDC * 4], F32), ("memgb_s", [GM, 2], F32),
            ("gpart", [1, 16], F32), ("thr_s", [128, 3], F32),
            ("nthr_s", [128, 3], F32),
            ("moff_sb", [128, NSLOT], F32), ("smask_sb", [GM, 8], F32),
            ("idxk_sb", [128, 16], I32), ("idxv_sb", [128, 16], I32),
            ("bout_s", [128, NDC], F32),
        ):
            g[nm] = E(nc.sbuf_tensor(nm, shp, dt))

        _p0(nc, tr, g, ein)
        tr.flush()
        if _DBG_STOP[0] == 1:
            _dump_x(nc, tr, g, y)
            return _finish(nc, tr)
        with ExitStack() as gla:
            Eg = gla.enter_context
            for nm, shp, dt in (
                ("r_bf", [GM, TPC], BF16), ("kh_bf", [GM, TPC], BF16),
                ("d_f", [GM, TPC], F32), ("S_f", [GM, D], F32),
                ("S_bf", [GM, D], BF16), ("eD_f", [GM, NCH], F32),
                ("ident_bf", [128, 128], BF16), ("glam_f", [128, 128], F32),
                ("ones64", [GM, 128], F32),
            ):
                g[nm] = Eg(nc.sbuf_tensor(nm, shp, dt))
            tr.gx(lambda e: e.memset(g["ones64"][:], 1.0), w=[tr.buf(g["ones64"])])
            tr.dma("g", lambda e: e.dma_start(out=g["ident_bf"][:], in_=ein["ident"]),
                   w=[tr.buf(g["ident_bf"])])
            tr.dma("sp", lambda e: e.dma_start(out=g["glam_f"][:], in_=ein["glamask"]),
                   w=[tr.buf(g["glam_f"])])
            _p1(nc, tr, g, ein)
            tr.flush()
            _p2(nc, tr, g, ein, u_dram)
            tr.flush()
            _p3(nc, tr, g, ein, u_dram, s_stage, s_gath)
            tr.flush()
            if _DBG_STOP[0] not in (0, 2, 3, 4, 5):
                pass
            _p4(nc, tr, g, ein, s_gath, w12qc, w3qc)
            tr.flush()
        if _DBG_STOP[0] == 2:
            _dump_x(nc, tr, g, y)
            return _finish(nc, tr)
        _p4q(nc, tr, g, ein, w12qc, w3qc)
        tr.flush()
        if _DBG_STOP[0] == 6:
            tr.dma("g", lambda e: e.dma_start(
                out=y[0:128, :], in_=w12qc[0:D, :].rearrange("(i p) c -> p i c", p=128)),
                r=[tr.buf(w12qc)], w=[tr.buf(y)])
            tr.dma("g", lambda e: e.dma_start(
                out=y[128:256, :], in_=w3qc[0:D, :].rearrange("(i p) c -> p i c", p=128)),
                r=[tr.buf(w3qc)], w=[tr.buf(y)])
            tr.dma("g", lambda e: e.dma_start(
                out=y[256:384, 0:3], in_=g["thr_s"][0:128, 0:3]),
                r=[tr.buf(g["thr_s"])], w=[tr.buf(y)])
            tr.flush()
            return _finish(nc, tr)
        _ln_x(nc, tr, g)
        tr.flush()
        if _DBG_STOP[0] == 5:
            bb = tr.buf
            for dc in range(NDC):
                tr.dma("g", lambda e, dc=dc: e.dma_start(
                    out=y[dc * 128:(dc + 1) * 128, :], in_=g["ln"][dc][:, 0:TPC]),
                    r=[bb(g["ln"][dc])], w=[bb(y)])
            tr.flush()
            return _finish(nc, tr)
        _p_ffn(nc, tr, g, w12qc, w3qc, y=None)
        tr.flush()
        if _DBG_STOP[0] == 3:
            _dump_x(nc, tr, g, y)
            return _finish(nc, tr)
        _ln_x(nc, tr, g)
        tr.flush()
        _p6(nc, tr, g, ein, q_dram, kv_stage, kv_gath)
        tr.flush()
        _p7(nc, tr, g, ein, kv_stage, kv_gath, kbuf1, vbuf1, q_dram)
        tr.flush()
        if _DBG_STOP[0] == 4:
            _dump_x(nc, tr, g, y)
            return _finish(nc, tr)
        _ln_x(nc, tr, g)
        tr.flush()
        _p_ffn(nc, tr, g, w12qc, w3qc, y=y)
        tr.flush()

        _finish(nc, tr)
    return nc


def _finish(nc, tr):
    with nc.Block() as blk:
        sems, engs = tr.sems, tr.engs

        @blk.sync
        def _(sy):
            sy.wait_ge(sems["sp_d"], 16 * engs["sp"].n_dma)

        @blk.gpsimd
        def _(gq):
            gq.wait_ge(sems["g_d"], 16 * engs["g"].n_dma)
    return nc



def _p0(nc, tr, g, ein):
    bb = tr.buf
    xb, ln = g["xb"], g["ln"]
    with ExitStack() as px:
        E = px.enter_context
        xin = [E(nc.sbuf_tensor(f"p0xin{i}", [128, TPC + HALO], F32)) for i in range(2)]
        sqt = [E(nc.sbuf_tensor(f"p0sq{i}", [128, TPC + HALO], BF16)) for i in range(2)]
        hx = E(nc.sbuf_tensor("p0hx", [128, NDC * HALO], F32))
        hq = E(nc.sbuf_tensor("p0hq", [128, NDC * HALO], F32))
        gtile = [E(nc.sbuf_tensor(f"p0gt{i}", [128, 2048], F32)) for i in range(2)]
        gacc = E(nc.sbuf_tensor("p0gacc", [128, 32], F32))
        gcol = E(nc.sbuf_tensor("p0gcol", [128, 1], F32))
        scr = E(nc.sbuf_tensor("p0scr", [128, 512 + HALO], F32))
        g["row_t"] = E(nc.sbuf_tensor("p0rowt", [1, TPC + HALO], F32))

        seg1 = ExitStack()
        E1 = seg1.enter_context
        psx = [E1(nc.psum_tensor(f"p0psx{i}", [1, 512], F32)) for i in range(4)]
        psq = [E1(nc.psum_tensor(f"p0psq{i}", [1, 512], F32)) for i in range(4)]

        tr.gx(lambda e: e.memset(g["ones_bf"][:], 1.0), w=[bb(g["ones_bf"])])
        tr.gx(lambda e: e.memset(g["ones_f"][:], 1.0), w=[bb(g["ones_f"])])
        tr.gx(lambda e: e.memset(g["onesr_bf"][:], 1.0), w=[bb(g["onesr_bf"])])
        tr.gx(lambda e: e.memset(g["onesr_f"][:], 1.0), w=[bb(g["onesr_f"])])
        for nm, src in (("convb_s", "convb"), ("resw_s", "resw"),
                        ("convw_s", "convw"), ("memgb_s", "memgb2"),
                        ("moff_sb", "moff"), ("smask_sb", "smask"),
                        ("idxk_sb", "idxk"), ("idxv_sb", "idxv"),
                        ("bout_s", "bout")):
            tr.dma("sp", lambda e, nm=nm, src=src: e.dma_start(
                out=g[nm][:], in_=ein[src]), w=[bb(g[nm])])

        for dc in range(NDC):
            xt = xin[dc % 2]
            sq = sqt[dc % 2]
            tr.dma("sp", lambda e, dc=dc, xt=xt: e.dma_start(
                out=xt[:], in_=ein["xT"][dc * 128:(dc + 1) * 128, :]),
                w=[bb(xt)])
            tr.dve(lambda e, dc=dc, xt=xt: e.tensor_copy(ln[dc][:], xt[:]),
                   r=[bb(xt)], w=[bb(ln[dc])])
            tr.dve(lambda e, dc=dc: e.tensor_copy(
                xb[dc][:], ln[dc][:, HALO:]), r=[bb(ln[dc])], w=[bb(xb[dc])])
            tr.act(lambda e, xt=xt, sq=sq: e.activation(
                sq[:], xt[:], AF.Square), r=[bb(xt)], w=[bb(sq)])
            for tg in range(4):
                c0 = HALO + tg * 512
                tr.pe(lambda e, dc=dc, tg=tg, c0=c0: e.matmul(
                    psx[tg][:], g["ones_bf"][:], ln[dc][:, c0:c0 + 512],
                    start=(dc == 0), stop=(dc == NDC - 1)),
                    r=[bb(ln[dc]), bb(g["ones_bf"])], w=[bb(psx[tg])])
                tr.pe(lambda e, dc=dc, tg=tg, c0=c0, sq=sq: e.matmul(
                    psq[tg][:], g["ones_bf"][:], sq[:, c0:c0 + 512],
                    start=(dc == 0), stop=(dc == NDC - 1)),
                    r=[bb(sq), bb(g["ones_bf"])], w=[bb(psq[tg])])
        for tg in range(4):
            tr.dve(lambda e, tg=tg: e.tensor_copy(
                g["row_m"][:, HALO + tg * 512:HALO + (tg + 1) * 512],
                psx[tg][:]), r=[bb(psx[tg])], w=[bb(g["row_m"])])
            tr.dve(lambda e, tg=tg: e.tensor_copy(
                g["row_r"][:, HALO + tg * 512:HALO + (tg + 1) * 512],
                psq[tg][:]), r=[bb(psq[tg])], w=[bb(g["row_r"])])
        seg1.close()
        tr.flush()

        psh = [E(nc.psum_tensor(f"p0psh{i}", [1, HALO], F32)) for i in range(2)]
        psb = [E(nc.psum_tensor(f"p0psb{i}", [128, 512], F32)) for i in range(2)]
        psg = E(nc.psum_tensor("p0psg", [1, 1], F32))

        for dc in range(NDC):
            tr.dma("sp", lambda e, dc=dc: e.dma_start(
                out=hx[:, dc * HALO:(dc + 1) * HALO],
                in_=ein["xT"][dc * 128:(dc + 1) * 128, 0:HALO]), w=[bb(hx)])
        tr.act(lambda e: e.activation(hq[:], hx[:], AF.Square),
               r=[bb(hx)], w=[bb(hq)])
        for dc in range(NDC):
            tr.pe(lambda e, dc=dc: e.matmul(
                psh[0][:], g["ones_f"][:], hx[:, dc * HALO:(dc + 1) * HALO],
                start=(dc == 0), stop=(dc == NDC - 1)),
                r=[bb(hx), bb(g["ones_f"])], w=[bb(psh[0])])
            tr.pe(lambda e, dc=dc: e.matmul(
                psh[1][:], g["ones_f"][:], hq[:, dc * HALO:(dc + 1) * HALO],
                start=(dc == 0), stop=(dc == NDC - 1)),
                r=[bb(hq), bb(g["ones_f"])], w=[bb(psh[1])])
        tr.act(lambda e: e.copy(g["row_m"][:, 0:HALO], psh[0][:]),
               r=[bb(psh[0])], w=[bb(g["row_m"])])
        tr.act(lambda e: e.copy(g["row_r"][:, 0:HALO], psh[1][:]),
               r=[bb(psh[1])], w=[bb(g["row_r"])])

        _stats_pipeline(nc, tr, g, n=TPC + HALO)

        groups = [(0, HALO)] + [(HALO + t * 512, 512) for t in range(4)]
        for (c0, n) in groups:
            tr.pe(lambda e, c0=c0, n=n: e.matmul(
                psb[0][:, 0:n], g["onesr_f"][:], g["row_m"][0:1, c0:c0 + n],
                start=True, stop=True),
                r=[bb(g["row_m"]), bb(g["onesr_f"])], w=[bb(psb[0])])
            tr.pe(lambda e, c0=c0, n=n: e.matmul(
                psb[1][:, 0:n], g["onesr_f"][:], g["row_r"][0:1, c0:c0 + n],
                start=True, stop=True),
                r=[bb(g["row_r"]), bb(g["onesr_f"])], w=[bb(psb[1])])
            for dc in range(NDC):
                tr.dve(lambda e, dc=dc, c0=c0, n=n: e.tensor_tensor(
                    scr[:, 0:n], ln[dc][:, c0:c0 + n], psb[0][:, 0:n],
                    ALU.subtract), r=[bb(ln[dc]), bb(psb[0])], w=[bb(scr)])
                if n < 128:
                    tr.act(lambda e, n=n: e.copy(scr[:, 512:512 + n], scr[:, 0:n]),
                           r=[bb(scr)], w=[bb(scr)])
                    tr.dve(lambda e, dc=dc, c0=c0, n=n: e.tensor_tensor(
                        ln[dc][:, c0:c0 + n], scr[:, 512:512 + n], psb[1][:, 0:n],
                        ALU.mult), r=[bb(scr), bb(psb[1])], w=[bb(ln[dc])])
                else:
                    tr.dve(lambda e, dc=dc, c0=c0, n=n: e.tensor_tensor(
                        ln[dc][:, c0:c0 + n], scr[:, 0:n], psb[1][:, 0:n],
                        ALU.mult), r=[bb(scr), bb(psb[1])], w=[bb(ln[dc])])

        tr.gx(lambda e: e.memset(gacc[:], 0.0), w=[bb(gacc)])
        col = 0
        spans = []
        for nm, rows, cols in (("g1", 256, FH), ("g2", 256, FH), ("g3", 704, D)):
            start_col = col
            r0 = 0
            while r0 < rows:
                pr = min(128, rows - r0)
                c0 = 0
                while c0 < cols:
                    pc = min(2048, cols - c0)
                    gt = gtile[col % 2]
                    tr.dma("sp", lambda e, nm=nm, r0=r0, pr=pr, c0=c0, pc=pc, gt=gt:
                           e.dma_start(out=gt[0:pr, 0:pc],
                                       in_=ein[nm][r0:r0 + pr, c0:c0 + pc]),
                           w=[bb(gt)])
                    tr.dve(lambda e, gt=gt, pr=pr, pc=pc, col=col: e.tensor_reduce(
                        gacc[0:pr, col:col + 1], gt[0:pr, 0:pc],
                        mybir.AxisListType.X, ALU.add, apply_absolute_value=True),
                        r=[bb(gt)], w=[bb(gacc)])
                    col += 1
                    c0 += pc
                r0 += pr
            spans.append((start_col, col))
        for j, (a, bcol) in enumerate(spans):
            tr.dve(lambda e, a=a, bcol=bcol: e.tensor_reduce(
                gcol[:, 0:1], gacc[:, a:bcol], mybir.AxisListType.X, ALU.add),
                r=[bb(gacc)], w=[bb(gcol)])
            tr.pe(lambda e: e.matmul(psg[:], g["ones_f"][:], gcol[:, 0:1],
                                     start=True, stop=True),
                  r=[bb(gcol), bb(g["ones_f"])], w=[bb(psg)])
            tr.dve(lambda e, j=j: e.tensor_copy(g["gpart"][:, j:j + 1], psg[:]),
                   r=[bb(psg)], w=[bb(g["gpart"])])


def _stats_pipeline(nc, tr, g, n):
    bb = tr.buf
    m, r, t = g["row_m"], g["row_r"], g["row_t"]
    tr.dve(lambda e: e.tensor_scalar_mul(m[:, 0:n], m[:, 0:n], 1.0 / D),
           r=[], w=[bb(m)])
    tr.dve(lambda e: e.tensor_tensor(t[:, 0:n], m[:, 0:n], m[:, 0:n], ALU.mult),
           r=[bb(m)], w=[bb(t)])
    tr.dve(lambda e: e.scalar_tensor_tensor(
        r[:, 0:n], r[:, 0:n], 1.0 / D, t[:, 0:n], ALU.mult, ALU.subtract),
        r=[bb(t)], w=[bb(r)])
    tr.dve(lambda e: e.tensor_scalar_add(r[:, 0:n], r[:, 0:n], LN_EPS),
           r=[], w=[bb(r)])
    tr.act(lambda e: e.activation(t[:, 0:n], r[:, 0:n], AF.Ln),
           r=[bb(r)], w=[bb(t)])
    tr.dve(lambda e: e.tensor_scalar_mul(t[:, 0:n], t[:, 0:n], -0.5),
           r=[], w=[bb(t)])
    tr.act(lambda e: e.activation(r[:, 0:n], t[:, 0:n], AF.Exp),
           r=[bb(t)], w=[bb(r)])


def _p1(nc, tr, g, ein):
    bb = tr.buf
    xb, ln = g["xb"], g["ln"]
    with ExitStack() as px:
        E = px.enter_context
        inpj = E(nc.sbuf_tensor("p1inpj", [128, NDC * 128], BF16))
        memg = E(nc.sbuf_tensor("p1memg", [128, NDC * 128], BF16))
        shf = [E(nc.sbuf_tensor(f"p1s{i}", [128, 512], BF16)) for i in range(2)]
        acc = E(nc.sbuf_tensor("p1acc", [128, 512], F32))
        g_f = E(nc.sbuf_tensor("p1gf", [GM, 512], F32))
        sel = E(nc.sbuf_tensor("p1sel", [GM, 512], F32))
        tau = E(nc.sbuf_tensor("p1tau", [GM, 512], F32))
        aa = E(nc.sbuf_tensor("p1aa", [GM, 512], F32))
        mm_ = E(nc.sbuf_tensor("p1mm", [GM, 512], F32))
        kpre = E(nc.sbuf_tensor("p1kpre", [GM, 512], F32))
        k2 = E(nc.sbuf_tensor("p1k2", [GM, 512], F32))
        lns = aa
        itg = mm_
        kq = k2
        nr = E(nc.sbuf_tensor("p1nr", [1, 512], F32))
        pr_ = E(nc.psum_tensor("p1pr", [GM, 512], F32))
        pk = E(nc.psum_tensor("p1pk", [GM, 512], F32))
        pse = E(nc.psum_tensor("p1pse", [GM, 512], F32))
        pta = E(nc.psum_tensor("p1pta", [GM, 512], F32))
        pn2 = E(nc.psum_tensor("p1pn2", [1, 512], F32))
        pnb = E(nc.psum_tensor("p1pnb", [GM, 512], F32))

        tr.dma("g", lambda e: e.dma_start(
            out=inpj[:], in_=ein["inprojT"].rearrange("(i p) c -> p i c", p=128)),
            w=[bb(inpj)])
        tr.dma("g", lambda e: e.dma_start(
            out=memg[:], in_=ein["memgT"].rearrange("(i p) c -> p i c", p=128)),
            w=[bb(memg)])

        for tg in range(NTG):
            t0 = tg * 512
            for dc in range(NDC):
                sh = shf[dc % 2]
                tr.dve(lambda e, dc=dc, t0=t0: e.tensor_scalar_mul(
                    acc[:], ln[dc][:, t0:t0 + 512],
                    g["convw_s"][:, dc * 4:dc * 4 + 1]),
                    r=[bb(ln[dc]), bb(g["convw_s"])], w=[bb(acc)])
                for i in range(1, 4):
                    tr.dve(lambda e, dc=dc, t0=t0, i=i: e.scalar_tensor_tensor(
                        acc[:], ln[dc][:, t0 + i:t0 + i + 512],
                        g["convw_s"][:, dc * 4 + i:dc * 4 + i + 1], acc[:],
                        ALU.mult, ALU.add),
                        r=[bb(ln[dc]), bb(g["convw_s"])], w=[bb(acc)])
                tr.act(lambda e, dc=dc, sh=sh: e.activation(
                    sh[:], acc[:], AF.Silu, bias=g["convb_s"][:, dc:dc + 1]),
                    r=[bb(acc), bb(g["convb_s"])], w=[bb(sh)])
                tr.dve(lambda e, dc=dc, t0=t0, sh=sh: e.scalar_tensor_tensor(
                    xb[dc][:, t0:t0 + 512], sh[:],
                    g["resw_s"][:, dc:dc + 1], xb[dc][:, t0:t0 + 512],
                    ALU.mult, ALU.add),
                    r=[bb(sh), bb(g["resw_s"])], w=[bb(xb[dc])])
                st, sp_ = (dc == 0), (dc == NDC - 1)
                tr.pe(lambda e, dc=dc, sh=sh, st=st, sp_=sp_: e.matmul(
                    pr_[:], inpj[:, dc * 128:dc * 128 + GM], sh[:],
                    start=st, stop=sp_), r=[bb(inpj), bb(sh)], w=[bb(pr_)])
                tr.pe(lambda e, dc=dc, sh=sh, st=st, sp_=sp_: e.matmul(
                    pk[:], inpj[:, dc * 128 + GM:dc * 128 + 128], sh[:],
                    start=st, stop=sp_), r=[bb(inpj), bb(sh)], w=[bb(pk)])
                tr.pe(lambda e, dc=dc, sh=sh, st=st, sp_=sp_: e.matmul(
                    pse[:], memg[:, dc * 128:dc * 128 + GM], sh[:],
                    start=st, stop=sp_), r=[bb(memg), bb(sh)], w=[bb(pse)])
                tr.pe(lambda e, dc=dc, sh=sh, st=st, sp_=sp_: e.matmul(
                    pta[:], memg[:, dc * 128 + GM:dc * 128 + 128], sh[:],
                    start=st, stop=sp_), r=[bb(memg), bb(sh)], w=[bb(pta)])
            tr.dve(lambda e, t0=t0: e.tensor_copy(
                g["r_bf"][:, t0:t0 + 512], pr_[:]),
                r=[bb(pr_)], w=[bb(g["r_bf"])])
            tr.dve(lambda e: e.tensor_copy(kpre[:], pk[:]),
                   r=[bb(pk)], w=[bb(kpre)])
            tr.act(lambda e: e.activation(tau[:], pta[:], AF.Sigmoid,
                                          bias=g["memgb_s"][:, 1:2]),
                   r=[bb(pta), bb(g["memgb_s"])], w=[bb(tau)])
            tr.dve(lambda e: e.tensor_scalar_add(mm_[:], pse[:],
                                                 g["memgb_s"][:, 0:1]),
                   r=[bb(pse), bb(g["memgb_s"])], w=[bb(mm_)])
            tr.act(lambda e: e.activation(aa[:], mm_[:], AF.Abs),
                   r=[bb(mm_)], w=[bb(aa)])
            tr.act(lambda e: e.activation(aa[:], aa[:], AF.Exp, scale=-1.0),
                   r=[], w=[bb(aa)])
            tr.act(lambda e: e.activation(aa[:], aa[:], AF.Ln, bias=1.0),
                   r=[], w=[bb(aa)])
            tr.dve(lambda e: e.tensor_scalar_max(mm_[:], mm_[:], 0.0),
                   r=[], w=[bb(mm_)])
            tr.dve(lambda e: e.tensor_add(sel[:], mm_[:], aa[:]),
                   r=[bb(mm_), bb(aa)], w=[bb(sel)])
            tr.dve(lambda e: e.tensor_scalar_max(sel[:], sel[:], 1e-38),
                   r=[], w=[bb(sel)])
            tr.act(lambda e: e.activation(lns[:], sel[:], AF.Ln),
                   r=[bb(sel)], w=[bb(lns)])
            tr.dve(lambda e: e.tensor_mul(lns[:], tau[:], lns[:]),
                   r=[bb(tau)], w=[bb(lns)])
            tr.act(lambda e: e.activation(itg[:], lns[:], AF.Exp),
                   r=[bb(lns)], w=[bb(itg)])
            tr.dve(lambda e: e.scalar_tensor_tensor(
                g_f[:], sel[:], -1.0, tau[:],
                ALU.mult, ALU.mult),
                r=[bb(sel), bb(tau)], w=[bb(g_f)])
            for cc_ in range(4):
                c = tg * 4 + cc_
                tr.dve(lambda e, c=c, cc_=cc_: e.tensor_tensor_scan(
                    g["d_f"][:, c * CH:(c + 1) * CH], g["ones64"][:, 0:CH],
                    g_f[:, cc_ * CH:(cc_ + 1) * CH], 0.0, ALU.mult, ALU.add),
                    r=[bb(g["ones64"]), bb(g_f)], w=[bb(g["d_f"])])
            tr.act(lambda e: e.activation(k2[:], kpre[:], AF.Square),
                   r=[bb(kpre)], w=[bb(k2)])
            tr.pe(lambda e: e.matmul(pn2[:], g["ones_f"][0:GM, 0:1], k2[:],
                                     start=True, stop=True),
                  r=[bb(k2), bb(g["ones_f"])], w=[bb(pn2)])
            tr.act(lambda e: e.activation(nr[:], pn2[:], AF.Sqrt),
                   r=[bb(pn2)], w=[bb(nr)])
            tr.dve(lambda e: e.tensor_scalar_max(nr[:], nr[:], 1e-12),
                   r=[], w=[bb(nr)])
            tr.dve(lambda e: e.reciprocal(nr[:], nr[:]), r=[], w=[bb(nr)])
            tr.pe(lambda e: e.matmul(pnb[:], g["onesr_f"][0:1, 0:GM], nr[:],
                                     start=True, stop=True),
                  r=[bb(nr), bb(g["onesr_f"])], w=[bb(pnb)])
            tr.dve(lambda e: e.tensor_tensor(kq[:], kpre[:], pnb[:], ALU.mult),
                   r=[bb(kpre), bb(pnb)], w=[bb(kq)])
            tr.dve(lambda e, t0=t0: e.tensor_tensor(
                g["kh_bf"][:, t0:t0 + 512], kq[:], itg[:], ALU.mult),
                r=[bb(kq), bb(itg)], w=[bb(g["kh_bf"])])



def _p2(nc, tr, g, ein, u_dram):
    bb = tr.buf
    ln = g["ln"]
    with ExitStack() as px:
        E = px.enter_context
        chw0_sb = E(nc.sbuf_tensor("p2w0", [128, NDC * 128], BF16))
        chw1_sb = E(nc.sbuf_tensor("p2w1", [128, D], BF16))
        chb1_sb = E(nc.sbuf_tensor("p2b1", [1, D], BF16))
        h0 = E(nc.sbuf_tensor("p2h0", [128, TPC], BF16))
        lntm = E(nc.sbuf_tensor("p2lntm", [128, D], BF16))
        chs = E(nc.sbuf_tensor("p2chs", [128, 512], BF16))
        ut = E(nc.sbuf_tensor("p2ut", [128, D], BF16))
        ph = [E(nc.psum_tensor(f"p2ph{i}", [128, 512], F32)) for i in range(2)]
        ptr = [E(nc.psum_tensor(f"p2ptr{i}", [128, 128], BF16)) for i in range(2)]
        pch = [E(nc.psum_tensor(f"p2pch{i}", [128, 512], F32)) for i in range(2)]

        tr.dma("g", lambda e: e.dma_start(
            out=chw0_sb[:], in_=ein["chw0T"].rearrange("(i p) c -> p i c", p=128)),
            w=[bb(chw0_sb)])
        tr.dma("g", lambda e: e.dma_start(out=chw1_sb[:], in_=ein["chw1T"]),
               w=[bb(chw1_sb)])
        tr.dma("g", lambda e: e.dma_start(out=chb1_sb[:], in_=ein["chb1"]),
               w=[bb(chb1_sb)])

        for tg in range(NTG):
            p = ph[tg % 2]
            for dc in range(NDC):
                tr.pe(lambda e, dc=dc, tg=tg, p=p: e.matmul(
                    p[:], chw0_sb[:, dc * 128:(dc + 1) * 128],
                    ln[dc][:, HALO + tg * 512:HALO + (tg + 1) * 512],
                    start=(dc == 0), stop=(dc == NDC - 1)),
                    r=[bb(chw0_sb), bb(ln[dc])], w=[bb(p)])
            tr.dve(lambda e, tg=tg, p=p: e.tensor_copy(
                h0[:, tg * 512:(tg + 1) * 512], p[:]), r=[bb(p)], w=[bb(h0)])

        for tc in range(NCH):
            c0 = HALO + tc * 128
            for dc in range(NDC):
                pt = ptr[dc % 2]
                tr.pe(lambda e, dc=dc, c0=c0, pt=pt: e.transpose(
                    pt[:], ln[dc][:, c0:c0 + 128], g["ident_bf"][:]),
                    r=[bb(ln[dc]), bb(g["ident_bf"])], w=[bb(pt)])
                tr.dve(lambda e, dc=dc, pt=pt: e.tensor_copy(
                    lntm[:, dc * 128:(dc + 1) * 128], pt[:]),
                    r=[bb(pt)], w=[bb(lntm)])
            for dg in range(4):
                p = pch[dg % 2]
                tr.pe(lambda e, tc=tc, dg=dg, p=p: e.matmul(
                    p[:], h0[:, tc * 128:(tc + 1) * 128],
                    chw1_sb[:, dg * 512:(dg + 1) * 512],
                    start=True, stop=False),
                    r=[bb(h0), bb(chw1_sb)], w=[bb(p)])
                tr.pe(lambda e, dg=dg, p=p: e.matmul(
                    p[:], g["onesr_bf"][:], chb1_sb[0:1, dg * 512:(dg + 1) * 512],
                    start=False, stop=True),
                    r=[bb(g["onesr_bf"]), bb(chb1_sb)], w=[bb(p)])
                tr.act(lambda e, p=p: e.activation(chs[:], p[:], AF.Sigmoid),
                       r=[bb(p)], w=[bb(chs)])
                tr.dve(lambda e, dg=dg: e.tensor_tensor(
                    ut[:, dg * 512:(dg + 1) * 512], chs[:],
                    lntm[:, dg * 512:(dg + 1) * 512], ALU.mult),
                    r=[bb(chs), bb(lntm)], w=[bb(ut)])
            tr.dma("sp", lambda e, tc=tc: e.dma_start(
                out=u_dram[tc * 128:(tc + 1) * 128, :], in_=ut[:]),
                r=[bb(ut)], w=[bb(u_dram)])



def _p3(nc, tr, g, ein, u_dram, s_stage, s_gath):
    bb = tr.buf
    xb = g["xb"]
    with ExitStack() as px:
        E = px.enter_context
        Uc = [E(nc.sbuf_tensor(f"p3u{i}", [128, D], BF16)) for i in range(2)]
        ndB = [E(nc.sbuf_tensor(f"p3nd{i}", [GM, CH], F32)) for i in range(2)]
        e0B = [E(nc.sbuf_tensor(f"p3e0{i}", [GM, CH], BF16)) for i in range(2)]
        q0B = [E(nc.sbuf_tensor(f"p3q0{i}", [GM, CH], BF16)) for i in range(2)]
        eqB = [E(nc.sbuf_tensor(f"p3eq{i}", [GM, 32], BF16)) for i in range(2)]
        ekB = [E(nc.sbuf_tensor(f"p3ek{i}", [GM, CH], BF16)) for i in range(2)]
        qjB = [E(nc.sbuf_tensor(f"p3qj{i}", [GM, 32], BF16)) for i in range(2)]
        kjB = [E(nc.sbuf_tensor(f"p3kj{i}", [GM, CH], BF16)) for i in range(2)]
        esB = [E(nc.sbuf_tensor(f"p3es{i}", [GM, CH], BF16)) for i in range(2)]
        kSB = [E(nc.sbuf_tensor(f"p3kS{i}", [GM, CH], BF16)) for i in range(2)]
        kTB = [E(nc.sbuf_tensor(f"p3kT{i}", [128, GM], BF16)) for i in range(2)]
        PtB = [E(nc.sbuf_tensor(f"p3Pt{i}", [128, CH], BF16)) for i in range(2)]
        pPB = [E(nc.psum_tensor(f"p3pP{i}", [128, CH], F32)) for i in range(2)]
        pT = E(nc.psum_tensor("p3pT", [128, GM], BF16))
        pI = E(nc.psum_tensor("p3pI", [GM, 512], F32))
        po = [E(nc.psum_tensor(f"p3po{i}", [128, CH], F32)) for i in range(4)]

        tr.gx(lambda e: e.memset(g["S_f"][:], 0.0), w=[bb(g["S_f"])])
        dbuf = bb(g["d_f"])

        for c in range(NCH):
            u = Uc[c % 2]
            nd, e0, q0 = ndB[c % 2], e0B[c % 2], q0B[c % 2]
            eq, ek, qj, kj = eqB[c % 2], ekB[c % 2], qjB[c % 2], kjB[c % 2]
            es, kS, kT, Pt = esB[c % 2], kSB[c % 2], kTB[c % 2], PtB[c % 2]
            pP = pPB[c % 2]
            tr.dma("sp" if c % 2 == 0 else "g", lambda e, c=c, u=u: e.dma_start(
                out=u[:], in_=u_dram[c * 128:(c + 1) * 128, :]),
                r=[bb(u_dram)], w=[bb(u)])
            dsl = g["d_f"][:, c * CH:(c + 1) * CH]
            tr.dve(lambda e, dsl=dsl, nd=nd: e.tensor_scalar_mul(nd[:], dsl, -1.0),
                   r=[dbuf], w=[bb(nd)])
            tr.act(lambda e, c=c, dsl=dsl: e.activation(
                g["eD_f"][:, c:c + 1], dsl[:, CH - 1:CH], AF.Exp),
                r=[dbuf], w=[bb(g["eD_f"])])
            tr.dve(lambda e: e.tensor_copy(g["S_bf"][:], g["S_f"][:]),
                   r=[bb(g["S_f"])], w=[bb(g["S_bf"])])
            tr.act(lambda e, dsl=dsl, e0=e0: e.activation(e0[:], dsl, AF.Exp),
                   r=[dbuf], w=[bb(e0)])
            tr.dve(lambda e, c=c, q0=q0, e0=e0: e.tensor_tensor(
                q0[:], g["r_bf"][:, c * CH:(c + 1) * CH], e0[:], ALU.mult),
                r=[bb(g["r_bf"]), bb(e0)], w=[bb(q0)])
            tr.dve(lambda e, pP=pP: e.memset(pP[:], 0.0), w=[bb(pP)])
            for J in range(4):
                w_ = 32 * J + 32
                mid = 32 * J + 16
                tr.act(lambda e, dsl=dsl, J=J, mid=mid, w_=w_, eq=eq, nd=nd: e.activation(
                    eq[:], dsl[:, 32 * J:w_], AF.Exp,
                    bias=nd[:, mid:mid + 1]),
                    r=[dbuf, bb(nd)], w=[bb(eq)])
                tr.dve(lambda e, c=c, J=J, w_=w_, qj=qj, eq=eq: e.tensor_tensor(
                    qj[:], g["r_bf"][:, c * CH + 32 * J:c * CH + w_],
                    eq[:], ALU.mult), r=[bb(g["r_bf"]), bb(eq)], w=[bb(qj)])
                tr.act(lambda e, dsl=dsl, w_=w_, mid=mid, ek=ek, nd=nd: e.activation(
                    ek[:, 0:w_], nd[:, 0:w_], AF.Exp,
                    bias=dsl[:, mid:mid + 1]),
                    r=[dbuf, bb(nd)], w=[bb(ek)])
                tr.dve(lambda e, c=c, w_=w_, kj=kj, ek=ek: e.tensor_tensor(
                    kj[:, 0:w_], g["kh_bf"][:, c * CH:c * CH + w_],
                    ek[:, 0:w_], ALU.mult),
                    r=[bb(g["kh_bf"]), bb(ek)], w=[bb(kj)])
                tr.pe(lambda e, J=J, w_=w_, pP=pP, kj=kj, qj=qj: e.matmul(
                    pP[0:w_, 32 * J:w_], kj[0:GM, 0:w_], qj[:],
                    start=True, stop=True),
                    r=[bb(kj), bb(qj)], w=[bb(pP)])
            tr.dve(lambda e, Pt=Pt, pP=pP: e.tensor_tensor(Pt[:], pP[:], g["glam_f"][:],
                                             ALU.mult),
                   r=[bb(pP), bb(g["glam_f"])], w=[bb(Pt)])
            for w0 in range(0, NDC, 4):
                dcs = range(w0, min(w0 + 4, NDC))
                for i, dc in enumerate(dcs):
                    tr.pe(lambda e, dc=dc, i=i, u=u, Pt=Pt: e.matmul(
                        po[i][:], u[:, dc * 128:(dc + 1) * 128], Pt[:],
                        start=True, stop=False),
                        r=[bb(u), bb(Pt)], w=[bb(po[i])])
                    tr.pe(lambda e, dc=dc, i=i, q0=q0: e.matmul(
                        po[i][:], g["S_bf"][:, dc * 128:(dc + 1) * 128], q0[:],
                        start=False, stop=True),
                        r=[bb(g["S_bf"]), bb(q0)], w=[bb(po[i])])
                for i, dc in enumerate(dcs):
                    tr.dve(lambda e, dc=dc, i=i, c=c: e.scalar_tensor_tensor(
                        xb[dc][:, c * CH:(c + 1) * CH], po[i][:], SCALE_GLA,
                        xb[dc][:, c * CH:(c + 1) * CH], ALU.mult, ALU.add),
                        r=[bb(po[i])], w=[bb(xb[dc])])
            tr.act(lambda e, dsl=dsl, es=es, nd=nd: e.activation(
                es[:], nd[:], AF.Exp, bias=dsl[:, CH - 1:CH]),
                r=[dbuf, bb(nd)], w=[bb(es)])
            tr.dve(lambda e, c=c, kS=kS, es=es: e.tensor_tensor(
                kS[:], g["kh_bf"][:, c * CH:(c + 1) * CH], es[:], ALU.mult),
                r=[bb(g["kh_bf"]), bb(es)], w=[bb(kS)])
            tr.pe(lambda e, kS=kS: e.transpose(pT[:], kS[:], g["ident_bf"][0:GM, 0:GM]),
                  r=[bb(kS), bb(g["ident_bf"])], w=[bb(pT)])
            tr.dve(lambda e, kT=kT: e.tensor_copy(kT[:], pT[:]), r=[bb(pT)], w=[bb(kT)])
            for dg in range(4):
                tr.pe(lambda e, dg=dg, u=u, kT=kT: e.matmul(
                    pI[:], kT[:, 0:GM], u[:, dg * 512:(dg + 1) * 512],
                    start=True, stop=True),
                    r=[bb(kT), bb(u)], w=[bb(pI)])
                tr.dve(lambda e, c=c, dg=dg: e.scalar_tensor_tensor(
                    g["S_f"][:, dg * 512:(dg + 1) * 512],
                    g["S_f"][:, dg * 512:(dg + 1) * 512],
                    g["eD_f"][:, c:c + 1], pI[:], ALU.mult, ALU.add),
                    r=[bb(g["eD_f"]), bb(pI)], w=[bb(g["S_f"])])

        tr.dma("sp", lambda e: e.dma_start(out=s_stage[0:GM, :], in_=g["S_f"][:]),
               r=[bb(g["S_f"])], w=[bb(s_stage)])
        tr.dma("sp", lambda e: e.dma_start(
            out=s_stage[GM:GM + 1, 0:16], in_=g["gpart"][:]),
            r=[bb(g["gpart"])], w=[bb(s_stage)])
        tr.cc(lambda e: e.collective_compute(
            "AllGather", ALU.bypass,
            replica_groups=[[0, 1, 2, 3, 4, 5, 6, 7]],
            ins=[s_stage.opt()], outs=[s_gath.opt()]),
            r=[bb(s_stage)], w=[bb(s_gath)])


def _p4(nc, tr, g, ein, s_gath, w12qc, w3qc):
    bb = tr.buf
    xb = g["xb"]
    with ExitStack() as px:
        E = px.enter_context
        sg_t = E(nc.sbuf_tensor("p4sg", [GM, D], F32))
        e0 = E(nc.sbuf_tensor("p4e0", [GM, CH], BF16))
        q0 = E(nc.sbuf_tensor("p4q0", [GM, CH], BF16))
        q0a = E(nc.sbuf_tensor("p4q0a", [GM, CH], BF16))
        Acum = E(nc.sbuf_tensor("p4A", [GM, NCH + 1], F32))
        Acs = E(nc.sbuf_tensor("p4Ac", [GM, NCH + 1], F32))
        grows = E(nc.sbuf_tensor("p4grows", [1, 24], F32))
        gsum = E(nc.sbuf_tensor("p4gsum", [1, 12], F32))
        g["_dbg_grows"] = grows
        g["_dbg_gsum"] = gsum
        g["_dbg_sgath"] = s_gath
        po = [E(nc.psum_tensor(f"p4po{i}", [128, CH], F32)) for i in range(6)]
        pb = E(nc.psum_tensor("p4pb", [128, 1], F32))

        tr.gx(lambda e: e.memset(g["S_f"][:], 0.0), w=[bb(g["S_f"])])
        for r in range(8):
            tr.dma("sp", lambda e, r=r: e.dma_start(
                out=sg_t[:], in_=s_gath[r * (GM + 1):r * (GM + 1) + GM, :]),
                r=[bb(s_gath)], w=[bb(sg_t)])
            tr.dve(lambda e, r=r: e.scalar_tensor_tensor(
                g["S_f"][:], sg_t[:], g["smask_sb"][:, r:r + 1], g["S_f"][:],
                ALU.mult, ALU.add),
                r=[bb(sg_t), bb(g["smask_sb"])], w=[bb(g["S_f"])])
        tr.dve(lambda e: e.tensor_copy(g["S_bf"][:], g["S_f"][:]),
               r=[bb(g["S_f"])], w=[bb(g["S_bf"])])
        tr.gx(lambda e: e.memset(Acum[:, 0:NCH + 1], 1.0), w=[bb(Acum)])
        tr.gx(lambda e: e.memset(Acs[:, 0:NCH + 1], 1.0), w=[bb(Acs)])
        for c in range(NCH):
            tr.dve(lambda e, c=c: e.tensor_tensor(
                Acum[:, c + 1:c + 2], Acs[:, c:c + 1], g["eD_f"][:, c:c + 1],
                ALU.mult), r=[bb(g["eD_f"]), bb(Acs)], w=[bb(Acum)])
            tr.act(lambda e, c=c: e.copy(Acs[:, c + 1:c + 2],
                                         Acum[:, c + 1:c + 2]),
                   r=[bb(Acum)], w=[bb(Acs)])

        dbuf = bb(g["d_f"])
        for c in range(NCH):
            dsl = g["d_f"][:, c * CH:(c + 1) * CH]
            tr.act(lambda e, dsl=dsl: e.activation(e0[:], dsl, AF.Exp),
                   r=[dbuf], w=[bb(e0)])
            tr.dve(lambda e, c=c: e.tensor_tensor(
                q0[:], g["r_bf"][:, c * CH:(c + 1) * CH], e0[:], ALU.mult),
                r=[bb(g["r_bf"]), bb(e0)], w=[bb(q0)])
            tr.dve(lambda e, c=c: e.tensor_scalar_mul(
                q0a[:], q0[:], Acs[:, c:c + 1]),
                r=[bb(q0), bb(Acs)], w=[bb(q0a)])
            for w0 in range(0, NDC, 6):
                dcs = range(w0, min(w0 + 6, NDC))
                for i, dc in enumerate(dcs):
                    tr.pe(lambda e, dc=dc, i=i: e.matmul(
                        po[i][:], g["S_bf"][:, dc * 128:(dc + 1) * 128], q0a[:],
                        start=True, stop=True),
                        r=[bb(g["S_bf"]), bb(q0a)], w=[bb(po[i])])
                for i, dc in enumerate(dcs):
                    tr.dve(lambda e, dc=dc, i=i, c=c: e.scalar_tensor_tensor(
                        xb[dc][:, c * CH:(c + 1) * CH], po[i][:], SCALE_GLA,
                        xb[dc][:, c * CH:(c + 1) * CH], ALU.mult, ALU.add),
                        r=[bb(po[i])], w=[bb(xb[dc])])

        for r in range(8):
            tr.dma("sp", lambda e, r=r: e.dma_start(
                out=grows[0:1, r * 3:(r + 1) * 3],
                in_=s_gath[r * (GM + 1) + GM:r * (GM + 1) + GM + 1, 0:3]),
                r=[bb(s_gath)], w=[bb(grows)])
        RW = dict(r=[bb(grows), bb(gsum)], w=[bb(grows), bb(gsum)])
        for p_ in range(4):
            tr.dve(lambda e, p_=p_: e.tensor_tensor(
                grows[0:1, p_ * 3:p_ * 3 + 3], grows[0:1, 2 * p_ * 3:2 * p_ * 3 + 3],
                grows[0:1, (2 * p_ + 1) * 3:(2 * p_ + 1) * 3 + 3], ALU.add),
                **RW)
        tr.act(lambda e: e.copy(gsum[:, 0:12], grows[0:1, 0:12]), **RW)
        for p_ in range(2):
            tr.dve(lambda e, p_=p_: e.tensor_tensor(
                grows[0:1, p_ * 3:p_ * 3 + 3], gsum[0:1, 2 * p_ * 3:2 * p_ * 3 + 3],
                gsum[0:1, (2 * p_ + 1) * 3:(2 * p_ + 1) * 3 + 3], ALU.add),
                **RW)
        tr.act(lambda e: e.copy(gsum[:, 0:6], grows[0:1, 0:6]), **RW)
        tr.dve(lambda e: e.tensor_tensor(
            grows[0:1, 0:3], gsum[0:1, 0:3], gsum[0:1, 3:6], ALU.add), **RW)
        tr.act(lambda e: e.copy(gsum[:, 0:3], grows[0:1, 0:3]), **RW)
        for j in range(3):
            tr.dve(lambda e, j=j: e.tensor_scalar(
                gsum[:, j:j + 1], gsum[:, j:j + 1], 0.5 / (FH * D), 0.5e-8,
                ALU.mult, ALU.add), **RW)
        tr.act(lambda e: e.copy(gsum[:, 4:7], gsum[:, 0:3]), **RW)
        for j in range(3):
            tr.pe(lambda e, j=j: e.matmul(
                pb[:], g["onesr_f"][:], gsum[0:1, 4 + j:5 + j], start=True,
                stop=True), r=[bb(gsum), bb(g["onesr_f"])], w=[bb(pb)])
            tr.dve(lambda e, j=j: e.tensor_copy(g["thr_s"][:, j:j + 1], pb[:]),
                   r=[bb(pb)], w=[bb(g["thr_s"])])
            tr.act(lambda e, j=j: e.mul(
                g["nthr_s"][:, j:j + 1], g["thr_s"][:, j:j + 1], -1.0),
                r=[bb(g["thr_s"])], w=[bb(g["nthr_s"])])



def _p4q(nc, tr, g, ein, w12qc, w3qc):
    bb = tr.buf
    with ExitStack() as px:
        E = px.enter_context
        wf_t = [E(nc.sbuf_tensor(f"p4wf{i}", [128, 2048], F32)) for i in range(2)]
        bl_t = [E(nc.sbuf_tensor("p4bl0", [128, 2048], BF16))] * 2
        wq_t = [E(nc.sbuf_tensor(f"p4wq{i}", [128, 2048], BF16)) for i in range(2)]

        qjobs = []
        for hc in range(NHC):
            qjobs.append((ein["w14"][hc * D:(hc + 1) * D, :],
                          w12qc[hc * 2 * D:hc * 2 * D + D, :], w12qc, 0, 2048))
            qjobs.append((ein["w24"][hc * D:(hc + 1) * D, :],
                          w12qc[hc * 2 * D + D:(hc + 1) * 2 * D, :], w12qc, 1, 2048))
        for dc in range(NDC):
            for (o_, n_) in ((0, 2048), (2048, 2048), (4096, 1536)):
                r0 = dc * FH + o_
                qjobs.append((ein["w34"][r0:r0 + n_, :],
                              w3qc[r0:r0 + n_, :], w3qc, 2, n_))
        for i, (src, dst, track, j, rows) in enumerate(qjobs):
            rows = rows
            wf = wf_t[i % 2]
            bl = bl_t[i % 2]
            wq = wq_t[i % 2]
            nc_ = rows
            em = tr.dve
            tr.dma("sp" if i % 2 == 0 else "g",
                   lambda e, src=src, wf=wf, nc_=nc_: e.dma_start(
                       out=wf[:, 0:nc_],
                       in_=src.rearrange("(i p) c -> p i c", p=128)),
                   w=[bb(wf)])
            em(lambda e, wf=wf, bl=bl, nc_=nc_, j=j: e.tensor_scalar(
                bl[:, 0:nc_], wf[:, 0:nc_], g["nthr_s"][:, j:j + 1], None,
                ALU.is_lt),
                r=[bb(wf), bb(g["nthr_s"])], w=[bb(bl)])
            em(lambda e, wf=wf, bl=bl, wq=wq, nc_=nc_, j=j: e.scalar_tensor_tensor(
                wq[:, 0:nc_], wf[:, 0:nc_], g["thr_s"][:, j:j + 1], bl[:, 0:nc_],
                ALU.is_gt, ALU.subtract),
                r=[bb(wf), bb(bl), bb(g["thr_s"])], w=[bb(wq)])
            tr.dma("g" if i % 2 == 0 else "sp",
                   lambda e, dst=dst, wq=wq, nc_=nc_: e.dma_start(
                       out=dst.rearrange("(i p) c -> p i c", p=128),
                       in_=wq[:, 0:nc_]),
                   r=[bb(wq)], w=[bb(track)])



_LN_N = [0]


def _ln_x(nc, tr, g):
    bb = tr.buf
    xb, ln = g["xb"], g["ln"]
    _LN_N[0] += 1
    u_ = f"_{_LN_N[0]}"
    with ExitStack() as px:
        E = px.enter_context
        sq = [E(nc.sbuf_tensor(f"lnsq{i}{u_}", [128, 512], BF16)) for i in range(2)]
        scr = E(nc.sbuf_tensor("lnscr" + u_, [128, 512], F32))
        g["row_t"] = E(nc.sbuf_tensor("lnrowt" + u_, [1, TPC], F32))
        psx = E(nc.psum_tensor("lnpsx" + u_, [1, 512], F32))
        psq = E(nc.psum_tensor("lnpsq" + u_, [1, 512], F32))
        psb = [E(nc.psum_tensor(f"lnpsb{i}{u_}", [128, 512], F32)) for i in range(2)]

        for tg in range(NTG):
            t0 = tg * 512
            for dc in range(NDC):
                s = sq[dc % 2]
                tr.act(lambda e, dc=dc, t0=t0, s=s: e.activation(
                    s[:], xb[dc][:, t0:t0 + 512], AF.Square),
                    r=[bb(xb[dc])], w=[bb(s)])
                tr.pe(lambda e, dc=dc, t0=t0: e.matmul(
                    psx[:], g["ones_bf"][:], xb[dc][:, t0:t0 + 512],
                    start=(dc == 0), stop=(dc == NDC - 1)),
                    r=[bb(xb[dc]), bb(g["ones_bf"])], w=[bb(psx)])
                tr.pe(lambda e, s=s, dc=dc: e.matmul(
                    psq[:], g["ones_bf"][:], s[:],
                    start=(dc == 0), stop=(dc == NDC - 1)),
                    r=[bb(s), bb(g["ones_bf"])], w=[bb(psq)])
            tr.dve(lambda e, t0=t0: e.tensor_copy(
                g["row_m"][:, t0:t0 + 512], psx[:]),
                r=[bb(psx)], w=[bb(g["row_m"])])
            tr.dve(lambda e, t0=t0: e.tensor_copy(
                g["row_r"][:, t0:t0 + 512], psq[:]),
                r=[bb(psq)], w=[bb(g["row_r"])])
        _stats_pipeline(nc, tr, g, n=TPC)
        for tg in range(NTG):
            t0 = tg * 512
            tr.pe(lambda e, t0=t0: e.matmul(
                psb[0][:], g["onesr_f"][:], g["row_m"][0:1, t0:t0 + 512],
                start=True, stop=True),
                r=[bb(g["row_m"]), bb(g["onesr_f"])], w=[bb(psb[0])])
            tr.pe(lambda e, t0=t0: e.matmul(
                psb[1][:], g["onesr_f"][:], g["row_r"][0:1, t0:t0 + 512],
                start=True, stop=True),
                r=[bb(g["row_r"]), bb(g["onesr_f"])], w=[bb(psb[1])])
            for dc in range(NDC):
                tr.dve(lambda e, dc=dc, t0=t0: e.tensor_tensor(
                    scr[:], xb[dc][:, t0:t0 + 512], psb[0][:], ALU.subtract),
                    r=[bb(xb[dc]), bb(psb[0])], w=[bb(scr)])
                tr.dve(lambda e, dc=dc, t0=t0: e.tensor_tensor(
                    ln[dc][:, t0:t0 + 512], scr[:], psb[1][:], ALU.mult),
                    r=[bb(scr), bb(psb[1])], w=[bb(ln[dc])])



_FFN_N = [0]


def _p_ffn(nc, tr, g, w12qc, w3qc, y):
    bb = tr.buf
    xb, ln = g["xb"], g["ln"]
    _FFN_N[0] += 1
    u_ = f"_{_FFN_N[0]}"
    with ExitStack() as px:
        E = px.enter_context
        hid = [E(nc.sbuf_tensor(f"fh{i}{u_}", [128, 512], BF16)) for i in range(HHC)]
        wq12 = [E(nc.sbuf_tensor(f"fwq12{i}{u_}", [128, 4096], BF16)) for i in range(2)]
        wq3 = [E(nc.sbuf_tensor(f"fwq3{i}{u_}", [128, HHC * 128], BF16))
               for i in range(2)]
        sg = E(nc.sbuf_tensor("fsg" + u_, [128, 512], BF16))
        outf = E(nc.sbuf_tensor("foutf" + u_, [128, 512], F32))
        psA = [E(nc.psum_tensor(f"fpsA{i}{u_}", [128, 512], F32)) for i in range(2)]
        psB = [E(nc.psum_tensor(f"fpsB{i}{u_}", [128, 512], F32)) for i in range(2)]
        ps3 = [E(nc.psum_tensor(f"fps3{i}{u_}", [128, 512], F32)) for i in range(2)]

        for tt in range(NTG):
            t0 = tt * 512
            for hh in range(2):
                for i in range(HHC):
                    hc = hh * HHC + i
                    b12 = wq12[i % 2]
                    b1 = b12[:, 0:2048]
                    b2 = b12[:, 2048:4096]
                    tr.dma("sp" if i % 2 == 0 else "g",
                           lambda e, hc=hc, b12=b12: e.dma_start(
                               out=b12[:], in_=w12qc[hc * 2 * D:(hc + 1) * 2 * D, :]
                               .rearrange("(i p) c -> p i c", p=128)),
                           r=[bb(w12qc)], w=[bb(b12)])
                    pA, pB = psA[i % 2], psB[i % 2]
                    b12b = bb(wq12[i % 2])
                    for ic in range(NDC):
                        tr.pe(lambda e, ic=ic, t0=t0, b1=b1, pA=pA: e.matmul(
                            pA[:], b1[:, ic * 128:(ic + 1) * 128],
                            ln[ic][:, t0:t0 + 512],
                            start=(ic == 0), stop=(ic == NDC - 1)),
                            r=[b12b, bb(ln[ic])], w=[bb(pA)])
                    for ic in range(NDC):
                        tr.pe(lambda e, ic=ic, t0=t0, b2=b2, pB=pB: e.matmul(
                            pB[:], b2[:, ic * 128:(ic + 1) * 128],
                            ln[ic][:, t0:t0 + 512],
                            start=(ic == 0), stop=(ic == NDC - 1)),
                            r=[b12b, bb(ln[ic])], w=[bb(pB)])
                    tr.act(lambda e, pA=pA: e.activation(sg[:], pA[:], AF.Sigmoid),
                           r=[bb(pA)], w=[bb(sg)])
                    tr.dve(lambda e, i=i, pB=pB: e.tensor_tensor(
                        hid[i][:], sg[:], pB[:], ALU.mult),
                        r=[bb(sg), bb(pB)], w=[bb(hid[i])])
                for dc in range(NDC):
                    b3 = wq3[dc % 2]
                    r0 = dc * FH + hh * (HHC * 128)
                    tr.dma("sp" if dc % 2 == 0 else "g",
                           lambda e, r0=r0, b3=b3: e.dma_start(
                               out=b3[:], in_=w3qc[r0:r0 + HHC * 128, :]
                               .rearrange("(i p) c -> p i c", p=128)),
                           r=[bb(w3qc)], w=[bb(b3)])
                    p3 = ps3[dc % 2]
                    for i in range(HHC):
                        tr.pe(lambda e, i=i, b3=b3, p3=p3: e.matmul(
                            p3[:], b3[:, i * 128:(i + 1) * 128], hid[i][:],
                            start=(i == 0), stop=(i == HHC - 1)),
                            r=[bb(b3), bb(hid[i])], w=[bb(p3)])
                    if y is None or hh == 0:
                        tr.dve(lambda e, dc=dc, t0=t0, p3=p3: e.scalar_tensor_tensor(
                            xb[dc][:, t0:t0 + 512], p3[:], 1.0,
                            xb[dc][:, t0:t0 + 512], ALU.mult, ALU.add),
                            r=[bb(p3)], w=[bb(xb[dc])])
                    else:
                        tr.dve(lambda e, dc=dc, t0=t0, p3=p3: e.scalar_tensor_tensor(
                            outf[:], p3[:], 1.0, xb[dc][:, t0:t0 + 512],
                            ALU.mult, ALU.add),
                            r=[bb(p3), bb(xb[dc])], w=[bb(outf)])
                        tr.dma("sp", lambda e, dc=dc, t0=t0: e.dma_start(
                            out=y[dc * 128:(dc + 1) * 128, t0:t0 + 512],
                            in_=outf[:]), r=[bb(outf)], w=[bb(y)])


def _p6(nc, tr, g, ein, q_dram, kv_stage, kv_gath):
    bb = tr.buf
    ln = g["ln"]
    with ExitStack() as px:
        E = px.enter_context
        wt = [E(nc.sbuf_tensor(f"p6w{i}", [128, NDC * 512], BF16)) for i in range(2)]
        cosc = E(nc.sbuf_tensor("p6cos", [128, NCH * 256], BF16))
        sinc = E(nc.sbuf_tensor("p6sin", [128, NCH * 256], BF16))
        brow = [E(nc.sbuf_tensor(f"p6brow{i}", [1, 512], BF16)) for i in range(2)]
        qk = E(nc.sbuf_tensor("p6qk", [128, 512], BF16))
        rq = E(nc.sbuf_tensor("p6rq", [128, 512], BF16))
        tA = E(nc.sbuf_tensor("p6tA", [128, 256], BF16))
        tB = E(nc.sbuf_tensor("p6tB", [128, 256], BF16))
        ksq = E(nc.sbuf_tensor("p6ksq", [128, 512], BF16))
        nscr = E(nc.sbuf_tensor("p6nscr", [128, 1], F32))
        nstore = E(nc.sbuf_tensor("p6nst", [128, 512], F32))
        nsum = E(nc.sbuf_tensor("p6nsum", [128, 32], F32))
        ntr = E(nc.sbuf_tensor("p6ntr", [32, 1], F32))
        nflat = E(nc.sbuf_tensor("p6nflat", [1, 32], F32))
        ex2 = E(nc.sbuf_tensor("p6ex2", [1, 2], F32))
        ps = [E(nc.psum_tensor(f"p6ps{i}", [128, 512], F32)) for i in range(2)]
        pnt = E(nc.psum_tensor("p6pnt", [32, 128], F32))

        tr.dma("g", lambda e: e.dma_start(
            out=cosc[:], in_=ein["cosP"].rearrange("(tc p) c -> p tc c", p=128)),
            w=[bb(cosc)])
        tr.dma("g", lambda e: e.dma_start(
            out=sinc[:], in_=ein["sinP"].rearrange("(tc p) c -> p tc c", p=128)),
            w=[bb(sinc)])
        tr.gx(lambda e: e.memset(nsum[:], 0.0), w=[bb(nsum)])
        tr.gx(lambda e: e.memset(nstore[:], 0.0), w=[bb(nstore)])

        for ng in range(12):
            w_ = wt[ng % 2]
            br = brow[ng % 2]
            if ng < 8:
                src = ein["wqkT"]
                co = ng * 512
                bsrc = ein["bqkrow"][0:1, ng * 512:(ng + 1) * 512]
            else:
                src = ein["wvT"]
                co = (ng - 8) * 512
                bsrc = ein["bvrow"][0:1, co:co + 512]
            tr.dma("g", lambda e, bsrc=bsrc, br=br: e.dma_start(
                out=br[:], in_=bsrc), w=[bb(br)])
            for ic in range(NDC):
                tr.dma("g", lambda e, ic=ic, co=co, src=src, w_=w_: e.dma_start(
                    out=w_[:, ic * 512:(ic + 1) * 512],
                    in_=src[ic * 128:(ic + 1) * 128, co:co + 512]), w=[bb(w_)])
            for tc in range(NCH):
                p = ps[tc % 2]
                for ic in range(NDC):
                    tr.pe(lambda e, ic=ic, tc=tc, p=p, w_=w_: e.matmul(
                        p[:], ln[ic][:, tc * 128:(tc + 1) * 128],
                        w_[:, ic * 512:(ic + 1) * 512],
                        start=(ic == 0), stop=False),
                        r=[bb(ln[ic]), bb(w_)], w=[bb(p)])
                tr.pe(lambda e, p=p, br=br: e.matmul(
                    p[:], g["onesr_bf"][:], br[0:1, :],
                    start=False, stop=True),
                    r=[bb(g["onesr_bf"]), bb(br)], w=[bb(p)])
                tr.dve(lambda e, p=p: e.tensor_copy(qk[:], p[:]),
                       r=[bb(p)], w=[bb(qk)])
                if ng >= 8:
                    tr.dma("sp", lambda e, tc=tc, co=co: e.dma_start(
                        out=kv_stage[TPC + tc * 128:TPC + (tc + 1) * 128,
                                     co:co + 512], in_=qk[:]),
                        r=[bb(qk)], w=[bb(kv_stage)])
                    continue
                cs4 = cosc[:, tc * 256:(tc + 1) * 256].rearrange(
                    "p (h c) -> p h c", c=GM)
                sn4 = sinc[:, tc * 256:(tc + 1) * 256].rearrange(
                    "p (h c) -> p h c", c=GM)
                tA4 = tA[:].rearrange("p (h c) -> p h c", c=GM)
                tB4 = tB[:].rearrange("p (h c) -> p h c", c=GM)
                qk4 = qk[:].rearrange("p (h c) -> p h c", c=128)
                rq4 = rq[:].rearrange("p (h c) -> p h c", c=128)
                qe, qo = qk4[:, :, 0:GM], qk4[:, :, GM:128]
                re_, ro = rq4[:, :, 0:GM], rq4[:, :, GM:128]
                csb, snb = bb(cosc), bb(sinc)
                tr.dve(lambda e, qe=qe, cs4=cs4, tA4=tA4: e.tensor_tensor(
                    tA4, qe, cs4, ALU.mult), r=[bb(qk), csb], w=[bb(tA)])
                tr.dve(lambda e, qo=qo, sn4=sn4, tB4=tB4: e.tensor_tensor(
                    tB4, qo, sn4, ALU.mult), r=[bb(qk), snb], w=[bb(tB)])
                tr.dve(lambda e, re_=re_, tA4=tA4, tB4=tB4: e.tensor_tensor(
                    re_, tA4, tB4, ALU.subtract),
                    r=[bb(tA), bb(tB)], w=[bb(rq)])
                tr.dve(lambda e, qe=qe, sn4=sn4, tA4=tA4: e.tensor_tensor(
                    tA4, qe, sn4, ALU.mult), r=[bb(qk), snb], w=[bb(tA)])
                tr.dve(lambda e, qo=qo, cs4=cs4, tB4=tB4: e.tensor_tensor(
                    tB4, qo, cs4, ALU.mult), r=[bb(qk), csb], w=[bb(tB)])
                tr.dve(lambda e, ro=ro, tA4=tA4, tB4=tB4: e.tensor_tensor(
                    ro, tA4, tB4, ALU.add),
                    r=[bb(tA), bb(tB)], w=[bb(rq)])
                if ng < 4:
                    tr.dma("sp", lambda e, tc=tc, ng=ng: e.dma_start(
                        out=q_dram[tc * 128:(tc + 1) * 128,
                                   ng * 512:(ng + 1) * 512], in_=rq[:]),
                        r=[bb(rq)], w=[bb(q_dram)])
                else:
                    tr.dma("sp", lambda e, tc=tc, ng=ng: e.dma_start(
                        out=kv_stage[tc * 128:(tc + 1) * 128,
                                     (ng - 4) * 512:(ng - 3) * 512], in_=rq[:]),
                        r=[bb(rq)], w=[bb(kv_stage)])
                tr.act(lambda e: e.activation(ksq[:], rq[:], AF.Square),
                       r=[bb(rq)], w=[bb(ksq)])
                for hh in range(4):
                    colid = ng * 4 + hh
                    tr.dve(lambda e, hh=hh, tc=tc, colid=colid: e.tensor_reduce(
                        nstore[:, tc * 32 + colid:tc * 32 + colid + 1],
                        ksq[:, hh * 128:(hh + 1) * 128],
                        mybir.AxisListType.X, ALU.add),
                        r=[bb(ksq)], w=[bb(nstore)])

        for colid in range(32):
            tr.dve(lambda e, colid=colid: e.tensor_reduce(
                nsum[:, colid:colid + 1],
                nstore[:, colid:512:32], mybir.AxisListType.X, ALU.max),
                r=[bb(nstore)], w=[bb(nsum)])
        ident_f = px.enter_context(nc.sbuf_tensor("p6idf", [128, 128], F32))
        ndram = nc.dram_tensor("p6ndram", [32, 1], F32).ap()
        tr.dma("sp", lambda e: e.dma_start(out=ident_f[:], in_=ein["ident"]),
               w=[bb(ident_f)])
        tr.pe(lambda e: e.transpose(pnt[:], nsum[:], ident_f[:]),
              r=[bb(nsum), bb(ident_f)], w=[bb(pnt)])
        tr.dve(lambda e: e.tensor_reduce(ntr[:, 0:1], pnt[:],
                                         mybir.AxisListType.X, ALU.max),
               r=[bb(pnt)], w=[bb(ntr)])
        tr.dma("g", lambda e: e.dma_start(out=ndram, in_=ntr[:]),
               r=[bb(ntr)], w=[bb(ndram)])
        tr.dma("g", lambda e: e.dma_start(
            out=nflat[:], in_=ndram.rearrange("r c -> c r")),
            r=[bb(ndram)], w=[bb(nflat)])
        tr.dve(lambda e: e.tensor_reduce(ex2[:, 0:1], nflat[0:1, 0:16],
                                         mybir.AxisListType.X, ALU.max),
               r=[bb(nflat)], w=[bb(ex2)])
        tr.dve(lambda e: e.tensor_reduce(ex2[:, 1:2], nflat[0:1, 16:32],
                                         mybir.AxisListType.X, ALU.max),
               r=[bb(nflat)], w=[bb(ex2)])
        tr.act(lambda e: e.mul(ex2[:, 0:2], ex2[:, 0:2], 1.05),
               r=[bb(ex2)], w=[bb(ex2)])
        tr.dma("g", lambda e: e.dma_start(
            out=kv_stage[2 * TPC:2 * TPC + 1, 0:2], in_=ex2[:]),
            r=[bb(ex2)], w=[bb(kv_stage)])
        tr.cc(lambda e: e.collective_compute(
            "AllGather", ALU.bypass,
            replica_groups=[[0, 1, 2, 3, 4, 5, 6, 7]],
            ins=[kv_stage.opt()], outs=[kv_gath.opt()]),
            r=[bb(kv_stage)], w=[bb(kv_gath)])



def _p7(nc, tr, g, ein, kv_stage, kv_gath, kbuf1, vbuf1, q_dram):
    bb = tr.buf
    xb = g["xb"]
    av = g["ln"]
    with ExitStack() as px:
        E = px.enter_context
        bounce = [E(nc.sbuf_tensor(f"p7b{i}", [128, D], BF16)) for i in range(2)]
        tri_bf = E(nc.sbuf_tensor("p7tri", [128, 4 * 512], BF16))
        mrow = E(nc.sbuf_tensor("p7mrow", [1, 16], BF16))
        mser = E(nc.sbuf_tensor("p7mser", [1, 4], F32))
        mser2 = E(nc.sbuf_tensor("p7mser2", [1, 4], F32))
        negM = E(nc.sbuf_tensor("p7negM", [128, 1], F32))
        bias_all = E(nc.sbuf_tensor("p7bias", [128, NSLOT], F32))
        Ksb = E(nc.sbuf_tensor("p7K", [128, 2 * TPC], BF16))
        Vsb = E(nc.sbuf_tensor("p7V", [128, 2 * TPC], BF16))
        Qsb = E(nc.sbuf_tensor("p7Q", [128, TPC], BF16))
        Pt = [E(nc.sbuf_tensor(f"p7P{i}", [128, 512], BF16)) for i in range(6)]
        srow = E(nc.sbuf_tensor("p7srow", [1, 512], F32))
        rbsb = E(nc.sbuf_tensor("p7rb", [128, 512], F32))
        wo = [E(nc.sbuf_tensor(f"p7wo{i}", [128, 2048], BF16)) for i in range(2)]
        psS = [E(nc.psum_tensor(f"p7psS{i}", [128, 512], F32)) for i in range(3)]
        psAV = E(nc.psum_tensor("p7psAV", [128, 512], F32))
        psSum = E(nc.psum_tensor("p7psSum", [1, 512], F32))
        psR = E(nc.psum_tensor("p7psR", [128, 512], F32))
        pso = [E(nc.psum_tensor(f"p7pso{i}", [128, 512], F32)) for i in range(2)]

        for i in range(4):
            tr.dma("g", lambda e, i=i: e.dma_start(
                out=tri_bf[:, i * 512:(i + 1) * 512],
                in_=ein["triband"][i * 128:(i + 1) * 128, :]), w=[bb(tri_bf)])

        for grp in range(16):
            bt = bounce[grp % 2]
            tr.dma("g", lambda e, grp=grp, bt=bt: e.indirect_dma_start(
                bt[:], None, kv_gath, IndirectOffsetOnAxis(
                    g["idxk_sb"][:, grp:grp + 1], 0)),
                r=[bb(kv_gath), bb(g["idxk_sb"])], w=[bb(bt)])
            tr.dma("g", lambda e, grp=grp, bt=bt: e.dma_start(
                out=kbuf1[grp * 128:(grp + 1) * 128, :], in_=bt[:]),
                r=[bb(bt)], w=[bb(kbuf1)])
        for grp in range(16):
            bt = bounce[grp % 2]
            tr.dma("g", lambda e, grp=grp, bt=bt: e.indirect_dma_start(
                bt[:], None, kv_gath, IndirectOffsetOnAxis(
                    g["idxv_sb"][:, grp:grp + 1], 0)),
                r=[bb(kv_gath), bb(g["idxv_sb"])], w=[bb(bt)])
            tr.dma("g", lambda e, grp=grp, bt=bt: e.dma_start(
                out=vbuf1[grp * 128:(grp + 1) * 128, :], in_=bt[:]),
                r=[bb(bt)], w=[bb(vbuf1)])

        for r in range(8):
            tr.dma("sp", lambda e, r=r: e.dma_start(
                out=mrow[0:1, r * 2:(r + 1) * 2],
                in_=kv_gath[r * KV_ROWS + 2 * TPC:r * KV_ROWS + 2 * TPC + 1, 0:2]),
                r=[bb(kv_gath)], w=[bb(mrow)])
        tr.dve(lambda e: e.tensor_reduce(mser[:, 0:1], mrow[0:1, 0:16:2],
                                         mybir.AxisListType.X, ALU.max),
               r=[bb(mrow)], w=[bb(mser)])
        tr.dve(lambda e: e.tensor_reduce(mser[:, 1:2], mrow[0:1, 1:16:2],
                                         mybir.AxisListType.X, ALU.max),
               r=[bb(mrow)], w=[bb(mser)])
        tr.act(lambda e: e.copy(mser2[:, 0:2], mser[:, 0:2]),
               r=[bb(mser)], w=[bb(mser2)])
        tr.dve(lambda e: e.tensor_tensor(mser[:, 2:3], mser2[:, 0:1],
                                         mser2[:, 1:2], ALU.mult),
               r=[bb(mser2)], w=[bb(mser)])
        tr.act(lambda e: e.activation(mser[:, 3:4], mser[:, 2:3], AF.Sqrt),
               r=[], w=[bb(mser)])
        tr.dve(lambda e: e.tensor_scalar_mul(mser[:, 3:4], mser[:, 3:4], -1.0),
               r=[], w=[bb(mser)])
        tr.pe(lambda e: e.matmul(psR[:, 0:1], g["onesr_f"][:], mser[0:1, 3:4],
                                 start=True, stop=True),
              r=[bb(mser), bb(g["onesr_f"])], w=[bb(psR)])
        tr.dve(lambda e: e.tensor_copy(negM[:], psR[:, 0:1]),
               r=[bb(psR)], w=[bb(negM)])
        tr.act(lambda e: e.activation(bias_all[:], g["moff_sb"][:],
                                      AF.Identity, bias=negM[:, 0:1]),
               r=[bb(g["moff_sb"]), bb(negM)], w=[bb(bias_all)])

        for h in range(NH):
            for q4 in range(4):
                tr.dma("sp", lambda e, h=h, q4=q4: e.dma_start(
                    out=Ksb[:, q4 * 512:(q4 + 1) * 512],
                    in_=kbuf1[q4 * 512:(q4 + 1) * 512, h * 128:(h + 1) * 128],
                    transpose=True), r=[bb(kbuf1)], w=[bb(Ksb)])
                tr.dma("sp", lambda e, h=h, q4=q4: e.dma_start(
                    out=Ksb[:, TPC + q4 * 512:TPC + (q4 + 1) * 512],
                    in_=kv_stage[q4 * 512:(q4 + 1) * 512, h * 128:(h + 1) * 128],
                    transpose=True), r=[bb(kv_stage)], w=[bb(Ksb)])
                tr.dma("sp", lambda e, h=h, q4=q4: e.dma_start(
                    out=Qsb[:, q4 * 512:(q4 + 1) * 512],
                    in_=q_dram[q4 * 512:(q4 + 1) * 512, h * 128:(h + 1) * 128],
                    transpose=True), r=[bb(q_dram)], w=[bb(Qsb)])
                tr.dma("sp", lambda e, h=h, q4=q4: e.dma_start(
                    out=Vsb[:, q4 * 512:(q4 + 1) * 512],
                    in_=vbuf1[q4 * 512:(q4 + 1) * 512, h * 128:(h + 1) * 128]
                    .rearrange("(kt p) c -> p kt c", p=128)),
                    r=[bb(vbuf1)], w=[bb(Vsb)])
                tr.dma("sp", lambda e, h=h, q4=q4: e.dma_start(
                    out=Vsb[:, TPC + q4 * 512:TPC + (q4 + 1) * 512],
                    in_=kv_stage[TPC + q4 * 512:TPC + (q4 + 1) * 512,
                                 h * 128:(h + 1) * 128]
                    .rearrange("(kt p) c -> p kt c", p=128)),
                    r=[bb(kv_stage)], w=[bb(Vsb)])
            slot = 0
            for gq in range(4):
                nkt = 20 + 4 * gq
                for kt in range(nkt):
                    pS = psS[kt % 3]
                    P = Pt[kt % 6]
                    tr.pe(lambda e, kt=kt, gq=gq, pS=pS: e.matmul(
                        pS[:], Ksb[:, kt * 128:(kt + 1) * 128],
                        Qsb[:, gq * 512:(gq + 1) * 512],
                        start=True, stop=True),
                        r=[bb(Ksb), bb(Qsb)], w=[bb(pS)])
                    tr.act(lambda e, pS=pS, P=P, slot=slot: e.activation(
                        P[:], pS[:], AF.Exp, bias=bias_all[:, slot:slot + 1]),
                        r=[bb(pS), bb(bias_all)], w=[bb(P)])
                    if kt >= 16 + 4 * gq:
                        ti = kt - 16 - 4 * gq
                        tr.dve(lambda e, P=P, ti=ti: e.tensor_tensor(
                            P[:], P[:], tri_bf[:, ti * 512:(ti + 1) * 512],
                            ALU.mult), r=[bb(tri_bf)], w=[bb(P)])
                    tr.pe(lambda e, kt=kt, P=P, nkt=nkt: e.matmul(
                        psAV[:], Vsb[:, kt * 128:(kt + 1) * 128], P[:],
                        start=(kt == 0), stop=(kt == nkt - 1)),
                        r=[bb(Vsb), bb(P)], w=[bb(psAV)])
                    tr.pe(lambda e, kt=kt, P=P, nkt=nkt: e.matmul(
                        psSum[:], g["ones_bf"][:], P[:],
                        start=(kt == 0), stop=(kt == nkt - 1)),
                        r=[bb(g["ones_bf"]), bb(P)], w=[bb(psSum)])
                    slot += 1
                tr.act(lambda e: e.copy(srow[:], psSum[:]),
                       r=[bb(psSum)], w=[bb(srow)])
                tr.dve(lambda e: e.reciprocal(srow[:], srow[:]),
                       r=[bb(srow)], w=[bb(srow)])
                tr.pe(lambda e: e.matmul(psR[:], g["onesr_f"][:], srow[:],
                                         start=True, stop=True),
                      r=[bb(srow), bb(g["onesr_f"])], w=[bb(psR)])
                tr.dve(lambda e: e.tensor_copy(rbsb[:], psR[:]),
                       r=[bb(psR)], w=[bb(rbsb)])
                tr.dve(lambda e, h=h, gq=gq: e.tensor_tensor(
                    av[h][:, gq * 512:(gq + 1) * 512], psAV[:], rbsb[:],
                    ALU.mult), r=[bb(psAV), bb(rbsb)], w=[bb(av[h])])

        for oc in range(NDC):
            w_ = wo[oc % 2]
            tr.dma("g", lambda e, oc=oc, w_=w_: e.dma_start(
                out=w_[:], in_=ein["wout4"][oc * D:(oc + 1) * D, :]
                .rearrange("(i p) c -> p i c", p=128)), w=[bb(w_)])
            for tg in range(NTG):
                p = pso[tg % 2]
                for ic in range(NDC):
                    tr.pe(lambda e, ic=ic, tg=tg, w_=w_, p=p: e.matmul(
                        p[:], w_[:, ic * 128:(ic + 1) * 128],
                        av[ic][:, tg * 512:(tg + 1) * 512],
                        start=(ic == 0), stop=(ic == NDC - 1)),
                        r=[bb(w_), bb(av[ic])], w=[bb(p)])
                tr.dve(lambda e, oc=oc, tg=tg, p=p: e.scalar_tensor_tensor(
                    xb[oc][:, tg * 512:(tg + 1) * 512], p[:],
                    g["bout_s"][:, oc:oc + 1],
                    xb[oc][:, tg * 512:(tg + 1) * 512], ALU.add, ALU.add),
                    r=[bb(p), bb(g["bout_s"])], w=[bb(xb[oc])])



def kernel(**inputs):
    in_maps = _host_inputs(inputs)
    nc = build()
    res = run_bass_kernel_spmd(nc, in_maps, core_ids=list(range(8)))
    out = np.empty((B, T, D), np.float32)
    for c in range(8):
        b, hlf = c // 2, c % 2
        out[b, hlf * TPC:(hlf + 1) * TPC, :] = np.asarray(res.results[c]["y"]).T
    return out


# version 16
# speedup vs baseline: 1.3295x; 1.2812x over previous
import math
import numpy as np
from contextlib import ExitStack

import concourse.bass as bass
import concourse.mybir as mybir
from concourse.bass_utils import run_bass_kernel_spmd
from concourse.bass import IndirectOffsetOnAxis

F32 = mybir.dt.float32
BF16 = mybir.dt.bfloat16
I32 = mybir.dt.int32
AF = mybir.ActivationFunctionType
ALU = mybir.AluOpType

B, T, D = 4, 4096, 2048
TPC = 2048
HALO = 3
NDC = D // 128
NTG = TPC // 512
CH = 128
NCH = TPC // CH
GM = 64
FH = 5632
NHC = FH // 128
NH, HD = 16, 128
SCALE_GLA = 1.0 / math.sqrt(GM)
SCALEB = 1.0 / math.sqrt(HD)
KV_ROWS = 2 * TPC + 1
NSLOT = sum(20 + 4 * g for g in range(4))
LN_EPS = 1e-5
HHC = NHC // 2
_DBG_STOP = [0]



class _Eng:
    def __init__(self, name):
        self.name = name
        self.ops = []
        self.n_dma = 0
        self.n_ms = 0
        self.seq = 0
        self.last_ms_seq = -1
        self.ms_list = []
        self.waited = {}


class _Op:
    __slots__ = ("eng", "fn", "kind", "seq", "phase", "waits", "incs",
                 "ms_val", "dma_val", "tiny")

    def __init__(self, eng, fn, kind, seq, phase):
        self.eng, self.fn, self.kind = eng, fn, kind
        self.seq, self.phase = seq, phase
        self.waits = []
        self.incs = []
        self.ms_val = None
        self.dma_val = None
        self.tiny = False


class _Buf:
    __slots__ = ("writer", "readers")

    def __init__(self):
        self.writer = None
        self.readers = []


class Prog:
    ENG2SECT = {"pe": "tensor", "act": "scalar", "dve": "vector",
                "sp": "sync", "g": "gpsimd"}

    def __init__(self, nc, ctx):
        self.nc = nc
        self.engs = {n: _Eng(n) for n in ("pe", "act", "dve", "sp", "g")}
        self.sems = {}
        for n in ("pe_s", "act_s", "dve_s", "g_s", "sp_s", "sp_d", "g_d", "cc_s"):
            self.sems[n] = ctx.enter_context(nc.semaphore(n))
        self.phase = 0
        self.n_cc = 0
        self._bufs = {}

    def buf(self, obj):
        key = id(obj)
        b = self._bufs.get(key)
        if b is None:
            b = self._bufs[key] = _Buf()
        return b

    def _milestone(self, op):
        e = op.eng
        if op.kind == "dma":
            return (e.name + "_d", 16 * e.n_dma)
        if op.kind == "cc":
            return ("cc_s", self.n_cc)
        if op.ms_val is None:
            if op.seq <= e.last_ms_seq:
                import bisect
                i = bisect.bisect_left(e.ms_list, (op.seq, -1))
                return (e.name + "_s", e.ms_list[i][1])
            e.n_ms += 1
            e.last_ms_seq = op.seq
            op.ms_val = e.n_ms
            op.incs.append((e.name + "_s", 1))
            e.ms_list.append((op.seq, op.ms_val))
        return (e.name + "_s", op.ms_val)

    def _wait(self, cons, prod):
        if prod is None:
            return
        if prod.kind == "exec":
            if prod.eng is cons.eng:
                if not prod.tiny:
                    return
            elif prod.phase < self.phase:
                return
        sem, val = self._milestone(prod)
        w = cons.eng.waited
        if w.get(sem, -1) >= val:
            return
        w[sem] = val
        cons.waits.append((sem, val))

    def emit(self, eng_name, fn, reads=(), writes=(), kind="exec", tiny=False):
        e = self.engs[eng_name]
        op = _Op(e, fn, kind, e.seq, self.phase)
        op.tiny = tiny
        e.seq += 1
        for r in reads:
            b = self.buf(r)
            self._wait(op, b.writer)
            b.readers.append(op)
        for wt in writes:
            b = self.buf(wt)
            self._wait(op, b.writer)
            for r in b.readers:
                self._wait(op, r)
            b.writer = op
            b.readers = []
        if kind == "dma":
            e.n_dma += 1
            op.dma_val = 16 * e.n_dma
            op.incs.append((e.name + "_d", 16))
        elif kind == "cc":
            self.n_cc += 1
            op.dma_val = self.n_cc
            op.incs.append(("cc_s", 1))
        e.ops.append(op)
        return op

    def flush(self):
        nc, sems = self.nc, self.sems
        if any(e.ops for e in self.engs.values()):
            with nc.Block() as blk:
                for name, e in self.engs.items():
                    if not e.ops:
                        continue
                    ops, e.ops = e.ops, []

                    def runner(engine, ops=ops):
                        for op in ops:
                            for sem, val in op.waits:
                                engine.wait_ge(sems[sem], val)
                            inst = op.fn(engine)
                            for sem, amt in op.incs:
                                inst.then_inc(sems[sem], amt)
                    getattr(blk, self.ENG2SECT[name])(runner)
        self.phase += 1

    def pe(self, fn, r=(), w=()):
        return self.emit("pe", fn, r, w)

    def act(self, fn, r=(), w=()):
        return self.emit("act", fn, r, w)

    def dve(self, fn, r=(), w=()):
        return self.emit("dve", fn, r, w)

    def dve_t(self, fn, r=(), w=()):
        return self.emit("dve", fn, r, w, tiny=True)

    def act_t(self, fn, r=(), w=()):
        return self.emit("act", fn, r, w, tiny=True)

    def dma(self, q, fn, r=(), w=()):
        return self.emit(q, fn, r, w, kind="dma")

    def gx(self, fn, r=(), w=()):
        return self.emit("g", fn, r, w)

    def cc(self, fn, r=(), w=()):
        return self.emit("g", fn, r, w, kind="cc")



def _t4(wT, n_in_tiles, n_out_tiles):
    a = wT.reshape(n_in_tiles, 128, n_out_tiles, 128)
    return np.ascontiguousarray(
        a.transpose(2, 0, 1, 3).reshape(n_out_tiles * n_in_tiles * 128, 128),
        dtype=np.float32)


def _host_inputs(inp):
    x = np.asarray(inp["L0_attn_a_hidden_states"])
    A = lambda a: np.ascontiguousarray(a, dtype=np.float32)

    perm = np.concatenate([h * HD + np.concatenate([np.arange(0, HD, 2),
                                                    np.arange(1, HD, 2)])
                           for h in range(NH)])
    wqkvT = np.asarray(inp["L0_attn_b_w_qkv"], dtype=np.float32).T.copy()
    wqkvT[:, 0:D] = wqkvT[:, 0:D][:, perm]
    wqkvT[:, D:2 * D] = wqkvT[:, D:2 * D][:, perm]
    bqkv = np.asarray(inp["L0_attn_b_b_qkv"], dtype=np.float32).copy()
    bqkv[0:D] = bqkv[0:D][perm]
    bqkv[D:2 * D] = bqkv[D:2 * D][perm]

    w1T = np.asarray(inp["L0_ffn_w1"], dtype=np.float32).T.copy()
    w2T = np.asarray(inp["L0_ffn_w2"], dtype=np.float32).T.copy()
    w3T = np.asarray(inp["L0_ffn_w3"], dtype=np.float32).T.copy()

    half = HD // 2
    thetas = 10000.0 ** (-np.arange(half, dtype=np.float64) / half)
    rs = math.sqrt(SCALEB)

    tri = (np.arange(512)[:, None] <= np.arange(512)[None, :]).astype(np.float32)
    glam = (np.arange(128)[:, None] <= np.arange(128)[None, :]).astype(np.float32)

    pcol = lambda v: A(np.asarray(v).reshape(-1, 128).T)
    memgb = np.asarray(inp["L0_attn_a_mem_gate_b"], dtype=np.float32)

    shared = {
        "convw": A(np.asarray(inp["L0_attn_a_conv_w"]).reshape(NDC, 128, 4)
                   .transpose(1, 0, 2).reshape(128, NDC * 4)),
        "convb": pcol(inp["L0_attn_a_conv_b"]),
        "resw": pcol(inp["L0_attn_a_residual_weight"]),
        "inprojT": A(np.asarray(inp["L0_attn_a_in_proj_w"]).T),
        "memgT": A(np.asarray(inp["L0_attn_a_mem_gate_w"]).T),
        "memgb2": A(np.stack([memgb[:GM], memgb[GM:]], axis=1)),
        "chw0T": A(np.asarray(inp["L0_attn_a_ch_gate_w0"]).T),
        "chw1T": A(np.asarray(inp["L0_attn_a_ch_gate_w1"]).T),
        "chb1": A(np.asarray(inp["L0_attn_a_ch_gate_b1"]).reshape(1, D)),
        "wqkT": A(wqkvT[:, 0:2 * D]),
        "wvT": A(wqkvT[:, 2 * D:]),
        "bqkrow": A(bqkv[0:2 * D].reshape(1, 2 * D)),
        "bvrow": A(bqkv[2 * D:].reshape(1, D)),
        "wout4": _t4(np.asarray(inp["L0_attn_b_w_out"]).T, NDC, NDC),
        "bout": pcol(inp["L0_attn_b_b_out"]),
        "w14": _t4(w1T, NDC, NHC),
        "w24": _t4(w2T, NDC, NHC),
        "w34": _t4(w3T, NHC, NDC),
        "triband": tri, "glamask": glam,
        "ident": np.eye(128, dtype=np.float32),
    }

    in_maps = []
    for c in range(8):
        b, hlf = c // 2, c % 2
        t0 = hlf * TPC
        xs = x[b, t0:t0 + TPC, :].T
        if hlf == 0:
            haloc = np.zeros((D, HALO), np.float32)
        else:
            haloc = x[b, t0 - HALO:t0, :].T
        xT = A(np.concatenate([haloc, xs], axis=1))

        pos = np.arange(t0, t0 + TPC, dtype=np.float64)
        ang = pos[:, None] * thetas[None, :]
        cosP = np.tile((np.cos(ang) * rs).astype(np.float32), (1, 4))
        sinP = np.tile((np.sin(ang) * rs).astype(np.float32), (1, 4))

        moff = np.zeros((128, NSLOT), np.float32)
        col = 0
        for g_ in range(4):
            for kt in range(20 + 4 * g_):
                if hlf == 0 and kt < 16:
                    moff[:, col] = -30000.0
                col += 1

        smask = np.zeros((GM, 8), np.float32)
        if hlf == 1:
            smask[:, c - 1] = 1.0

        ra = c - 1 if hlf == 1 else c
        grp = np.arange(16)
        p = np.arange(128)
        idxk = (ra * KV_ROWS + grp[None, :] * 128 + p[:, None]).astype(np.int32)
        idxv = (ra * KV_ROWS + TPC + grp[None, :] * 128 + p[:, None]).astype(np.int32)

        m = dict(shared)
        m.update({
            "xT": xT, "cosP": A(cosP), "sinP": A(sinP), "moff": moff,
            "smask": smask,
            "idxk": np.ascontiguousarray(idxk),
            "idxv": np.ascontiguousarray(idxv),
            "g1": A(w1T[c * 256:(c + 1) * 256, :]),
            "g2": A(w2T[c * 256:(c + 1) * 256, :]),
            "g3": A(w3T[c * 704:(c + 1) * 704, :]),
        })
        in_maps.append(m)
    return in_maps


def _dump_x(nc, tr, g, y):
    bb = tr.buf
    for dc in range(NDC):
        tr.dma("g", lambda e, dc=dc: e.dma_start(
            out=y[dc * 128:(dc + 1) * 128, :], in_=g["xb"][dc][:]),
            r=[bb(g["xb"][dc])], w=[bb(y)])
    tr.flush()


def build():
    _LN_N[0] = 0
    _FFN_N[0] = 0
    nc = bass.Bass("TRN2", target_bir_lowering=False, debug=False)

    ein = {}

    def EIN(name, shape, dtype=F32):
        ein[name] = nc.dram_tensor(name, list(shape), dtype,
                                   kind="ExternalInput").ap()

    EIN("xT", (D, TPC + HALO))
    EIN("convw", (128, NDC * 4)); EIN("convb", (128, NDC)); EIN("resw", (128, NDC))
    EIN("inprojT", (D, 2 * GM)); EIN("memgT", (D, 2 * GM)); EIN("memgb2", (GM, 2))
    EIN("chw0T", (D, 128)); EIN("chw1T", (128, D)); EIN("chb1", (1, D))
    EIN("wqkT", (D, 2 * D)); EIN("wvT", (D, D))
    EIN("bqkrow", (1, 2 * D)); EIN("bvrow", (1, D))
    EIN("wout4", (NDC * D, 128)); EIN("bout", (128, NDC))
    EIN("w14", (NHC * D, 128)); EIN("w24", (NHC * D, 128)); EIN("w34", (NDC * FH, 128))
    EIN("triband", (512, 512)); EIN("glamask", (128, 128)); EIN("ident", (128, 128))
    EIN("cosP", (TPC, 256)); EIN("sinP", (TPC, 256))
    EIN("moff", (128, NSLOT)); EIN("smask", (GM, 8))
    EIN("idxk", (128, 16), I32); EIN("idxv", (128, 16), I32)
    EIN("g1", (256, FH)); EIN("g2", (256, FH)); EIN("g3", (704, D))

    y = nc.dram_tensor("y", [D, TPC], F32, kind="ExternalOutput").ap()

    u_dram = nc.dram_tensor("u_dram", [TPC, D], BF16).ap()
    q_dram = nc.dram_tensor("q_dram", [TPC, D], BF16).ap()
    kv_stage = nc.dram_tensor("kv_stage", [KV_ROWS, D], BF16).ap()
    kv_gath = nc.dram_tensor("kv_gath", [8 * KV_ROWS, D], BF16,
                             addr_space="Shared").ap()
    s_stage = nc.dram_tensor("s_stage", [GM + 1, D], F32).ap()
    s_gath = nc.dram_tensor("s_gath", [8 * (GM + 1), D], F32,
                            addr_space="Shared").ap()
    kbuf1 = nc.dram_tensor("kbuf1", [TPC, D], BF16).ap()
    vbuf1 = nc.dram_tensor("vbuf1", [TPC, D], BF16).ap()
    w12qc = nc.dram_tensor("w12qc", [NHC * 2 * D, 128], BF16).ap()
    w3qc = nc.dram_tensor("w3qc", [NDC * FH, 128], BF16).ap()

    with ExitStack() as ctx:
        E = ctx.enter_context
        tr = Prog(nc, ctx)

        g = {}
        g["xb"] = [E(nc.sbuf_tensor(f"x{i}", [128, TPC], BF16)) for i in range(NDC)]
        g["ln"] = [E(nc.sbuf_tensor(f"ln{i}", [128, TPC + HALO], BF16))
                   for i in range(NDC)]
        for nm, shp, dt in (
            ("ones_bf", [128, 1], BF16), ("ones_f", [128, 1], F32),
            ("onesr_bf", [1, 128], BF16), ("onesr_f", [1, 128], F32),
            ("row_m", [1, TPC + HALO], F32),
            ("row_r", [1, TPC + HALO], F32),
            ("convb_s", [128, NDC], F32), ("resw_s", [128, NDC], F32),
            ("convw_s", [128, NDC * 4], F32), ("memgb_s", [GM, 2], F32),
            ("gpart", [1, 16], F32), ("thr_s", [128, 3], F32),
            ("nthr_s", [128, 3], F32),
            ("moff_sb", [128, NSLOT], F32), ("smask_sb", [GM, 8], F32),
            ("idxk_sb", [128, 16], I32), ("idxv_sb", [128, 16], I32),
            ("bout_s", [128, NDC], F32),
        ):
            g[nm] = E(nc.sbuf_tensor(nm, shp, dt))

        _p0(nc, tr, g, ein)
        tr.flush()
        if _DBG_STOP[0] == 1:
            _dump_x(nc, tr, g, y)
            return _finish(nc, tr)
        with ExitStack() as gla:
            Eg = gla.enter_context
            for nm, shp, dt in (
                ("r_bf", [GM, TPC], BF16), ("kh_bf", [GM, TPC], BF16),
                ("d_f", [GM, TPC], F32), ("S_f", [GM, D], F32),
                ("S_bf", [GM, D], BF16), ("eD_f", [GM, NCH], F32),
                ("ident_bf", [128, 128], BF16), ("glam_f", [128, 128], F32),
                ("ones64", [GM, 128], F32),
            ):
                g[nm] = Eg(nc.sbuf_tensor(nm, shp, dt))
            tr.gx(lambda e: e.memset(g["ones64"][:], 1.0), w=[tr.buf(g["ones64"])])
            tr.dma("g", lambda e: e.dma_start(out=g["ident_bf"][:], in_=ein["ident"]),
                   w=[tr.buf(g["ident_bf"])])
            tr.dma("sp", lambda e: e.dma_start(out=g["glam_f"][:], in_=ein["glamask"]),
                   w=[tr.buf(g["glam_f"])])
            _p1(nc, tr, g, ein)
            tr.flush()
            _p2(nc, tr, g, ein, u_dram)
            tr.flush()
            _p3(nc, tr, g, ein, u_dram, s_stage, s_gath)
            tr.flush()
            if _DBG_STOP[0] not in (0, 2, 3, 4, 5):
                pass
            _p4(nc, tr, g, ein, s_gath, w12qc, w3qc)
            tr.flush()
        if _DBG_STOP[0] == 2:
            _dump_x(nc, tr, g, y)
            return _finish(nc, tr)
        _p4q(nc, tr, g, ein, w12qc, w3qc)
        tr.flush()
        if _DBG_STOP[0] == 6:
            tr.dma("g", lambda e: e.dma_start(
                out=y[0:128, :], in_=w12qc[0:D, :].rearrange("(i p) c -> p i c", p=128)),
                r=[tr.buf(w12qc)], w=[tr.buf(y)])
            tr.dma("g", lambda e: e.dma_start(
                out=y[128:256, :], in_=w3qc[0:D, :].rearrange("(i p) c -> p i c", p=128)),
                r=[tr.buf(w3qc)], w=[tr.buf(y)])
            tr.dma("g", lambda e: e.dma_start(
                out=y[256:384, 0:3], in_=g["thr_s"][0:128, 0:3]),
                r=[tr.buf(g["thr_s"])], w=[tr.buf(y)])
            tr.flush()
            return _finish(nc, tr)
        _ln_x(nc, tr, g)
        tr.flush()
        if _DBG_STOP[0] == 5:
            bb = tr.buf
            for dc in range(NDC):
                tr.dma("g", lambda e, dc=dc: e.dma_start(
                    out=y[dc * 128:(dc + 1) * 128, :], in_=g["ln"][dc][:, 0:TPC]),
                    r=[bb(g["ln"][dc])], w=[bb(y)])
            tr.flush()
            return _finish(nc, tr)
        _p_ffn(nc, tr, g, w12qc, w3qc, y=None)
        tr.flush()
        if _DBG_STOP[0] == 3:
            _dump_x(nc, tr, g, y)
            return _finish(nc, tr)
        _ln_x(nc, tr, g)
        tr.flush()
        _p6(nc, tr, g, ein, q_dram, kv_stage, kv_gath)
        tr.flush()
        _p7(nc, tr, g, ein, kv_stage, kv_gath, kbuf1, vbuf1, q_dram)
        tr.flush()
        if _DBG_STOP[0] == 4:
            _dump_x(nc, tr, g, y)
            return _finish(nc, tr)
        _ln_x(nc, tr, g)
        tr.flush()
        _p_ffn(nc, tr, g, w12qc, w3qc, y=y)
        tr.flush()

        _finish(nc, tr)
    return nc


def _finish(nc, tr):
    with nc.Block() as blk:
        sems, engs = tr.sems, tr.engs

        @blk.sync
        def _(sy):
            sy.wait_ge(sems["sp_d"], 16 * engs["sp"].n_dma)

        @blk.gpsimd
        def _(gq):
            gq.wait_ge(sems["g_d"], 16 * engs["g"].n_dma)
    return nc



def _p0(nc, tr, g, ein):
    bb = tr.buf
    xb, ln = g["xb"], g["ln"]
    with ExitStack() as px:
        E = px.enter_context
        xin = [E(nc.sbuf_tensor(f"p0xin{i}", [128, TPC + HALO], F32)) for i in range(2)]
        sqt = [E(nc.sbuf_tensor(f"p0sq{i}", [128, TPC + HALO], BF16)) for i in range(2)]
        hx = E(nc.sbuf_tensor("p0hx", [128, NDC * HALO], F32))
        hq = E(nc.sbuf_tensor("p0hq", [128, NDC * HALO], F32))
        gtile = [E(nc.sbuf_tensor(f"p0gt{i}", [128, 2048], F32)) for i in range(2)]
        gacc = E(nc.sbuf_tensor("p0gacc", [128, 32], F32))
        gcol = E(nc.sbuf_tensor("p0gcol", [128, 1], F32))
        scr = E(nc.sbuf_tensor("p0scr", [128, 512 + HALO], F32))
        g["row_t"] = E(nc.sbuf_tensor("p0rowt", [1, TPC + HALO], F32))

        seg1 = ExitStack()
        E1 = seg1.enter_context
        psx = [E1(nc.psum_tensor(f"p0psx{i}", [1, 512], F32)) for i in range(4)]
        psq = [E1(nc.psum_tensor(f"p0psq{i}", [1, 512], F32)) for i in range(4)]

        tr.gx(lambda e: e.memset(g["ones_bf"][:], 1.0), w=[bb(g["ones_bf"])])
        tr.gx(lambda e: e.memset(g["ones_f"][:], 1.0), w=[bb(g["ones_f"])])
        tr.gx(lambda e: e.memset(g["onesr_bf"][:], 1.0), w=[bb(g["onesr_bf"])])
        tr.gx(lambda e: e.memset(g["onesr_f"][:], 1.0), w=[bb(g["onesr_f"])])
        for nm, src in (("convb_s", "convb"), ("resw_s", "resw"),
                        ("convw_s", "convw"), ("memgb_s", "memgb2"),
                        ("moff_sb", "moff"), ("smask_sb", "smask"),
                        ("idxk_sb", "idxk"), ("idxv_sb", "idxv"),
                        ("bout_s", "bout")):
            tr.dma("sp", lambda e, nm=nm, src=src: e.dma_start(
                out=g[nm][:], in_=ein[src]), w=[bb(g[nm])])

        for dc in range(NDC):
            xt = xin[dc % 2]
            sq = sqt[dc % 2]
            tr.dma("sp", lambda e, dc=dc, xt=xt: e.dma_start(
                out=xt[:], in_=ein["xT"][dc * 128:(dc + 1) * 128, :]),
                w=[bb(xt)])
            tr.dve(lambda e, dc=dc, xt=xt: e.tensor_copy(ln[dc][:], xt[:]),
                   r=[bb(xt)], w=[bb(ln[dc])])
            tr.dve(lambda e, dc=dc: e.tensor_copy(
                xb[dc][:], ln[dc][:, HALO:]), r=[bb(ln[dc])], w=[bb(xb[dc])])
            tr.act(lambda e, xt=xt, sq=sq: e.activation(
                sq[:], xt[:], AF.Square), r=[bb(xt)], w=[bb(sq)])
            for tg in range(4):
                c0 = HALO + tg * 512
                tr.pe(lambda e, dc=dc, tg=tg, c0=c0: e.matmul(
                    psx[tg][:], g["ones_bf"][:], ln[dc][:, c0:c0 + 512],
                    start=(dc == 0), stop=(dc == NDC - 1)),
                    r=[bb(ln[dc]), bb(g["ones_bf"])], w=[bb(psx[tg])])
                tr.pe(lambda e, dc=dc, tg=tg, c0=c0, sq=sq: e.matmul(
                    psq[tg][:], g["ones_bf"][:], sq[:, c0:c0 + 512],
                    start=(dc == 0), stop=(dc == NDC - 1)),
                    r=[bb(sq), bb(g["ones_bf"])], w=[bb(psq[tg])])
        for tg in range(4):
            tr.dve(lambda e, tg=tg: e.tensor_copy(
                g["row_m"][:, HALO + tg * 512:HALO + (tg + 1) * 512],
                psx[tg][:]), r=[bb(psx[tg])], w=[bb(g["row_m"])])
            tr.dve(lambda e, tg=tg: e.tensor_copy(
                g["row_r"][:, HALO + tg * 512:HALO + (tg + 1) * 512],
                psq[tg][:]), r=[bb(psq[tg])], w=[bb(g["row_r"])])
        seg1.close()
        tr.flush()

        psh = [E(nc.psum_tensor(f"p0psh{i}", [1, HALO], F32)) for i in range(2)]
        psb = [E(nc.psum_tensor(f"p0psb{i}", [128, 512], F32)) for i in range(2)]
        psg = E(nc.psum_tensor("p0psg", [1, 1], F32))

        for dc in range(NDC):
            tr.dma("sp", lambda e, dc=dc: e.dma_start(
                out=hx[:, dc * HALO:(dc + 1) * HALO],
                in_=ein["xT"][dc * 128:(dc + 1) * 128, 0:HALO]), w=[bb(hx)])
        tr.act(lambda e: e.activation(hq[:], hx[:], AF.Square),
               r=[bb(hx)], w=[bb(hq)])
        for dc in range(NDC):
            tr.pe(lambda e, dc=dc: e.matmul(
                psh[0][:], g["ones_f"][:], hx[:, dc * HALO:(dc + 1) * HALO],
                start=(dc == 0), stop=(dc == NDC - 1)),
                r=[bb(hx), bb(g["ones_f"])], w=[bb(psh[0])])
            tr.pe(lambda e, dc=dc: e.matmul(
                psh[1][:], g["ones_f"][:], hq[:, dc * HALO:(dc + 1) * HALO],
                start=(dc == 0), stop=(dc == NDC - 1)),
                r=[bb(hq), bb(g["ones_f"])], w=[bb(psh[1])])
        tr.act(lambda e: e.copy(g["row_m"][:, 0:HALO], psh[0][:]),
               r=[bb(psh[0])], w=[bb(g["row_m"])])
        tr.act(lambda e: e.copy(g["row_r"][:, 0:HALO], psh[1][:]),
               r=[bb(psh[1])], w=[bb(g["row_r"])])

        _stats_pipeline(nc, tr, g, n=TPC + HALO)

        groups = [(0, HALO)] + [(HALO + t * 512, 512) for t in range(4)]
        for (c0, n) in groups:
            tr.pe(lambda e, c0=c0, n=n: e.matmul(
                psb[0][:, 0:n], g["onesr_f"][:], g["row_m"][0:1, c0:c0 + n],
                start=True, stop=True),
                r=[bb(g["row_m"]), bb(g["onesr_f"])], w=[bb(psb[0])])
            tr.pe(lambda e, c0=c0, n=n: e.matmul(
                psb[1][:, 0:n], g["onesr_f"][:], g["row_r"][0:1, c0:c0 + n],
                start=True, stop=True),
                r=[bb(g["row_r"]), bb(g["onesr_f"])], w=[bb(psb[1])])
            for dc in range(NDC):
                tr.dve(lambda e, dc=dc, c0=c0, n=n: e.tensor_tensor(
                    scr[:, 0:n], ln[dc][:, c0:c0 + n], psb[0][:, 0:n],
                    ALU.subtract), r=[bb(ln[dc]), bb(psb[0])], w=[bb(scr)])
                if n < 128:
                    tr.act(lambda e, n=n: e.copy(scr[:, 512:512 + n], scr[:, 0:n]),
                           r=[bb(scr)], w=[bb(scr)])
                    tr.dve(lambda e, dc=dc, c0=c0, n=n: e.tensor_tensor(
                        ln[dc][:, c0:c0 + n], scr[:, 512:512 + n], psb[1][:, 0:n],
                        ALU.mult), r=[bb(scr), bb(psb[1])], w=[bb(ln[dc])])
                else:
                    tr.dve(lambda e, dc=dc, c0=c0, n=n: e.tensor_tensor(
                        ln[dc][:, c0:c0 + n], scr[:, 0:n], psb[1][:, 0:n],
                        ALU.mult), r=[bb(scr), bb(psb[1])], w=[bb(ln[dc])])

        tr.gx(lambda e: e.memset(gacc[:], 0.0), w=[bb(gacc)])
        col = 0
        spans = []
        for nm, rows, cols in (("g1", 256, FH), ("g2", 256, FH), ("g3", 704, D)):
            start_col = col
            r0 = 0
            while r0 < rows:
                pr = min(128, rows - r0)
                c0 = 0
                while c0 < cols:
                    pc = min(2048, cols - c0)
                    gt = gtile[col % 2]
                    tr.dma("sp", lambda e, nm=nm, r0=r0, pr=pr, c0=c0, pc=pc, gt=gt:
                           e.dma_start(out=gt[0:pr, 0:pc],
                                       in_=ein[nm][r0:r0 + pr, c0:c0 + pc]),
                           w=[bb(gt)])
                    tr.dve(lambda e, gt=gt, pr=pr, pc=pc, col=col: e.tensor_reduce(
                        gacc[0:pr, col:col + 1], gt[0:pr, 0:pc],
                        mybir.AxisListType.X, ALU.add, apply_absolute_value=True),
                        r=[bb(gt)], w=[bb(gacc)])
                    col += 1
                    c0 += pc
                r0 += pr
            spans.append((start_col, col))
        for j, (a, bcol) in enumerate(spans):
            tr.dve(lambda e, a=a, bcol=bcol: e.tensor_reduce(
                gcol[:, 0:1], gacc[:, a:bcol], mybir.AxisListType.X, ALU.add),
                r=[bb(gacc)], w=[bb(gcol)])
            tr.pe(lambda e: e.matmul(psg[:], g["ones_f"][:], gcol[:, 0:1],
                                     start=True, stop=True),
                  r=[bb(gcol), bb(g["ones_f"])], w=[bb(psg)])
            tr.dve(lambda e, j=j: e.tensor_copy(g["gpart"][:, j:j + 1], psg[:]),
                   r=[bb(psg)], w=[bb(g["gpart"])])


def _stats_pipeline(nc, tr, g, n):
    bb = tr.buf
    m, r, t = g["row_m"], g["row_r"], g["row_t"]
    tr.dve(lambda e: e.tensor_scalar_mul(m[:, 0:n], m[:, 0:n], 1.0 / D),
           r=[], w=[bb(m)])
    tr.dve(lambda e: e.tensor_tensor(t[:, 0:n], m[:, 0:n], m[:, 0:n], ALU.mult),
           r=[bb(m)], w=[bb(t)])
    tr.dve(lambda e: e.scalar_tensor_tensor(
        r[:, 0:n], r[:, 0:n], 1.0 / D, t[:, 0:n], ALU.mult, ALU.subtract),
        r=[bb(t)], w=[bb(r)])
    tr.dve(lambda e: e.tensor_scalar_add(r[:, 0:n], r[:, 0:n], LN_EPS),
           r=[], w=[bb(r)])
    tr.act(lambda e: e.activation(t[:, 0:n], r[:, 0:n], AF.Ln),
           r=[bb(r)], w=[bb(t)])
    tr.dve(lambda e: e.tensor_scalar_mul(t[:, 0:n], t[:, 0:n], -0.5),
           r=[], w=[bb(t)])
    tr.act(lambda e: e.activation(r[:, 0:n], t[:, 0:n], AF.Exp),
           r=[bb(t)], w=[bb(r)])


def _p1(nc, tr, g, ein):
    bb = tr.buf
    xb, ln = g["xb"], g["ln"]
    with ExitStack() as px:
        E = px.enter_context
        inpj = E(nc.sbuf_tensor("p1inpj", [128, NDC * 128], BF16))
        memg = E(nc.sbuf_tensor("p1memg", [128, NDC * 128], BF16))
        shf = [E(nc.sbuf_tensor(f"p1s{i}", [128, 512], BF16)) for i in range(2)]
        acc = E(nc.sbuf_tensor("p1acc", [128, 512], F32))
        g_f = E(nc.sbuf_tensor("p1gf", [GM, 512], F32))
        sel = E(nc.sbuf_tensor("p1sel", [GM, 512], F32))
        tau = E(nc.sbuf_tensor("p1tau", [GM, 512], F32))
        aa = E(nc.sbuf_tensor("p1aa", [GM, 512], F32))
        mm_ = E(nc.sbuf_tensor("p1mm", [GM, 512], F32))
        kpre = E(nc.sbuf_tensor("p1kpre", [GM, 512], F32))
        k2 = E(nc.sbuf_tensor("p1k2", [GM, 512], F32))
        lns = aa
        itg = mm_
        kq = k2
        nr = E(nc.sbuf_tensor("p1nr", [1, 512], F32))
        pr_ = E(nc.psum_tensor("p1pr", [GM, 512], F32))
        pk = E(nc.psum_tensor("p1pk", [GM, 512], F32))
        pse = E(nc.psum_tensor("p1pse", [GM, 512], F32))
        pta = E(nc.psum_tensor("p1pta", [GM, 512], F32))
        pn2 = E(nc.psum_tensor("p1pn2", [1, 512], F32))
        pnb = E(nc.psum_tensor("p1pnb", [GM, 512], F32))

        tr.dma("g", lambda e: e.dma_start(
            out=inpj[:], in_=ein["inprojT"].rearrange("(i p) c -> p i c", p=128)),
            w=[bb(inpj)])
        tr.dma("g", lambda e: e.dma_start(
            out=memg[:], in_=ein["memgT"].rearrange("(i p) c -> p i c", p=128)),
            w=[bb(memg)])

        for tg in range(NTG):
            t0 = tg * 512
            for dc in range(NDC):
                sh = shf[dc % 2]
                tr.dve(lambda e, dc=dc, t0=t0: e.tensor_scalar_mul(
                    acc[:], ln[dc][:, t0:t0 + 512],
                    g["convw_s"][:, dc * 4:dc * 4 + 1]),
                    r=[bb(ln[dc]), bb(g["convw_s"])], w=[bb(acc)])
                for i in range(1, 4):
                    tr.dve(lambda e, dc=dc, t0=t0, i=i: e.scalar_tensor_tensor(
                        acc[:], ln[dc][:, t0 + i:t0 + i + 512],
                        g["convw_s"][:, dc * 4 + i:dc * 4 + i + 1], acc[:],
                        ALU.mult, ALU.add),
                        r=[bb(ln[dc]), bb(g["convw_s"])], w=[bb(acc)])
                tr.act(lambda e, dc=dc, sh=sh: e.activation(
                    sh[:], acc[:], AF.Silu, bias=g["convb_s"][:, dc:dc + 1]),
                    r=[bb(acc), bb(g["convb_s"])], w=[bb(sh)])
                tr.dve(lambda e, dc=dc, t0=t0, sh=sh: e.scalar_tensor_tensor(
                    xb[dc][:, t0:t0 + 512], sh[:],
                    g["resw_s"][:, dc:dc + 1], xb[dc][:, t0:t0 + 512],
                    ALU.mult, ALU.add),
                    r=[bb(sh), bb(g["resw_s"])], w=[bb(xb[dc])])
                st, sp_ = (dc == 0), (dc == NDC - 1)
                tr.pe(lambda e, dc=dc, sh=sh, st=st, sp_=sp_: e.matmul(
                    pr_[:], inpj[:, dc * 128:dc * 128 + GM], sh[:],
                    start=st, stop=sp_), r=[bb(inpj), bb(sh)], w=[bb(pr_)])
                tr.pe(lambda e, dc=dc, sh=sh, st=st, sp_=sp_: e.matmul(
                    pk[:], inpj[:, dc * 128 + GM:dc * 128 + 128], sh[:],
                    start=st, stop=sp_), r=[bb(inpj), bb(sh)], w=[bb(pk)])
                tr.pe(lambda e, dc=dc, sh=sh, st=st, sp_=sp_: e.matmul(
                    pse[:], memg[:, dc * 128:dc * 128 + GM], sh[:],
                    start=st, stop=sp_), r=[bb(memg), bb(sh)], w=[bb(pse)])
                tr.pe(lambda e, dc=dc, sh=sh, st=st, sp_=sp_: e.matmul(
                    pta[:], memg[:, dc * 128 + GM:dc * 128 + 128], sh[:],
                    start=st, stop=sp_), r=[bb(memg), bb(sh)], w=[bb(pta)])
            tr.dve(lambda e, t0=t0: e.tensor_copy(
                g["r_bf"][:, t0:t0 + 512], pr_[:]),
                r=[bb(pr_)], w=[bb(g["r_bf"])])
            tr.dve(lambda e: e.tensor_copy(kpre[:], pk[:]),
                   r=[bb(pk)], w=[bb(kpre)])
            tr.act(lambda e: e.activation(tau[:], pta[:], AF.Sigmoid,
                                          bias=g["memgb_s"][:, 1:2]),
                   r=[bb(pta), bb(g["memgb_s"])], w=[bb(tau)])
            tr.dve(lambda e: e.tensor_scalar_add(mm_[:], pse[:],
                                                 g["memgb_s"][:, 0:1]),
                   r=[bb(pse), bb(g["memgb_s"])], w=[bb(mm_)])
            tr.act(lambda e: e.activation(aa[:], mm_[:], AF.Abs),
                   r=[bb(mm_)], w=[bb(aa)])
            tr.act(lambda e: e.activation(aa[:], aa[:], AF.Exp, scale=-1.0),
                   r=[], w=[bb(aa)])
            tr.act(lambda e: e.activation(aa[:], aa[:], AF.Ln, bias=1.0),
                   r=[], w=[bb(aa)])
            tr.dve(lambda e: e.tensor_scalar_max(mm_[:], mm_[:], 0.0),
                   r=[], w=[bb(mm_)])
            tr.dve(lambda e: e.tensor_add(sel[:], mm_[:], aa[:]),
                   r=[bb(mm_), bb(aa)], w=[bb(sel)])
            tr.dve(lambda e: e.tensor_scalar_max(sel[:], sel[:], 1e-38),
                   r=[], w=[bb(sel)])
            tr.act(lambda e: e.activation(lns[:], sel[:], AF.Ln),
                   r=[bb(sel)], w=[bb(lns)])
            tr.dve(lambda e: e.tensor_mul(lns[:], tau[:], lns[:]),
                   r=[bb(tau)], w=[bb(lns)])
            tr.act(lambda e: e.activation(itg[:], lns[:], AF.Exp),
                   r=[bb(lns)], w=[bb(itg)])
            tr.dve(lambda e: e.scalar_tensor_tensor(
                g_f[:], sel[:], -1.0, tau[:],
                ALU.mult, ALU.mult),
                r=[bb(sel), bb(tau)], w=[bb(g_f)])
            for cc_ in range(4):
                c = tg * 4 + cc_
                tr.dve(lambda e, c=c, cc_=cc_: e.tensor_tensor_scan(
                    g["d_f"][:, c * CH:(c + 1) * CH], g["ones64"][:, 0:CH],
                    g_f[:, cc_ * CH:(cc_ + 1) * CH], 0.0, ALU.mult, ALU.add),
                    r=[bb(g["ones64"]), bb(g_f)], w=[bb(g["d_f"])])
            tr.act(lambda e: e.activation(k2[:], kpre[:], AF.Square),
                   r=[bb(kpre)], w=[bb(k2)])
            tr.pe(lambda e: e.matmul(pn2[:], g["ones_f"][0:GM, 0:1], k2[:],
                                     start=True, stop=True),
                  r=[bb(k2), bb(g["ones_f"])], w=[bb(pn2)])
            tr.act(lambda e: e.activation(nr[:], pn2[:], AF.Sqrt),
                   r=[bb(pn2)], w=[bb(nr)])
            tr.dve(lambda e: e.tensor_scalar_max(nr[:], nr[:], 1e-12),
                   r=[], w=[bb(nr)])
            tr.dve(lambda e: e.reciprocal(nr[:], nr[:]), r=[], w=[bb(nr)])
            tr.pe(lambda e: e.matmul(pnb[:], g["onesr_f"][0:1, 0:GM], nr[:],
                                     start=True, stop=True),
                  r=[bb(nr), bb(g["onesr_f"])], w=[bb(pnb)])
            tr.dve(lambda e: e.tensor_tensor(kq[:], kpre[:], pnb[:], ALU.mult),
                   r=[bb(kpre), bb(pnb)], w=[bb(kq)])
            tr.dve(lambda e, t0=t0: e.tensor_tensor(
                g["kh_bf"][:, t0:t0 + 512], kq[:], itg[:], ALU.mult),
                r=[bb(kq), bb(itg)], w=[bb(g["kh_bf"])])



def _p2(nc, tr, g, ein, u_dram):
    bb = tr.buf
    ln = g["ln"]
    with ExitStack() as px:
        E = px.enter_context
        chw0_sb = E(nc.sbuf_tensor("p2w0", [128, NDC * 128], BF16))
        chw1_sb = E(nc.sbuf_tensor("p2w1", [128, D], BF16))
        chb1_sb = E(nc.sbuf_tensor("p2b1", [1, D], BF16))
        h0 = E(nc.sbuf_tensor("p2h0", [128, TPC], BF16))
        lntm = E(nc.sbuf_tensor("p2lntm", [128, D], BF16))
        chs = E(nc.sbuf_tensor("p2chs", [128, 512], BF16))
        ut = E(nc.sbuf_tensor("p2ut", [128, D], BF16))
        ph = [E(nc.psum_tensor(f"p2ph{i}", [128, 512], F32)) for i in range(2)]
        ptr = [E(nc.psum_tensor(f"p2ptr{i}", [128, 128], BF16)) for i in range(2)]
        pch = [E(nc.psum_tensor(f"p2pch{i}", [128, 512], F32)) for i in range(2)]

        tr.dma("g", lambda e: e.dma_start(
            out=chw0_sb[:], in_=ein["chw0T"].rearrange("(i p) c -> p i c", p=128)),
            w=[bb(chw0_sb)])
        tr.dma("g", lambda e: e.dma_start(out=chw1_sb[:], in_=ein["chw1T"]),
               w=[bb(chw1_sb)])
        tr.dma("g", lambda e: e.dma_start(out=chb1_sb[:], in_=ein["chb1"]),
               w=[bb(chb1_sb)])

        for tg in range(NTG):
            p = ph[tg % 2]
            for dc in range(NDC):
                tr.pe(lambda e, dc=dc, tg=tg, p=p: e.matmul(
                    p[:], chw0_sb[:, dc * 128:(dc + 1) * 128],
                    ln[dc][:, HALO + tg * 512:HALO + (tg + 1) * 512],
                    start=(dc == 0), stop=(dc == NDC - 1)),
                    r=[bb(chw0_sb), bb(ln[dc])], w=[bb(p)])
            tr.dve(lambda e, tg=tg, p=p: e.tensor_copy(
                h0[:, tg * 512:(tg + 1) * 512], p[:]), r=[bb(p)], w=[bb(h0)])

        for tc in range(NCH):
            c0 = HALO + tc * 128
            for dc in range(NDC):
                pt = ptr[dc % 2]
                tr.pe(lambda e, dc=dc, c0=c0, pt=pt: e.transpose(
                    pt[:], ln[dc][:, c0:c0 + 128], g["ident_bf"][:]),
                    r=[bb(ln[dc]), bb(g["ident_bf"])], w=[bb(pt)])
                tr.dve(lambda e, dc=dc, pt=pt: e.tensor_copy(
                    lntm[:, dc * 128:(dc + 1) * 128], pt[:]),
                    r=[bb(pt)], w=[bb(lntm)])
            for dg in range(4):
                p = pch[dg % 2]
                tr.pe(lambda e, tc=tc, dg=dg, p=p: e.matmul(
                    p[:], h0[:, tc * 128:(tc + 1) * 128],
                    chw1_sb[:, dg * 512:(dg + 1) * 512],
                    start=True, stop=False),
                    r=[bb(h0), bb(chw1_sb)], w=[bb(p)])
                tr.pe(lambda e, dg=dg, p=p: e.matmul(
                    p[:], g["onesr_bf"][:], chb1_sb[0:1, dg * 512:(dg + 1) * 512],
                    start=False, stop=True),
                    r=[bb(g["onesr_bf"]), bb(chb1_sb)], w=[bb(p)])
                tr.act(lambda e, p=p: e.activation(chs[:], p[:], AF.Sigmoid),
                       r=[bb(p)], w=[bb(chs)])
                tr.dve(lambda e, dg=dg: e.tensor_tensor(
                    ut[:, dg * 512:(dg + 1) * 512], chs[:],
                    lntm[:, dg * 512:(dg + 1) * 512], ALU.mult),
                    r=[bb(chs), bb(lntm)], w=[bb(ut)])
            tr.dma("sp", lambda e, tc=tc: e.dma_start(
                out=u_dram[tc * 128:(tc + 1) * 128, :], in_=ut[:]),
                r=[bb(ut)], w=[bb(u_dram)])



def _p3(nc, tr, g, ein, u_dram, s_stage, s_gath):
    bb = tr.buf
    xb = g["xb"]
    with ExitStack() as px:
        E = px.enter_context
        Uc = [E(nc.sbuf_tensor(f"p3u{i}", [128, D], BF16)) for i in range(2)]
        ndB = [E(nc.sbuf_tensor(f"p3nd{i}", [GM, CH], F32)) for i in range(2)]
        e0B = [E(nc.sbuf_tensor(f"p3e0{i}", [GM, CH], BF16)) for i in range(2)]
        q0B = [E(nc.sbuf_tensor(f"p3q0{i}", [GM, CH], BF16)) for i in range(2)]
        eqB = [E(nc.sbuf_tensor(f"p3eq{i}", [GM, 32], BF16)) for i in range(2)]
        ekB = [E(nc.sbuf_tensor(f"p3ek{i}", [GM, CH], BF16)) for i in range(2)]
        qjB = [E(nc.sbuf_tensor(f"p3qj{i}", [GM, 32], BF16)) for i in range(2)]
        kjB = [E(nc.sbuf_tensor(f"p3kj{i}", [GM, CH], BF16)) for i in range(2)]
        esB = [E(nc.sbuf_tensor(f"p3es{i}", [GM, CH], BF16)) for i in range(2)]
        kSB = [E(nc.sbuf_tensor(f"p3kS{i}", [GM, CH], BF16)) for i in range(2)]
        kTB = [E(nc.sbuf_tensor(f"p3kT{i}", [128, GM], BF16)) for i in range(2)]
        PtB = [E(nc.sbuf_tensor(f"p3Pt{i}", [128, CH], BF16)) for i in range(2)]
        pPB = [E(nc.psum_tensor(f"p3pP{i}", [128, CH], F32)) for i in range(2)]
        pT = E(nc.psum_tensor("p3pT", [128, GM], BF16))
        pI = E(nc.psum_tensor("p3pI", [GM, 512], F32))
        po = [E(nc.psum_tensor(f"p3po{i}", [128, CH], F32)) for i in range(4)]

        tr.gx(lambda e: e.memset(g["S_f"][:], 0.0), w=[bb(g["S_f"])])
        dbuf = bb(g["d_f"])

        for c in range(NCH):
            u = Uc[c % 2]
            nd, e0, q0 = ndB[c % 2], e0B[c % 2], q0B[c % 2]
            eq, ek, qj, kj = eqB[c % 2], ekB[c % 2], qjB[c % 2], kjB[c % 2]
            es, kS, kT, Pt = esB[c % 2], kSB[c % 2], kTB[c % 2], PtB[c % 2]
            pP = pPB[c % 2]
            tr.dma("sp" if c % 2 == 0 else "g", lambda e, c=c, u=u: e.dma_start(
                out=u[:], in_=u_dram[c * 128:(c + 1) * 128, :]),
                r=[bb(u_dram)], w=[bb(u)])
            dsl = g["d_f"][:, c * CH:(c + 1) * CH]
            tr.dve(lambda e, dsl=dsl, nd=nd: e.tensor_scalar_mul(nd[:], dsl, -1.0),
                   r=[dbuf], w=[bb(nd)])
            tr.act(lambda e, c=c, dsl=dsl: e.activation(
                g["eD_f"][:, c:c + 1], dsl[:, CH - 1:CH], AF.Exp),
                r=[dbuf], w=[bb(g["eD_f"])])
            tr.dve(lambda e: e.tensor_copy(g["S_bf"][:], g["S_f"][:]),
                   r=[bb(g["S_f"])], w=[bb(g["S_bf"])])
            tr.act(lambda e, dsl=dsl, e0=e0: e.activation(e0[:], dsl, AF.Exp),
                   r=[dbuf], w=[bb(e0)])
            tr.dve(lambda e, c=c, q0=q0, e0=e0: e.tensor_tensor(
                q0[:], g["r_bf"][:, c * CH:(c + 1) * CH], e0[:], ALU.mult),
                r=[bb(g["r_bf"]), bb(e0)], w=[bb(q0)])
            tr.dve(lambda e, pP=pP: e.memset(pP[:], 0.0), w=[bb(pP)])
            for J in range(4):
                w_ = 32 * J + 32
                mid = 32 * J + 16
                tr.act(lambda e, dsl=dsl, J=J, mid=mid, w_=w_, eq=eq, nd=nd: e.activation(
                    eq[:], dsl[:, 32 * J:w_], AF.Exp,
                    bias=nd[:, mid:mid + 1]),
                    r=[dbuf, bb(nd)], w=[bb(eq)])
                tr.dve(lambda e, c=c, J=J, w_=w_, qj=qj, eq=eq: e.tensor_tensor(
                    qj[:], g["r_bf"][:, c * CH + 32 * J:c * CH + w_],
                    eq[:], ALU.mult), r=[bb(g["r_bf"]), bb(eq)], w=[bb(qj)])
                tr.act(lambda e, dsl=dsl, w_=w_, mid=mid, ek=ek, nd=nd: e.activation(
                    ek[:, 0:w_], nd[:, 0:w_], AF.Exp,
                    bias=dsl[:, mid:mid + 1]),
                    r=[dbuf, bb(nd)], w=[bb(ek)])
                tr.dve(lambda e, c=c, w_=w_, kj=kj, ek=ek: e.tensor_tensor(
                    kj[:, 0:w_], g["kh_bf"][:, c * CH:c * CH + w_],
                    ek[:, 0:w_], ALU.mult),
                    r=[bb(g["kh_bf"]), bb(ek)], w=[bb(kj)])
                tr.pe(lambda e, J=J, w_=w_, pP=pP, kj=kj, qj=qj: e.matmul(
                    pP[0:w_, 32 * J:w_], kj[0:GM, 0:w_], qj[:],
                    start=True, stop=True),
                    r=[bb(kj), bb(qj)], w=[bb(pP)])
            tr.dve(lambda e, Pt=Pt, pP=pP: e.tensor_tensor(Pt[:], pP[:], g["glam_f"][:],
                                             ALU.mult),
                   r=[bb(pP), bb(g["glam_f"])], w=[bb(Pt)])
            for w0 in range(0, NDC, 4):
                dcs = range(w0, min(w0 + 4, NDC))
                for i, dc in enumerate(dcs):
                    tr.pe(lambda e, dc=dc, i=i, u=u, Pt=Pt: e.matmul(
                        po[i][:], u[:, dc * 128:(dc + 1) * 128], Pt[:],
                        start=True, stop=False),
                        r=[bb(u), bb(Pt)], w=[bb(po[i])])
                    tr.pe(lambda e, dc=dc, i=i, q0=q0: e.matmul(
                        po[i][:], g["S_bf"][:, dc * 128:(dc + 1) * 128], q0[:],
                        start=False, stop=True),
                        r=[bb(g["S_bf"]), bb(q0)], w=[bb(po[i])])
                for i, dc in enumerate(dcs):
                    tr.dve(lambda e, dc=dc, i=i, c=c: e.scalar_tensor_tensor(
                        xb[dc][:, c * CH:(c + 1) * CH], po[i][:], SCALE_GLA,
                        xb[dc][:, c * CH:(c + 1) * CH], ALU.mult, ALU.add),
                        r=[bb(po[i])], w=[bb(xb[dc])])
            tr.act(lambda e, dsl=dsl, es=es, nd=nd: e.activation(
                es[:], nd[:], AF.Exp, bias=dsl[:, CH - 1:CH]),
                r=[dbuf, bb(nd)], w=[bb(es)])
            tr.dve(lambda e, c=c, kS=kS, es=es: e.tensor_tensor(
                kS[:], g["kh_bf"][:, c * CH:(c + 1) * CH], es[:], ALU.mult),
                r=[bb(g["kh_bf"]), bb(es)], w=[bb(kS)])
            tr.pe(lambda e, kS=kS: e.transpose(pT[:], kS[:], g["ident_bf"][0:GM, 0:GM]),
                  r=[bb(kS), bb(g["ident_bf"])], w=[bb(pT)])
            tr.dve(lambda e, kT=kT: e.tensor_copy(kT[:], pT[:]), r=[bb(pT)], w=[bb(kT)])
            for dg in range(4):
                tr.pe(lambda e, dg=dg, u=u, kT=kT: e.matmul(
                    pI[:], kT[:, 0:GM], u[:, dg * 512:(dg + 1) * 512],
                    start=True, stop=True),
                    r=[bb(kT), bb(u)], w=[bb(pI)])
                tr.dve(lambda e, c=c, dg=dg: e.scalar_tensor_tensor(
                    g["S_f"][:, dg * 512:(dg + 1) * 512],
                    g["S_f"][:, dg * 512:(dg + 1) * 512],
                    g["eD_f"][:, c:c + 1], pI[:], ALU.mult, ALU.add),
                    r=[bb(g["eD_f"]), bb(pI)], w=[bb(g["S_f"])])

        tr.dma("sp", lambda e: e.dma_start(out=s_stage[0:GM, :], in_=g["S_f"][:]),
               r=[bb(g["S_f"])], w=[bb(s_stage)])
        tr.dma("sp", lambda e: e.dma_start(
            out=s_stage[GM:GM + 1, 0:16], in_=g["gpart"][:]),
            r=[bb(g["gpart"])], w=[bb(s_stage)])
        tr.cc(lambda e: e.collective_compute(
            "AllGather", ALU.bypass,
            replica_groups=[[0, 1, 2, 3, 4, 5, 6, 7]],
            ins=[s_stage.opt()], outs=[s_gath.opt()]),
            r=[bb(s_stage)], w=[bb(s_gath)])


def _p4(nc, tr, g, ein, s_gath, w12qc, w3qc):
    bb = tr.buf
    xb = g["xb"]
    with ExitStack() as px:
        E = px.enter_context
        sg_t = E(nc.sbuf_tensor("p4sg", [GM, D], F32))
        e0 = E(nc.sbuf_tensor("p4e0", [GM, CH], BF16))
        q0 = E(nc.sbuf_tensor("p4q0", [GM, CH], BF16))
        q0a = E(nc.sbuf_tensor("p4q0a", [GM, CH], BF16))
        Acum = E(nc.sbuf_tensor("p4A", [GM, NCH + 1], F32))
        Acs = E(nc.sbuf_tensor("p4Ac", [GM, NCH + 1], F32))
        grows = E(nc.sbuf_tensor("p4grows", [1, 24], F32))
        gsum = E(nc.sbuf_tensor("p4gsum", [1, 12], F32))
        g["_dbg_grows"] = grows
        g["_dbg_gsum"] = gsum
        g["_dbg_sgath"] = s_gath
        po = [E(nc.psum_tensor(f"p4po{i}", [128, CH], F32)) for i in range(6)]
        pb = E(nc.psum_tensor("p4pb", [128, 1], F32))

        tr.gx(lambda e: e.memset(g["S_f"][:], 0.0), w=[bb(g["S_f"])])
        for r in range(8):
            tr.dma("sp", lambda e, r=r: e.dma_start(
                out=sg_t[:], in_=s_gath[r * (GM + 1):r * (GM + 1) + GM, :]),
                r=[bb(s_gath)], w=[bb(sg_t)])
            tr.dve(lambda e, r=r: e.scalar_tensor_tensor(
                g["S_f"][:], sg_t[:], g["smask_sb"][:, r:r + 1], g["S_f"][:],
                ALU.mult, ALU.add),
                r=[bb(sg_t), bb(g["smask_sb"])], w=[bb(g["S_f"])])
        tr.dve(lambda e: e.tensor_copy(g["S_bf"][:], g["S_f"][:]),
               r=[bb(g["S_f"])], w=[bb(g["S_bf"])])
        tr.gx(lambda e: e.memset(Acum[:, 0:NCH + 1], 1.0), w=[bb(Acum)])
        tr.gx(lambda e: e.memset(Acs[:, 0:NCH + 1], 1.0), w=[bb(Acs)])
        for c in range(NCH):
            tr.dve(lambda e, c=c: e.tensor_tensor(
                Acum[:, c + 1:c + 2], Acs[:, c:c + 1], g["eD_f"][:, c:c + 1],
                ALU.mult), r=[bb(g["eD_f"]), bb(Acs)], w=[bb(Acum)])
            tr.act(lambda e, c=c: e.copy(Acs[:, c + 1:c + 2],
                                         Acum[:, c + 1:c + 2]),
                   r=[bb(Acum)], w=[bb(Acs)])

        dbuf = bb(g["d_f"])
        for c in range(NCH):
            dsl = g["d_f"][:, c * CH:(c + 1) * CH]
            tr.act(lambda e, dsl=dsl: e.activation(e0[:], dsl, AF.Exp),
                   r=[dbuf], w=[bb(e0)])
            tr.dve(lambda e, c=c: e.tensor_tensor(
                q0[:], g["r_bf"][:, c * CH:(c + 1) * CH], e0[:], ALU.mult),
                r=[bb(g["r_bf"]), bb(e0)], w=[bb(q0)])
            tr.dve(lambda e, c=c: e.tensor_scalar_mul(
                q0a[:], q0[:], Acs[:, c:c + 1]),
                r=[bb(q0), bb(Acs)], w=[bb(q0a)])
            for w0 in range(0, NDC, 6):
                dcs = range(w0, min(w0 + 6, NDC))
                for i, dc in enumerate(dcs):
                    tr.pe(lambda e, dc=dc, i=i: e.matmul(
                        po[i][:], g["S_bf"][:, dc * 128:(dc + 1) * 128], q0a[:],
                        start=True, stop=True),
                        r=[bb(g["S_bf"]), bb(q0a)], w=[bb(po[i])])
                for i, dc in enumerate(dcs):
                    tr.dve(lambda e, dc=dc, i=i, c=c: e.scalar_tensor_tensor(
                        xb[dc][:, c * CH:(c + 1) * CH], po[i][:], SCALE_GLA,
                        xb[dc][:, c * CH:(c + 1) * CH], ALU.mult, ALU.add),
                        r=[bb(po[i])], w=[bb(xb[dc])])

        for r in range(8):
            tr.dma("sp", lambda e, r=r: e.dma_start(
                out=grows[0:1, r * 3:(r + 1) * 3],
                in_=s_gath[r * (GM + 1) + GM:r * (GM + 1) + GM + 1, 0:3]),
                r=[bb(s_gath)], w=[bb(grows)])
        RW = dict(r=[bb(grows), bb(gsum)], w=[bb(grows), bb(gsum)])
        for p_ in range(4):
            tr.dve(lambda e, p_=p_: e.tensor_tensor(
                grows[0:1, p_ * 3:p_ * 3 + 3], grows[0:1, 2 * p_ * 3:2 * p_ * 3 + 3],
                grows[0:1, (2 * p_ + 1) * 3:(2 * p_ + 1) * 3 + 3], ALU.add),
                **RW)
        tr.act(lambda e: e.copy(gsum[:, 0:12], grows[0:1, 0:12]), **RW)
        for p_ in range(2):
            tr.dve(lambda e, p_=p_: e.tensor_tensor(
                grows[0:1, p_ * 3:p_ * 3 + 3], gsum[0:1, 2 * p_ * 3:2 * p_ * 3 + 3],
                gsum[0:1, (2 * p_ + 1) * 3:(2 * p_ + 1) * 3 + 3], ALU.add),
                **RW)
        tr.act(lambda e: e.copy(gsum[:, 0:6], grows[0:1, 0:6]), **RW)
        tr.dve(lambda e: e.tensor_tensor(
            grows[0:1, 0:3], gsum[0:1, 0:3], gsum[0:1, 3:6], ALU.add), **RW)
        tr.act(lambda e: e.copy(gsum[:, 0:3], grows[0:1, 0:3]), **RW)
        for j in range(3):
            tr.dve(lambda e, j=j: e.tensor_scalar(
                gsum[:, j:j + 1], gsum[:, j:j + 1], 0.5 / (FH * D), 0.5e-8,
                ALU.mult, ALU.add), **RW)
        tr.act(lambda e: e.copy(gsum[:, 4:7], gsum[:, 0:3]), **RW)
        for j in range(3):
            tr.pe(lambda e, j=j: e.matmul(
                pb[:], g["onesr_f"][:], gsum[0:1, 4 + j:5 + j], start=True,
                stop=True), r=[bb(gsum), bb(g["onesr_f"])], w=[bb(pb)])
            tr.dve(lambda e, j=j: e.tensor_copy(g["thr_s"][:, j:j + 1], pb[:]),
                   r=[bb(pb)], w=[bb(g["thr_s"])])
            tr.act(lambda e, j=j: e.mul(
                g["nthr_s"][:, j:j + 1], g["thr_s"][:, j:j + 1], -1.0),
                r=[bb(g["thr_s"])], w=[bb(g["nthr_s"])])



def _p4q(nc, tr, g, ein, w12qc, w3qc):
    bb = tr.buf
    with ExitStack() as px:
        E = px.enter_context
        wf_t = [E(nc.sbuf_tensor(f"p4wf{i}", [128, 2048], F32)) for i in range(3)]
        bl_t = [E(nc.sbuf_tensor(f"p4bl{i}", [128, 2048], BF16)) for i in range(2)]
        wq_t = [E(nc.sbuf_tensor(f"p4wq{i}", [128, 2048], BF16)) for i in range(3)]

        qjobs = []
        for hc in range(NHC):
            qjobs.append((ein["w14"][hc * D:(hc + 1) * D, :],
                          w12qc[hc * 2 * D:hc * 2 * D + D, :], w12qc, 0, 2048))
            qjobs.append((ein["w24"][hc * D:(hc + 1) * D, :],
                          w12qc[hc * 2 * D + D:(hc + 1) * 2 * D, :], w12qc, 1, 2048))
        for dc in range(NDC):
            for (o_, n_) in ((0, 2048), (2048, 2048), (4096, 1536)):
                r0 = dc * FH + o_
                qjobs.append((ein["w34"][r0:r0 + n_, :],
                              w3qc[r0:r0 + n_, :], w3qc, 2, n_))
        for i, (src, dst, track, j, rows) in enumerate(qjobs):
            rows = rows
            wf = wf_t[i % 3]
            bl = bl_t[i % 2]
            wq = wq_t[i % 3]
            nc_ = rows
            em = tr.dve
            tr.dma("sp" if i % 2 == 0 else "g",
                   lambda e, src=src, wf=wf, nc_=nc_: e.dma_start(
                       out=wf[:, 0:nc_],
                       in_=src.rearrange("(i p) c -> p i c", p=128)),
                   w=[bb(wf)])
            em(lambda e, wf=wf, bl=bl, nc_=nc_, j=j: e.tensor_scalar(
                bl[:, 0:nc_], wf[:, 0:nc_], g["nthr_s"][:, j:j + 1], None,
                ALU.is_lt),
                r=[bb(wf), bb(g["nthr_s"])], w=[bb(bl)])
            em(lambda e, wf=wf, bl=bl, wq=wq, nc_=nc_, j=j: e.scalar_tensor_tensor(
                wq[:, 0:nc_], wf[:, 0:nc_], g["thr_s"][:, j:j + 1], bl[:, 0:nc_],
                ALU.is_gt, ALU.subtract),
                r=[bb(wf), bb(bl), bb(g["thr_s"])], w=[bb(wq)])
            tr.dma("g" if i % 2 == 0 else "sp",
                   lambda e, dst=dst, wq=wq, nc_=nc_: e.dma_start(
                       out=dst.rearrange("(i p) c -> p i c", p=128),
                       in_=wq[:, 0:nc_]),
                   r=[bb(wq)], w=[bb(track)])



_LN_N = [0]


def _ln_x(nc, tr, g):
    bb = tr.buf
    xb, ln = g["xb"], g["ln"]
    _LN_N[0] += 1
    u_ = f"_{_LN_N[0]}"
    with ExitStack() as px:
        E = px.enter_context
        sq = [E(nc.sbuf_tensor(f"lnsq{i}{u_}", [128, 512], BF16)) for i in range(2)]
        scr = E(nc.sbuf_tensor("lnscr" + u_, [128, 512], F32))
        g["row_t"] = E(nc.sbuf_tensor("lnrowt" + u_, [1, TPC], F32))
        psx = E(nc.psum_tensor("lnpsx" + u_, [1, 512], F32))
        psq = E(nc.psum_tensor("lnpsq" + u_, [1, 512], F32))
        psb = [E(nc.psum_tensor(f"lnpsb{i}{u_}", [128, 512], F32)) for i in range(2)]

        for tg in range(NTG):
            t0 = tg * 512
            for dc in range(NDC):
                s = sq[dc % 2]
                tr.act(lambda e, dc=dc, t0=t0, s=s: e.activation(
                    s[:], xb[dc][:, t0:t0 + 512], AF.Square),
                    r=[bb(xb[dc])], w=[bb(s)])
                tr.pe(lambda e, dc=dc, t0=t0: e.matmul(
                    psx[:], g["ones_bf"][:], xb[dc][:, t0:t0 + 512],
                    start=(dc == 0), stop=(dc == NDC - 1)),
                    r=[bb(xb[dc]), bb(g["ones_bf"])], w=[bb(psx)])
                tr.pe(lambda e, s=s, dc=dc: e.matmul(
                    psq[:], g["ones_bf"][:], s[:],
                    start=(dc == 0), stop=(dc == NDC - 1)),
                    r=[bb(s), bb(g["ones_bf"])], w=[bb(psq)])
            tr.dve(lambda e, t0=t0: e.tensor_copy(
                g["row_m"][:, t0:t0 + 512], psx[:]),
                r=[bb(psx)], w=[bb(g["row_m"])])
            tr.dve(lambda e, t0=t0: e.tensor_copy(
                g["row_r"][:, t0:t0 + 512], psq[:]),
                r=[bb(psq)], w=[bb(g["row_r"])])
        _stats_pipeline(nc, tr, g, n=TPC)
        for tg in range(NTG):
            t0 = tg * 512
            tr.pe(lambda e, t0=t0: e.matmul(
                psb[0][:], g["onesr_f"][:], g["row_m"][0:1, t0:t0 + 512],
                start=True, stop=True),
                r=[bb(g["row_m"]), bb(g["onesr_f"])], w=[bb(psb[0])])
            tr.pe(lambda e, t0=t0: e.matmul(
                psb[1][:], g["onesr_f"][:], g["row_r"][0:1, t0:t0 + 512],
                start=True, stop=True),
                r=[bb(g["row_r"]), bb(g["onesr_f"])], w=[bb(psb[1])])
            for dc in range(NDC):
                tr.dve(lambda e, dc=dc, t0=t0: e.tensor_tensor(
                    scr[:], xb[dc][:, t0:t0 + 512], psb[0][:], ALU.subtract),
                    r=[bb(xb[dc]), bb(psb[0])], w=[bb(scr)])
                tr.dve(lambda e, dc=dc, t0=t0: e.tensor_tensor(
                    ln[dc][:, t0:t0 + 512], scr[:], psb[1][:], ALU.mult),
                    r=[bb(scr), bb(psb[1])], w=[bb(ln[dc])])



_FFN_N = [0]


def _p_ffn(nc, tr, g, w12qc, w3qc, y):
    bb = tr.buf
    xb, ln = g["xb"], g["ln"]
    _FFN_N[0] += 1
    u_ = f"_{_FFN_N[0]}"
    with ExitStack() as px:
        E = px.enter_context
        hid = [E(nc.sbuf_tensor(f"fh{i}{u_}", [128, 512], BF16)) for i in range(HHC)]
        wq12 = [E(nc.sbuf_tensor(f"fwq12{i}{u_}", [128, 4096], BF16)) for i in range(2)]
        wq3 = [E(nc.sbuf_tensor(f"fwq3{i}{u_}", [128, HHC * 128], BF16))
               for i in range(3)]
        sg = E(nc.sbuf_tensor("fsg" + u_, [128, 512], BF16))
        outf = E(nc.sbuf_tensor("foutf" + u_, [128, 512], F32))
        psA = [E(nc.psum_tensor(f"fpsA{i}{u_}", [128, 512], F32)) for i in range(2)]
        psB = [E(nc.psum_tensor(f"fpsB{i}{u_}", [128, 512], F32)) for i in range(2)]
        ps3 = [E(nc.psum_tensor(f"fps3{i}{u_}", [128, 512], F32)) for i in range(2)]

        for tt in range(NTG):
            t0 = tt * 512
            for hh in range(2):
                for i in range(HHC):
                    hc = hh * HHC + i
                    b12 = wq12[i % 2]
                    b1 = b12[:, 0:2048]
                    b2 = b12[:, 2048:4096]
                    tr.dma("sp" if i % 2 == 0 else "g",
                           lambda e, hc=hc, b12=b12: e.dma_start(
                               out=b12[:], in_=w12qc[hc * 2 * D:(hc + 1) * 2 * D, :]
                               .rearrange("(i p) c -> p i c", p=128)),
                           r=[bb(w12qc)], w=[bb(b12)])
                    pA, pB = psA[i % 2], psB[i % 2]
                    b12b = bb(wq12[i % 2])
                    for ic in range(NDC):
                        tr.pe(lambda e, ic=ic, t0=t0, b1=b1, pA=pA: e.matmul(
                            pA[:], b1[:, ic * 128:(ic + 1) * 128],
                            ln[ic][:, t0:t0 + 512],
                            start=(ic == 0), stop=(ic == NDC - 1)),
                            r=[b12b, bb(ln[ic])], w=[bb(pA)])
                    for ic in range(NDC):
                        tr.pe(lambda e, ic=ic, t0=t0, b2=b2, pB=pB: e.matmul(
                            pB[:], b2[:, ic * 128:(ic + 1) * 128],
                            ln[ic][:, t0:t0 + 512],
                            start=(ic == 0), stop=(ic == NDC - 1)),
                            r=[b12b, bb(ln[ic])], w=[bb(pB)])
                    tr.act(lambda e, pA=pA: e.activation(sg[:], pA[:], AF.Sigmoid),
                           r=[bb(pA)], w=[bb(sg)])
                    tr.dve(lambda e, i=i, pB=pB: e.tensor_tensor(
                        hid[i][:], sg[:], pB[:], ALU.mult),
                        r=[bb(sg), bb(pB)], w=[bb(hid[i])])
                for dc in range(NDC):
                    b3 = wq3[dc % 3]
                    r0 = dc * FH + hh * (HHC * 128)
                    tr.dma("sp" if dc % 2 == 0 else "g",
                           lambda e, r0=r0, b3=b3: e.dma_start(
                               out=b3[:], in_=w3qc[r0:r0 + HHC * 128, :]
                               .rearrange("(i p) c -> p i c", p=128)),
                           r=[bb(w3qc)], w=[bb(b3)])
                    p3 = ps3[dc % 2]
                    for i in range(HHC):
                        tr.pe(lambda e, i=i, b3=b3, p3=p3: e.matmul(
                            p3[:], b3[:, i * 128:(i + 1) * 128], hid[i][:],
                            start=(i == 0), stop=(i == HHC - 1)),
                            r=[bb(b3), bb(hid[i])], w=[bb(p3)])
                    if y is None or hh == 0:
                        tr.dve(lambda e, dc=dc, t0=t0, p3=p3: e.scalar_tensor_tensor(
                            xb[dc][:, t0:t0 + 512], p3[:], 1.0,
                            xb[dc][:, t0:t0 + 512], ALU.mult, ALU.add),
                            r=[bb(p3)], w=[bb(xb[dc])])
                    else:
                        tr.dve(lambda e, dc=dc, t0=t0, p3=p3: e.scalar_tensor_tensor(
                            outf[:], p3[:], 1.0, xb[dc][:, t0:t0 + 512],
                            ALU.mult, ALU.add),
                            r=[bb(p3), bb(xb[dc])], w=[bb(outf)])
                        tr.dma("sp", lambda e, dc=dc, t0=t0: e.dma_start(
                            out=y[dc * 128:(dc + 1) * 128, t0:t0 + 512],
                            in_=outf[:]), r=[bb(outf)], w=[bb(y)])


def _p6(nc, tr, g, ein, q_dram, kv_stage, kv_gath):
    bb = tr.buf
    ln = g["ln"]
    with ExitStack() as px:
        E = px.enter_context
        wt = [E(nc.sbuf_tensor(f"p6w{i}", [128, NDC * 512], BF16)) for i in range(2)]
        cosc = E(nc.sbuf_tensor("p6cos", [128, NCH * 256], BF16))
        sinc = E(nc.sbuf_tensor("p6sin", [128, NCH * 256], BF16))
        brow = [E(nc.sbuf_tensor(f"p6brow{i}", [1, 512], BF16)) for i in range(2)]
        qk = E(nc.sbuf_tensor("p6qk", [128, 512], BF16))
        rq = E(nc.sbuf_tensor("p6rq", [128, 512], BF16))
        tA = E(nc.sbuf_tensor("p6tA", [128, 256], BF16))
        tB = E(nc.sbuf_tensor("p6tB", [128, 256], BF16))
        ksq = E(nc.sbuf_tensor("p6ksq", [128, 512], BF16))
        nscr = E(nc.sbuf_tensor("p6nscr", [128, 1], F32))
        nstore = E(nc.sbuf_tensor("p6nst", [128, 512], F32))
        nsum = E(nc.sbuf_tensor("p6nsum", [128, 32], F32))
        ntr = E(nc.sbuf_tensor("p6ntr", [32, 1], F32))
        nflat = E(nc.sbuf_tensor("p6nflat", [1, 32], F32))
        ex2 = E(nc.sbuf_tensor("p6ex2", [1, 2], F32))
        ps = [E(nc.psum_tensor(f"p6ps{i}", [128, 512], F32)) for i in range(2)]
        pnt = E(nc.psum_tensor("p6pnt", [32, 128], F32))

        tr.dma("g", lambda e: e.dma_start(
            out=cosc[:], in_=ein["cosP"].rearrange("(tc p) c -> p tc c", p=128)),
            w=[bb(cosc)])
        tr.dma("g", lambda e: e.dma_start(
            out=sinc[:], in_=ein["sinP"].rearrange("(tc p) c -> p tc c", p=128)),
            w=[bb(sinc)])
        tr.gx(lambda e: e.memset(nsum[:], 0.0), w=[bb(nsum)])
        tr.gx(lambda e: e.memset(nstore[:], 0.0), w=[bb(nstore)])

        for ng in range(12):
            w_ = wt[ng % 2]
            br = brow[ng % 2]
            if ng < 8:
                src = ein["wqkT"]
                co = ng * 512
                bsrc = ein["bqkrow"][0:1, ng * 512:(ng + 1) * 512]
            else:
                src = ein["wvT"]
                co = (ng - 8) * 512
                bsrc = ein["bvrow"][0:1, co:co + 512]
            tr.dma("g", lambda e, bsrc=bsrc, br=br: e.dma_start(
                out=br[:], in_=bsrc), w=[bb(br)])
            for ic in range(NDC):
                tr.dma("g", lambda e, ic=ic, co=co, src=src, w_=w_: e.dma_start(
                    out=w_[:, ic * 512:(ic + 1) * 512],
                    in_=src[ic * 128:(ic + 1) * 128, co:co + 512]), w=[bb(w_)])
            for tc in range(NCH):
                p = ps[tc % 2]
                for ic in range(NDC):
                    tr.pe(lambda e, ic=ic, tc=tc, p=p, w_=w_: e.matmul(
                        p[:], ln[ic][:, tc * 128:(tc + 1) * 128],
                        w_[:, ic * 512:(ic + 1) * 512],
                        start=(ic == 0), stop=False),
                        r=[bb(ln[ic]), bb(w_)], w=[bb(p)])
                tr.pe(lambda e, p=p, br=br: e.matmul(
                    p[:], g["onesr_bf"][:], br[0:1, :],
                    start=False, stop=True),
                    r=[bb(g["onesr_bf"]), bb(br)], w=[bb(p)])
                tr.dve(lambda e, p=p: e.tensor_copy(qk[:], p[:]),
                       r=[bb(p)], w=[bb(qk)])
                if ng >= 8:
                    tr.dma("sp", lambda e, tc=tc, co=co: e.dma_start(
                        out=kv_stage[TPC + tc * 128:TPC + (tc + 1) * 128,
                                     co:co + 512], in_=qk[:]),
                        r=[bb(qk)], w=[bb(kv_stage)])
                    continue
                cs4 = cosc[:, tc * 256:(tc + 1) * 256].rearrange(
                    "p (h c) -> p h c", c=GM)
                sn4 = sinc[:, tc * 256:(tc + 1) * 256].rearrange(
                    "p (h c) -> p h c", c=GM)
                tA4 = tA[:].rearrange("p (h c) -> p h c", c=GM)
                tB4 = tB[:].rearrange("p (h c) -> p h c", c=GM)
                qk4 = qk[:].rearrange("p (h c) -> p h c", c=128)
                rq4 = rq[:].rearrange("p (h c) -> p h c", c=128)
                qe, qo = qk4[:, :, 0:GM], qk4[:, :, GM:128]
                re_, ro = rq4[:, :, 0:GM], rq4[:, :, GM:128]
                csb, snb = bb(cosc), bb(sinc)
                tr.dve(lambda e, qe=qe, cs4=cs4, tA4=tA4: e.tensor_tensor(
                    tA4, qe, cs4, ALU.mult), r=[bb(qk), csb], w=[bb(tA)])
                tr.dve(lambda e, qo=qo, sn4=sn4, tB4=tB4: e.tensor_tensor(
                    tB4, qo, sn4, ALU.mult), r=[bb(qk), snb], w=[bb(tB)])
                tr.dve(lambda e, re_=re_, tA4=tA4, tB4=tB4: e.tensor_tensor(
                    re_, tA4, tB4, ALU.subtract),
                    r=[bb(tA), bb(tB)], w=[bb(rq)])
                tr.dve(lambda e, qe=qe, sn4=sn4, tA4=tA4: e.tensor_tensor(
                    tA4, qe, sn4, ALU.mult), r=[bb(qk), snb], w=[bb(tA)])
                tr.dve(lambda e, qo=qo, cs4=cs4, tB4=tB4: e.tensor_tensor(
                    tB4, qo, cs4, ALU.mult), r=[bb(qk), csb], w=[bb(tB)])
                tr.dve(lambda e, ro=ro, tA4=tA4, tB4=tB4: e.tensor_tensor(
                    ro, tA4, tB4, ALU.add),
                    r=[bb(tA), bb(tB)], w=[bb(rq)])
                if ng < 4:
                    tr.dma("sp", lambda e, tc=tc, ng=ng: e.dma_start(
                        out=q_dram[tc * 128:(tc + 1) * 128,
                                   ng * 512:(ng + 1) * 512], in_=rq[:]),
                        r=[bb(rq)], w=[bb(q_dram)])
                else:
                    tr.dma("sp", lambda e, tc=tc, ng=ng: e.dma_start(
                        out=kv_stage[tc * 128:(tc + 1) * 128,
                                     (ng - 4) * 512:(ng - 3) * 512], in_=rq[:]),
                        r=[bb(rq)], w=[bb(kv_stage)])
                tr.act(lambda e: e.activation(ksq[:], rq[:], AF.Square),
                       r=[bb(rq)], w=[bb(ksq)])
                for hh in range(4):
                    colid = ng * 4 + hh
                    tr.dve(lambda e, hh=hh, tc=tc, colid=colid: e.tensor_reduce(
                        nstore[:, tc * 32 + colid:tc * 32 + colid + 1],
                        ksq[:, hh * 128:(hh + 1) * 128],
                        mybir.AxisListType.X, ALU.add),
                        r=[bb(ksq)], w=[bb(nstore)])

        for colid in range(32):
            tr.dve(lambda e, colid=colid: e.tensor_reduce(
                nsum[:, colid:colid + 1],
                nstore[:, colid:512:32], mybir.AxisListType.X, ALU.max),
                r=[bb(nstore)], w=[bb(nsum)])
        ident_f = px.enter_context(nc.sbuf_tensor("p6idf", [128, 128], F32))
        ndram = nc.dram_tensor("p6ndram", [32, 1], F32).ap()
        tr.dma("sp", lambda e: e.dma_start(out=ident_f[:], in_=ein["ident"]),
               w=[bb(ident_f)])
        tr.pe(lambda e: e.transpose(pnt[:], nsum[:], ident_f[:]),
              r=[bb(nsum), bb(ident_f)], w=[bb(pnt)])
        tr.dve(lambda e: e.tensor_reduce(ntr[:, 0:1], pnt[:],
                                         mybir.AxisListType.X, ALU.max),
               r=[bb(pnt)], w=[bb(ntr)])
        tr.dma("g", lambda e: e.dma_start(out=ndram, in_=ntr[:]),
               r=[bb(ntr)], w=[bb(ndram)])
        tr.dma("g", lambda e: e.dma_start(
            out=nflat[:], in_=ndram.rearrange("r c -> c r")),
            r=[bb(ndram)], w=[bb(nflat)])
        tr.dve(lambda e: e.tensor_reduce(ex2[:, 0:1], nflat[0:1, 0:16],
                                         mybir.AxisListType.X, ALU.max),
               r=[bb(nflat)], w=[bb(ex2)])
        tr.dve(lambda e: e.tensor_reduce(ex2[:, 1:2], nflat[0:1, 16:32],
                                         mybir.AxisListType.X, ALU.max),
               r=[bb(nflat)], w=[bb(ex2)])
        tr.act(lambda e: e.mul(ex2[:, 0:2], ex2[:, 0:2], 1.05),
               r=[bb(ex2)], w=[bb(ex2)])
        tr.dma("g", lambda e: e.dma_start(
            out=kv_stage[2 * TPC:2 * TPC + 1, 0:2], in_=ex2[:]),
            r=[bb(ex2)], w=[bb(kv_stage)])
        tr.cc(lambda e: e.collective_compute(
            "AllGather", ALU.bypass,
            replica_groups=[[0, 1, 2, 3, 4, 5, 6, 7]],
            ins=[kv_stage.opt()], outs=[kv_gath.opt()]),
            r=[bb(kv_stage)], w=[bb(kv_gath)])



def _p7(nc, tr, g, ein, kv_stage, kv_gath, kbuf1, vbuf1, q_dram):
    bb = tr.buf
    xb = g["xb"]
    av = g["ln"]
    with ExitStack() as px:
        E = px.enter_context
        tri_bf = E(nc.sbuf_tensor("p7tri", [128, 4 * 512], BF16))
        mrow = E(nc.sbuf_tensor("p7mrow", [1, 16], BF16))
        mser = E(nc.sbuf_tensor("p7mser", [1, 4], F32))
        mser2 = E(nc.sbuf_tensor("p7mser2", [1, 4], F32))
        negM = E(nc.sbuf_tensor("p7negM", [128, 1], F32))
        bias_all = E(nc.sbuf_tensor("p7bias", [128, NSLOT], F32))
        KsbB = [E(nc.sbuf_tensor(f"p7K{i}", [128, 2 * TPC], BF16)) for i in range(2)]
        VsbB = [E(nc.sbuf_tensor(f"p7V{i}", [128, 2 * TPC], BF16)) for i in range(2)]
        QsbB = [E(nc.sbuf_tensor(f"p7Q{i}", [128, TPC], BF16)) for i in range(2)]
        Pt = [E(nc.sbuf_tensor(f"p7P{i}", [128, 512], BF16)) for i in range(4)]
        srow = E(nc.sbuf_tensor("p7srow", [1, 512], F32))
        rbsb = E(nc.sbuf_tensor("p7rb", [128, 512], F32))
        wo = [E(nc.sbuf_tensor(f"p7wo{i}", [128, 2048], BF16)) for i in range(2)]
        psS = [E(nc.psum_tensor(f"p7psS{i}", [128, 512], F32)) for i in range(2)]
        psAVB = [E(nc.psum_tensor(f"p7psAV{i}", [128, 512], F32)) for i in range(2)]
        psSumB = [E(nc.psum_tensor(f"p7psSum{i}", [1, 512], F32)) for i in range(2)]
        psR = E(nc.psum_tensor("p7psR", [128, 512], F32))
        pso = [E(nc.psum_tensor("p7pso0", [128, 512], F32))]

        for i in range(4):
            tr.dma("g", lambda e, i=i: e.dma_start(
                out=tri_bf[:, i * 512:(i + 1) * 512],
                in_=ein["triband"][i * 128:(i + 1) * 128, :]), w=[bb(tri_bf)])

        bnc = KsbB[1]
        for grp in range(16):
            lo = (grp % 2) * D
            tr.dma("g", lambda e, grp=grp, lo=lo: e.indirect_dma_start(
                bnc[:, lo:lo + D], None, kv_gath, IndirectOffsetOnAxis(
                    g["idxk_sb"][:, grp:grp + 1], 0)),
                r=[bb(kv_gath), bb(g["idxk_sb"])], w=[bb(bnc)])
            tr.dma("g", lambda e, grp=grp, lo=lo: e.dma_start(
                out=kbuf1[grp * 128:(grp + 1) * 128, :], in_=bnc[:, lo:lo + D]),
                r=[bb(bnc)], w=[bb(kbuf1)])
        for grp in range(16):
            lo = (grp % 2) * D
            tr.dma("g", lambda e, grp=grp, lo=lo: e.indirect_dma_start(
                bnc[:, lo:lo + D], None, kv_gath, IndirectOffsetOnAxis(
                    g["idxv_sb"][:, grp:grp + 1], 0)),
                r=[bb(kv_gath), bb(g["idxv_sb"])], w=[bb(bnc)])
            tr.dma("g", lambda e, grp=grp, lo=lo: e.dma_start(
                out=vbuf1[grp * 128:(grp + 1) * 128, :], in_=bnc[:, lo:lo + D]),
                r=[bb(bnc)], w=[bb(vbuf1)])

        for r in range(8):
            tr.dma("sp", lambda e, r=r: e.dma_start(
                out=mrow[0:1, r * 2:(r + 1) * 2],
                in_=kv_gath[r * KV_ROWS + 2 * TPC:r * KV_ROWS + 2 * TPC + 1, 0:2]),
                r=[bb(kv_gath)], w=[bb(mrow)])
        tr.dve(lambda e: e.tensor_reduce(mser[:, 0:1], mrow[0:1, 0:16:2],
                                         mybir.AxisListType.X, ALU.max),
               r=[bb(mrow)], w=[bb(mser)])
        tr.dve(lambda e: e.tensor_reduce(mser[:, 1:2], mrow[0:1, 1:16:2],
                                         mybir.AxisListType.X, ALU.max),
               r=[bb(mrow)], w=[bb(mser)])
        tr.act(lambda e: e.copy(mser2[:, 0:2], mser[:, 0:2]),
               r=[bb(mser)], w=[bb(mser2)])
        tr.dve(lambda e: e.tensor_tensor(mser[:, 2:3], mser2[:, 0:1],
                                         mser2[:, 1:2], ALU.mult),
               r=[bb(mser2)], w=[bb(mser)])
        tr.act(lambda e: e.activation(mser[:, 3:4], mser[:, 2:3], AF.Sqrt),
               r=[], w=[bb(mser)])
        tr.dve(lambda e: e.tensor_scalar_mul(mser[:, 3:4], mser[:, 3:4], -1.0),
               r=[], w=[bb(mser)])
        tr.pe(lambda e: e.matmul(psR[:, 0:1], g["onesr_f"][:], mser[0:1, 3:4],
                                 start=True, stop=True),
              r=[bb(mser), bb(g["onesr_f"])], w=[bb(psR)])
        tr.dve(lambda e: e.tensor_copy(negM[:], psR[:, 0:1]),
               r=[bb(psR)], w=[bb(negM)])
        tr.act(lambda e: e.activation(bias_all[:], g["moff_sb"][:],
                                      AF.Identity, bias=negM[:, 0:1]),
               r=[bb(g["moff_sb"]), bb(negM)], w=[bb(bias_all)])

        pend = [None]

        def finalize(fin):
            h, gq, psAV, psSum = fin
            tr.dve(lambda e, psSum=psSum: e.reciprocal(srow[:], psSum[:]),
                   r=[bb(psSum)], w=[bb(srow)])
            tr.pe(lambda e: e.matmul(psR[:], g["onesr_f"][:], srow[:],
                                     start=True, stop=True),
                  r=[bb(srow), bb(g["onesr_f"])], w=[bb(psR)])
            tr.act(lambda e: e.copy(rbsb[:], psR[:]),
                   r=[bb(psR)], w=[bb(rbsb)])
            tr.dve(lambda e, h=h, gq=gq, psAV=psAV: e.tensor_tensor(
                av[h][:, gq * 512:(gq + 1) * 512], psAV[:], rbsb[:],
                ALU.mult), r=[bb(psAV), bb(rbsb)], w=[bb(av[h])])

        for h in range(NH):
            Ksb, Vsb, Qsb = KsbB[h % 2], VsbB[h % 2], QsbB[h % 2]
            q1, q2 = "sp", "g"
            for q4 in range(4):
                tr.dma(q1, lambda e, h=h, q4=q4, Ksb=Ksb: e.dma_start(
                    out=Ksb[:, TPC + q4 * 512:TPC + (q4 + 1) * 512],
                    in_=kv_stage[q4 * 512:(q4 + 1) * 512, h * 128:(h + 1) * 128],
                    transpose=True), r=[bb(kv_stage)], w=[bb(Ksb)])
                tr.dma(q1, lambda e, h=h, q4=q4, Qsb=Qsb: e.dma_start(
                    out=Qsb[:, q4 * 512:(q4 + 1) * 512],
                    in_=q_dram[q4 * 512:(q4 + 1) * 512, h * 128:(h + 1) * 128],
                    transpose=True), r=[bb(q_dram)], w=[bb(Qsb)])
                tr.dma(q2, lambda e, h=h, q4=q4, Vsb=Vsb: e.dma_start(
                    out=Vsb[:, TPC + q4 * 512:TPC + (q4 + 1) * 512],
                    in_=kv_stage[TPC + q4 * 512:TPC + (q4 + 1) * 512,
                                 h * 128:(h + 1) * 128]
                    .rearrange("(kt p) c -> p kt c", p=128)),
                    r=[bb(kv_stage)], w=[bb(Vsb)])
            for q4 in range(4):
                tr.dma(q1, lambda e, h=h, q4=q4, Ksb=Ksb: e.dma_start(
                    out=Ksb[:, q4 * 512:(q4 + 1) * 512],
                    in_=kbuf1[q4 * 512:(q4 + 1) * 512, h * 128:(h + 1) * 128],
                    transpose=True), r=[bb(kbuf1)], w=[bb(Ksb)])
                tr.dma(q2, lambda e, h=h, q4=q4, Vsb=Vsb: e.dma_start(
                    out=Vsb[:, q4 * 512:(q4 + 1) * 512],
                    in_=vbuf1[q4 * 512:(q4 + 1) * 512, h * 128:(h + 1) * 128]
                    .rearrange("(kt p) c -> p kt c", p=128)),
                    r=[bb(vbuf1)], w=[bb(Vsb)])
            slot_base = 0
            for gq in range(4):
                nkt = 20 + 4 * gq
                psAV = psAVB[gq % 2]
                psSum = psSumB[gq % 2]
                order = list(range(16, nkt)) + list(range(16))
                stage = [None]

                def emit_av(pp):
                    kt, P, first, last, psAV, psSum, Vsb = pp
                    tr.pe(lambda e, kt=kt, P=P, first=first, last=last,
                          psAV=psAV, Vsb=Vsb: e.matmul(
                        psAV[:], Vsb[:, kt * 128:(kt + 1) * 128], P[:],
                        start=first, stop=last),
                        r=[bb(Vsb), bb(P)], w=[bb(psAV)])
                    tr.pe(lambda e, P=P, first=first, last=last,
                          psSum=psSum: e.matmul(
                        psSum[:], g["ones_bf"][:], P[:],
                        start=first, stop=last),
                        r=[bb(g["ones_bf"]), bb(P)], w=[bb(psSum)])

                for oi, kt in enumerate(order):
                    slot = slot_base + kt
                    first = (oi == 0)
                    last = (oi == nkt - 1)
                    pS = psS[oi % 2]
                    P = Pt[oi % 4]
                    tr.pe(lambda e, kt=kt, gq=gq, pS=pS, Ksb=Ksb, Qsb=Qsb: e.matmul(
                        pS[:], Ksb[:, kt * 128:(kt + 1) * 128],
                        Qsb[:, gq * 512:(gq + 1) * 512],
                        start=True, stop=True),
                        r=[bb(Ksb), bb(Qsb)], w=[bb(pS)])
                    tr.act(lambda e, pS=pS, P=P, slot=slot: e.activation(
                        P[:], pS[:], AF.Exp, bias=bias_all[:, slot:slot + 1]),
                        r=[bb(pS), bb(bias_all)], w=[bb(P)])
                    if kt >= 16 + 4 * gq:
                        ti = kt - 16 - 4 * gq
                        tr.dve(lambda e, P=P, ti=ti: e.tensor_tensor(
                            P[:], P[:], tri_bf[:, ti * 512:(ti + 1) * 512],
                            ALU.mult), r=[bb(tri_bf)], w=[bb(P)])
                    if stage[0] is not None:
                        emit_av(stage[0])
                    stage[0] = (kt, P, first, last, psAV, psSum, Vsb)
                    if oi == 4 and pend[0] is not None:
                        finalize(pend[0])
                        pend[0] = None
                emit_av(stage[0])
                slot_base += nkt
                pend[0] = (h, gq, psAV, psSum)
        if pend[0] is not None:
            finalize(pend[0])
            pend[0] = None

        for oc in range(NDC):
            w_ = wo[oc % 2]
            tr.dma("g", lambda e, oc=oc, w_=w_: e.dma_start(
                out=w_[:], in_=ein["wout4"][oc * D:(oc + 1) * D, :]
                .rearrange("(i p) c -> p i c", p=128)), w=[bb(w_)])
            for tg in range(NTG):
                p = pso[0]
                for ic in range(NDC):
                    tr.pe(lambda e, ic=ic, tg=tg, w_=w_, p=p: e.matmul(
                        p[:], w_[:, ic * 128:(ic + 1) * 128],
                        av[ic][:, tg * 512:(tg + 1) * 512],
                        start=(ic == 0), stop=(ic == NDC - 1)),
                        r=[bb(w_), bb(av[ic])], w=[bb(p)])
                tr.dve(lambda e, oc=oc, tg=tg, p=p: e.scalar_tensor_tensor(
                    xb[oc][:, tg * 512:(tg + 1) * 512], p[:],
                    g["bout_s"][:, oc:oc + 1],
                    xb[oc][:, tg * 512:(tg + 1) * 512], ALU.add, ALU.add),
                    r=[bb(p), bb(g["bout_s"])], w=[bb(xb[oc])])



def kernel(**inputs):
    in_maps = _host_inputs(inputs)
    nc = build()
    res = run_bass_kernel_spmd(nc, in_maps, core_ids=list(range(8)))
    out = np.empty((B, T, D), np.float32)
    for c in range(8):
        b, hlf = c // 2, c % 2
        out[b, hlf * TPC:(hlf + 1) * TPC, :] = np.asarray(res.results[c]["y"]).T
    return out


# version 17 (final)
# speedup vs baseline: 1.3383x; 1.0066x over previous
import math
import numpy as np
from contextlib import ExitStack

import concourse.bass as bass
import concourse.mybir as mybir
from concourse.bass_utils import run_bass_kernel_spmd
from concourse.bass import IndirectOffsetOnAxis

F32 = mybir.dt.float32
BF16 = mybir.dt.bfloat16
I32 = mybir.dt.int32
AF = mybir.ActivationFunctionType
ALU = mybir.AluOpType

B, T, D = 4, 4096, 2048
TPC = 2048
HALO = 3
NDC = D // 128
NTG = TPC // 512
CH = 128
NCH = TPC // CH
GM = 64
FH = 5632
NHC = FH // 128
NH, HD = 16, 128
SCALE_GLA = 1.0 / math.sqrt(GM)
SCALEB = 1.0 / math.sqrt(HD)
KV_ROWS = 2 * TPC + 1
NSLOT = sum(20 + 4 * g for g in range(4))
LN_EPS = 1e-5
HHC = NHC // 2
_DBG_STOP = [0]



class _Eng:
    def __init__(self, name):
        self.name = name
        self.ops = []
        self.n_dma = 0
        self.n_ms = 0
        self.seq = 0
        self.last_ms_seq = -1
        self.ms_list = []
        self.waited = {}


class _Op:
    __slots__ = ("eng", "fn", "kind", "seq", "phase", "waits", "incs",
                 "ms_val", "dma_val", "tiny")

    def __init__(self, eng, fn, kind, seq, phase):
        self.eng, self.fn, self.kind = eng, fn, kind
        self.seq, self.phase = seq, phase
        self.waits = []
        self.incs = []
        self.ms_val = None
        self.dma_val = None
        self.tiny = False


class _Buf:
    __slots__ = ("writer", "readers")

    def __init__(self):
        self.writer = None
        self.readers = []


class Prog:
    ENG2SECT = {"pe": "tensor", "act": "scalar", "dve": "vector",
                "sp": "sync", "g": "gpsimd"}

    def __init__(self, nc, ctx):
        self.nc = nc
        self.engs = {n: _Eng(n) for n in ("pe", "act", "dve", "sp", "g")}
        self.sems = {}
        for n in ("pe_s", "act_s", "dve_s", "g_s", "sp_s", "sp_d", "g_d", "cc_s"):
            self.sems[n] = ctx.enter_context(nc.semaphore(n))
        self.phase = 0
        self.n_cc = 0
        self._bufs = {}

    def buf(self, obj):
        key = id(obj)
        b = self._bufs.get(key)
        if b is None:
            b = self._bufs[key] = _Buf()
        return b

    def _milestone(self, op):
        e = op.eng
        if op.kind == "dma":
            return (e.name + "_d", 16 * e.n_dma)
        if op.kind == "cc":
            return ("cc_s", self.n_cc)
        if op.ms_val is None:
            if op.seq <= e.last_ms_seq:
                import bisect
                i = bisect.bisect_left(e.ms_list, (op.seq, -1))
                return (e.name + "_s", e.ms_list[i][1])
            e.n_ms += 1
            e.last_ms_seq = op.seq
            op.ms_val = e.n_ms
            op.incs.append((e.name + "_s", 1))
            e.ms_list.append((op.seq, op.ms_val))
        return (e.name + "_s", op.ms_val)

    def _wait(self, cons, prod):
        if prod is None:
            return
        if prod.kind == "exec":
            if prod.eng is cons.eng:
                if not prod.tiny:
                    return
            elif prod.phase < self.phase:
                return
        sem, val = self._milestone(prod)
        w = cons.eng.waited
        if w.get(sem, -1) >= val:
            return
        w[sem] = val
        cons.waits.append((sem, val))

    def emit(self, eng_name, fn, reads=(), writes=(), kind="exec", tiny=False):
        e = self.engs[eng_name]
        op = _Op(e, fn, kind, e.seq, self.phase)
        op.tiny = tiny
        e.seq += 1
        for r in reads:
            b = self.buf(r)
            self._wait(op, b.writer)
            b.readers.append(op)
        for wt in writes:
            b = self.buf(wt)
            self._wait(op, b.writer)
            for r in b.readers:
                self._wait(op, r)
            b.writer = op
            b.readers = []
        if kind == "dma":
            e.n_dma += 1
            op.dma_val = 16 * e.n_dma
            op.incs.append((e.name + "_d", 16))
        elif kind == "cc":
            self.n_cc += 1
            op.dma_val = self.n_cc
            op.incs.append(("cc_s", 1))
        e.ops.append(op)
        return op

    def flush(self):
        nc, sems = self.nc, self.sems
        if any(e.ops for e in self.engs.values()):
            with nc.Block() as blk:
                for name, e in self.engs.items():
                    if not e.ops:
                        continue
                    ops, e.ops = e.ops, []

                    def runner(engine, ops=ops):
                        for op in ops:
                            for sem, val in op.waits:
                                engine.wait_ge(sems[sem], val)
                            inst = op.fn(engine)
                            for sem, amt in op.incs:
                                inst.then_inc(sems[sem], amt)
                    getattr(blk, self.ENG2SECT[name])(runner)
        self.phase += 1

    def pe(self, fn, r=(), w=()):
        return self.emit("pe", fn, r, w)

    def act(self, fn, r=(), w=()):
        return self.emit("act", fn, r, w)

    def dve(self, fn, r=(), w=()):
        return self.emit("dve", fn, r, w)

    def dve_t(self, fn, r=(), w=()):
        return self.emit("dve", fn, r, w, tiny=True)

    def act_t(self, fn, r=(), w=()):
        return self.emit("act", fn, r, w, tiny=True)

    def dma(self, q, fn, r=(), w=()):
        return self.emit(q, fn, r, w, kind="dma")

    def gx(self, fn, r=(), w=()):
        return self.emit("g", fn, r, w)

    def cc(self, fn, r=(), w=()):
        return self.emit("g", fn, r, w, kind="cc")



def _t4(wT, n_in_tiles, n_out_tiles):
    a = wT.reshape(n_in_tiles, 128, n_out_tiles, 128)
    return np.ascontiguousarray(
        a.transpose(2, 0, 1, 3).reshape(n_out_tiles * n_in_tiles * 128, 128),
        dtype=np.float32)


def _host_inputs(inp):
    x = np.asarray(inp["L0_attn_a_hidden_states"])
    A = lambda a: np.ascontiguousarray(a, dtype=np.float32)

    perm = np.concatenate([h * HD + np.concatenate([np.arange(0, HD, 2),
                                                    np.arange(1, HD, 2)])
                           for h in range(NH)])
    wqkvT = np.asarray(inp["L0_attn_b_w_qkv"], dtype=np.float32).T.copy()
    wqkvT[:, 0:D] = wqkvT[:, 0:D][:, perm]
    wqkvT[:, D:2 * D] = wqkvT[:, D:2 * D][:, perm]
    bqkv = np.asarray(inp["L0_attn_b_b_qkv"], dtype=np.float32).copy()
    bqkv[0:D] = bqkv[0:D][perm]
    bqkv[D:2 * D] = bqkv[D:2 * D][perm]

    w1T = np.asarray(inp["L0_ffn_w1"], dtype=np.float32).T.copy()
    w2T = np.asarray(inp["L0_ffn_w2"], dtype=np.float32).T.copy()
    w3T = np.asarray(inp["L0_ffn_w3"], dtype=np.float32).T.copy()

    half = HD // 2
    thetas = 10000.0 ** (-np.arange(half, dtype=np.float64) / half)
    rs = math.sqrt(SCALEB)

    tri = (np.arange(512)[:, None] <= np.arange(512)[None, :]).astype(np.float32)
    glam = (np.arange(128)[:, None] <= np.arange(128)[None, :]).astype(np.float32)

    pcol = lambda v: A(np.asarray(v).reshape(-1, 128).T)
    memgb = np.asarray(inp["L0_attn_a_mem_gate_b"], dtype=np.float32)

    shared = {
        "convw": A(np.asarray(inp["L0_attn_a_conv_w"]).reshape(NDC, 128, 4)
                   .transpose(1, 0, 2).reshape(128, NDC * 4)),
        "convb": pcol(inp["L0_attn_a_conv_b"]),
        "resw": pcol(inp["L0_attn_a_residual_weight"]),
        "inprojT": A(np.asarray(inp["L0_attn_a_in_proj_w"]).T),
        "memgT": A(np.asarray(inp["L0_attn_a_mem_gate_w"]).T),
        "memgb2": A(np.stack([memgb[:GM], memgb[GM:]], axis=1)),
        "chw0T": A(np.asarray(inp["L0_attn_a_ch_gate_w0"]).T),
        "chw1T": A(np.asarray(inp["L0_attn_a_ch_gate_w1"]).T),
        "chb1": A(np.asarray(inp["L0_attn_a_ch_gate_b1"]).reshape(1, D)),
        "wqkT": A(wqkvT[:, 0:2 * D]),
        "wvT": A(wqkvT[:, 2 * D:]),
        "bqkrow": A(bqkv[0:2 * D].reshape(1, 2 * D)),
        "bvrow": A(bqkv[2 * D:].reshape(1, D)),
        "wout4": _t4(np.asarray(inp["L0_attn_b_w_out"]).T, NDC, NDC),
        "bout": pcol(inp["L0_attn_b_b_out"]),
        "w14": _t4(w1T, NDC, NHC),
        "w24": _t4(w2T, NDC, NHC),
        "w34": _t4(w3T, NHC, NDC),
        "triband": tri, "glamask": glam,
        "ident": np.eye(128, dtype=np.float32),
    }

    in_maps = []
    for c in range(8):
        b, hlf = c // 2, c % 2
        t0 = hlf * TPC
        xs = x[b, t0:t0 + TPC, :].T
        if hlf == 0:
            haloc = np.zeros((D, HALO), np.float32)
        else:
            haloc = x[b, t0 - HALO:t0, :].T
        xT = A(np.concatenate([haloc, xs], axis=1))

        pos = np.arange(t0, t0 + TPC, dtype=np.float64)
        ang = pos[:, None] * thetas[None, :]
        cosP = np.tile((np.cos(ang) * rs).astype(np.float32), (1, 4))
        sinP = np.tile((np.sin(ang) * rs).astype(np.float32), (1, 4))

        moff = np.zeros((128, NSLOT), np.float32)
        col = 0
        for g_ in range(4):
            for kt in range(20 + 4 * g_):
                if hlf == 0 and kt < 16:
                    moff[:, col] = -30000.0
                col += 1

        smask = np.zeros((GM, 8), np.float32)
        if hlf == 1:
            smask[:, c - 1] = 1.0

        ra = 1 - hlf
        grp = np.arange(16)
        p = np.arange(128)
        idxk = (ra * KV_ROWS + grp[None, :] * 128 + p[:, None]).astype(np.int32)
        idxv = (ra * KV_ROWS + TPC + grp[None, :] * 128 + p[:, None]).astype(np.int32)

        m = dict(shared)
        m.update({
            "xT": xT, "cosP": A(cosP), "sinP": A(sinP), "moff": moff,
            "smask": smask,
            "idxk": np.ascontiguousarray(idxk),
            "idxv": np.ascontiguousarray(idxv),
            "g1": A(w1T[c * 256:(c + 1) * 256, :]),
            "g2": A(w2T[c * 256:(c + 1) * 256, :]),
            "g3": A(w3T[c * 704:(c + 1) * 704, :]),
        })
        in_maps.append(m)
    return in_maps


def _dump_x(nc, tr, g, y):
    bb = tr.buf
    for dc in range(NDC):
        tr.dma("g", lambda e, dc=dc: e.dma_start(
            out=y[dc * 128:(dc + 1) * 128, :], in_=g["xb"][dc][:]),
            r=[bb(g["xb"][dc])], w=[bb(y)])
    tr.flush()


def build():
    _LN_N[0] = 0
    _FFN_N[0] = 0
    nc = bass.Bass("TRN2", target_bir_lowering=False, debug=False)

    ein = {}

    def EIN(name, shape, dtype=F32):
        ein[name] = nc.dram_tensor(name, list(shape), dtype,
                                   kind="ExternalInput").ap()

    EIN("xT", (D, TPC + HALO))
    EIN("convw", (128, NDC * 4)); EIN("convb", (128, NDC)); EIN("resw", (128, NDC))
    EIN("inprojT", (D, 2 * GM)); EIN("memgT", (D, 2 * GM)); EIN("memgb2", (GM, 2))
    EIN("chw0T", (D, 128)); EIN("chw1T", (128, D)); EIN("chb1", (1, D))
    EIN("wqkT", (D, 2 * D)); EIN("wvT", (D, D))
    EIN("bqkrow", (1, 2 * D)); EIN("bvrow", (1, D))
    EIN("wout4", (NDC * D, 128)); EIN("bout", (128, NDC))
    EIN("w14", (NHC * D, 128)); EIN("w24", (NHC * D, 128)); EIN("w34", (NDC * FH, 128))
    EIN("triband", (512, 512)); EIN("glamask", (128, 128)); EIN("ident", (128, 128))
    EIN("cosP", (TPC, 256)); EIN("sinP", (TPC, 256))
    EIN("moff", (128, NSLOT)); EIN("smask", (GM, 8))
    EIN("idxk", (128, 16), I32); EIN("idxv", (128, 16), I32)
    EIN("g1", (256, FH)); EIN("g2", (256, FH)); EIN("g3", (704, D))

    y = nc.dram_tensor("y", [D, TPC], F32, kind="ExternalOutput").ap()

    u_dram = nc.dram_tensor("u_dram", [TPC, D], BF16).ap()
    q_dram = nc.dram_tensor("q_dram", [TPC, D], BF16).ap()
    kv_stage = nc.dram_tensor("kv_stage", [KV_ROWS, D], BF16).ap()
    kv_gath = nc.dram_tensor("kv_gath", [2 * KV_ROWS, D], BF16).ap()
    s_stage = nc.dram_tensor("s_stage", [GM + 1, D], F32).ap()
    s_gath = nc.dram_tensor("s_gath", [8 * (GM + 1), D], F32,
                            addr_space="Shared").ap()
    kbuf1 = nc.dram_tensor("kbuf1", [TPC, D], BF16).ap()
    vbuf1 = nc.dram_tensor("vbuf1", [TPC, D], BF16).ap()
    w12qc = nc.dram_tensor("w12qc", [NHC * 2 * D, 128], BF16).ap()
    w3qc = nc.dram_tensor("w3qc", [NDC * FH, 128], BF16).ap()

    with ExitStack() as ctx:
        E = ctx.enter_context
        tr = Prog(nc, ctx)

        g = {}
        g["xb"] = [E(nc.sbuf_tensor(f"x{i}", [128, TPC], BF16)) for i in range(NDC)]
        g["ln"] = [E(nc.sbuf_tensor(f"ln{i}", [128, TPC + HALO], BF16))
                   for i in range(NDC)]
        for nm, shp, dt in (
            ("ones_bf", [128, 1], BF16), ("ones_f", [128, 1], F32),
            ("onesr_bf", [1, 128], BF16), ("onesr_f", [1, 128], F32),
            ("row_m", [1, TPC + HALO], F32),
            ("row_r", [1, TPC + HALO], F32),
            ("convb_s", [128, NDC], F32), ("resw_s", [128, NDC], F32),
            ("convw_s", [128, NDC * 4], F32), ("memgb_s", [GM, 2], F32),
            ("gpart", [1, 16], F32), ("thr_s", [128, 3], F32),
            ("nthr_s", [128, 3], F32),
            ("moff_sb", [128, NSLOT], F32), ("smask_sb", [GM, 8], F32),
            ("idxk_sb", [128, 16], I32), ("idxv_sb", [128, 16], I32),
            ("bout_s", [128, NDC], F32),
        ):
            g[nm] = E(nc.sbuf_tensor(nm, shp, dt))

        _p0(nc, tr, g, ein)
        tr.flush()
        if _DBG_STOP[0] == 1:
            _dump_x(nc, tr, g, y)
            return _finish(nc, tr)
        with ExitStack() as gla:
            Eg = gla.enter_context
            for nm, shp, dt in (
                ("r_bf", [GM, TPC], BF16), ("kh_bf", [GM, TPC], BF16),
                ("d_f", [GM, TPC], F32), ("S_f", [GM, D], F32),
                ("S_bf", [GM, D], BF16), ("eD_f", [GM, NCH], F32),
                ("ident_bf", [128, 128], BF16), ("glam_f", [128, 128], F32),
                ("ones64", [GM, 128], F32),
            ):
                g[nm] = Eg(nc.sbuf_tensor(nm, shp, dt))
            tr.gx(lambda e: e.memset(g["ones64"][:], 1.0), w=[tr.buf(g["ones64"])])
            tr.dma("g", lambda e: e.dma_start(out=g["ident_bf"][:], in_=ein["ident"]),
                   w=[tr.buf(g["ident_bf"])])
            tr.dma("sp", lambda e: e.dma_start(out=g["glam_f"][:], in_=ein["glamask"]),
                   w=[tr.buf(g["glam_f"])])
            _p1(nc, tr, g, ein)
            tr.flush()
            _p2(nc, tr, g, ein, u_dram)
            tr.flush()
            _p3(nc, tr, g, ein, u_dram, s_stage, s_gath)
            tr.flush()
            if _DBG_STOP[0] not in (0, 2, 3, 4, 5):
                pass
            _p4(nc, tr, g, ein, s_gath, w12qc, w3qc)
            tr.flush()
        if _DBG_STOP[0] == 2:
            _dump_x(nc, tr, g, y)
            return _finish(nc, tr)
        _p4q(nc, tr, g, ein, w12qc, w3qc)
        tr.flush()
        if _DBG_STOP[0] == 6:
            tr.dma("g", lambda e: e.dma_start(
                out=y[0:128, :], in_=w12qc[0:D, :].rearrange("(i p) c -> p i c", p=128)),
                r=[tr.buf(w12qc)], w=[tr.buf(y)])
            tr.dma("g", lambda e: e.dma_start(
                out=y[128:256, :], in_=w3qc[0:D, :].rearrange("(i p) c -> p i c", p=128)),
                r=[tr.buf(w3qc)], w=[tr.buf(y)])
            tr.dma("g", lambda e: e.dma_start(
                out=y[256:384, 0:3], in_=g["thr_s"][0:128, 0:3]),
                r=[tr.buf(g["thr_s"])], w=[tr.buf(y)])
            tr.flush()
            return _finish(nc, tr)
        _ln_x(nc, tr, g)
        tr.flush()
        if _DBG_STOP[0] == 5:
            bb = tr.buf
            for dc in range(NDC):
                tr.dma("g", lambda e, dc=dc: e.dma_start(
                    out=y[dc * 128:(dc + 1) * 128, :], in_=g["ln"][dc][:, 0:TPC]),
                    r=[bb(g["ln"][dc])], w=[bb(y)])
            tr.flush()
            return _finish(nc, tr)
        _p_ffn(nc, tr, g, w12qc, w3qc, y=None)
        tr.flush()
        if _DBG_STOP[0] == 3:
            _dump_x(nc, tr, g, y)
            return _finish(nc, tr)
        _ln_x(nc, tr, g)
        tr.flush()
        _p6(nc, tr, g, ein, q_dram, kv_stage, kv_gath)
        tr.flush()
        _p7(nc, tr, g, ein, kv_stage, kv_gath, kbuf1, vbuf1, q_dram)
        tr.flush()
        if _DBG_STOP[0] == 4:
            _dump_x(nc, tr, g, y)
            return _finish(nc, tr)
        _ln_x(nc, tr, g)
        tr.flush()
        _p_ffn(nc, tr, g, w12qc, w3qc, y=y)
        tr.flush()

        _finish(nc, tr)
    return nc


def _finish(nc, tr):
    with nc.Block() as blk:
        sems, engs = tr.sems, tr.engs

        @blk.sync
        def _(sy):
            sy.wait_ge(sems["sp_d"], 16 * engs["sp"].n_dma)

        @blk.gpsimd
        def _(gq):
            gq.wait_ge(sems["g_d"], 16 * engs["g"].n_dma)
    return nc



def _p0(nc, tr, g, ein):
    bb = tr.buf
    xb, ln = g["xb"], g["ln"]
    with ExitStack() as px:
        E = px.enter_context
        xin = [E(nc.sbuf_tensor(f"p0xin{i}", [128, TPC + HALO], F32)) for i in range(2)]
        sqt = [E(nc.sbuf_tensor(f"p0sq{i}", [128, TPC + HALO], BF16)) for i in range(2)]
        hx = E(nc.sbuf_tensor("p0hx", [128, NDC * HALO], F32))
        hq = E(nc.sbuf_tensor("p0hq", [128, NDC * HALO], F32))
        gtile = [E(nc.sbuf_tensor(f"p0gt{i}", [128, 2048], F32)) for i in range(2)]
        gacc = E(nc.sbuf_tensor("p0gacc", [128, 32], F32))
        gcol = E(nc.sbuf_tensor("p0gcol", [128, 1], F32))
        scr = E(nc.sbuf_tensor("p0scr", [128, 512 + HALO], F32))
        g["row_t"] = E(nc.sbuf_tensor("p0rowt", [1, TPC + HALO], F32))

        seg1 = ExitStack()
        E1 = seg1.enter_context
        psx = [E1(nc.psum_tensor(f"p0psx{i}", [1, 512], F32)) for i in range(4)]
        psq = [E1(nc.psum_tensor(f"p0psq{i}", [1, 512], F32)) for i in range(4)]

        tr.gx(lambda e: e.memset(g["ones_bf"][:], 1.0), w=[bb(g["ones_bf"])])
        tr.gx(lambda e: e.memset(g["ones_f"][:], 1.0), w=[bb(g["ones_f"])])
        tr.gx(lambda e: e.memset(g["onesr_bf"][:], 1.0), w=[bb(g["onesr_bf"])])
        tr.gx(lambda e: e.memset(g["onesr_f"][:], 1.0), w=[bb(g["onesr_f"])])
        for nm, src in (("convb_s", "convb"), ("resw_s", "resw"),
                        ("convw_s", "convw"), ("memgb_s", "memgb2"),
                        ("moff_sb", "moff"), ("smask_sb", "smask"),
                        ("idxk_sb", "idxk"), ("idxv_sb", "idxv"),
                        ("bout_s", "bout")):
            tr.dma("sp", lambda e, nm=nm, src=src: e.dma_start(
                out=g[nm][:], in_=ein[src]), w=[bb(g[nm])])

        for dc in range(NDC):
            xt = xin[dc % 2]
            sq = sqt[dc % 2]
            tr.dma("sp", lambda e, dc=dc, xt=xt: e.dma_start(
                out=xt[:], in_=ein["xT"][dc * 128:(dc + 1) * 128, :]),
                w=[bb(xt)])
            tr.dve(lambda e, dc=dc, xt=xt: e.tensor_copy(ln[dc][:], xt[:]),
                   r=[bb(xt)], w=[bb(ln[dc])])
            tr.dve(lambda e, dc=dc: e.tensor_copy(
                xb[dc][:], ln[dc][:, HALO:]), r=[bb(ln[dc])], w=[bb(xb[dc])])
            tr.act(lambda e, xt=xt, sq=sq: e.activation(
                sq[:], xt[:], AF.Square), r=[bb(xt)], w=[bb(sq)])
            for tg in range(4):
                c0 = HALO + tg * 512
                tr.pe(lambda e, dc=dc, tg=tg, c0=c0: e.matmul(
                    psx[tg][:], g["ones_bf"][:], ln[dc][:, c0:c0 + 512],
                    start=(dc == 0), stop=(dc == NDC - 1)),
                    r=[bb(ln[dc]), bb(g["ones_bf"])], w=[bb(psx[tg])])
                tr.pe(lambda e, dc=dc, tg=tg, c0=c0, sq=sq: e.matmul(
                    psq[tg][:], g["ones_bf"][:], sq[:, c0:c0 + 512],
                    start=(dc == 0), stop=(dc == NDC - 1)),
                    r=[bb(sq), bb(g["ones_bf"])], w=[bb(psq[tg])])
        for tg in range(4):
            tr.dve(lambda e, tg=tg: e.tensor_copy(
                g["row_m"][:, HALO + tg * 512:HALO + (tg + 1) * 512],
                psx[tg][:]), r=[bb(psx[tg])], w=[bb(g["row_m"])])
            tr.dve(lambda e, tg=tg: e.tensor_copy(
                g["row_r"][:, HALO + tg * 512:HALO + (tg + 1) * 512],
                psq[tg][:]), r=[bb(psq[tg])], w=[bb(g["row_r"])])
        seg1.close()
        tr.flush()

        psh = [E(nc.psum_tensor(f"p0psh{i}", [1, HALO], F32)) for i in range(2)]
        psb = [E(nc.psum_tensor(f"p0psb{i}", [128, 512], F32)) for i in range(2)]
        psg = E(nc.psum_tensor("p0psg", [1, 1], F32))

        for dc in range(NDC):
            tr.dma("sp", lambda e, dc=dc: e.dma_start(
                out=hx[:, dc * HALO:(dc + 1) * HALO],
                in_=ein["xT"][dc * 128:(dc + 1) * 128, 0:HALO]), w=[bb(hx)])
        tr.act(lambda e: e.activation(hq[:], hx[:], AF.Square),
               r=[bb(hx)], w=[bb(hq)])
        for dc in range(NDC):
            tr.pe(lambda e, dc=dc: e.matmul(
                psh[0][:], g["ones_f"][:], hx[:, dc * HALO:(dc + 1) * HALO],
                start=(dc == 0), stop=(dc == NDC - 1)),
                r=[bb(hx), bb(g["ones_f"])], w=[bb(psh[0])])
            tr.pe(lambda e, dc=dc: e.matmul(
                psh[1][:], g["ones_f"][:], hq[:, dc * HALO:(dc + 1) * HALO],
                start=(dc == 0), stop=(dc == NDC - 1)),
                r=[bb(hq), bb(g["ones_f"])], w=[bb(psh[1])])
        tr.act(lambda e: e.copy(g["row_m"][:, 0:HALO], psh[0][:]),
               r=[bb(psh[0])], w=[bb(g["row_m"])])
        tr.act(lambda e: e.copy(g["row_r"][:, 0:HALO], psh[1][:]),
               r=[bb(psh[1])], w=[bb(g["row_r"])])

        _stats_pipeline(nc, tr, g, n=TPC + HALO)

        groups = [(0, HALO)] + [(HALO + t * 512, 512) for t in range(4)]
        for (c0, n) in groups:
            tr.pe(lambda e, c0=c0, n=n: e.matmul(
                psb[0][:, 0:n], g["onesr_f"][:], g["row_m"][0:1, c0:c0 + n],
                start=True, stop=True),
                r=[bb(g["row_m"]), bb(g["onesr_f"])], w=[bb(psb[0])])
            tr.pe(lambda e, c0=c0, n=n: e.matmul(
                psb[1][:, 0:n], g["onesr_f"][:], g["row_r"][0:1, c0:c0 + n],
                start=True, stop=True),
                r=[bb(g["row_r"]), bb(g["onesr_f"])], w=[bb(psb[1])])
            for dc in range(NDC):
                tr.dve(lambda e, dc=dc, c0=c0, n=n: e.tensor_tensor(
                    scr[:, 0:n], ln[dc][:, c0:c0 + n], psb[0][:, 0:n],
                    ALU.subtract), r=[bb(ln[dc]), bb(psb[0])], w=[bb(scr)])
                if n < 128:
                    tr.act(lambda e, n=n: e.copy(scr[:, 512:512 + n], scr[:, 0:n]),
                           r=[bb(scr)], w=[bb(scr)])
                    tr.dve(lambda e, dc=dc, c0=c0, n=n: e.tensor_tensor(
                        ln[dc][:, c0:c0 + n], scr[:, 512:512 + n], psb[1][:, 0:n],
                        ALU.mult), r=[bb(scr), bb(psb[1])], w=[bb(ln[dc])])
                else:
                    tr.dve(lambda e, dc=dc, c0=c0, n=n: e.tensor_tensor(
                        ln[dc][:, c0:c0 + n], scr[:, 0:n], psb[1][:, 0:n],
                        ALU.mult), r=[bb(scr), bb(psb[1])], w=[bb(ln[dc])])

        tr.gx(lambda e: e.memset(gacc[:], 0.0), w=[bb(gacc)])
        col = 0
        spans = []
        for nm, rows, cols in (("g1", 256, FH), ("g2", 256, FH), ("g3", 704, D)):
            start_col = col
            r0 = 0
            while r0 < rows:
                pr = min(128, rows - r0)
                c0 = 0
                while c0 < cols:
                    pc = min(2048, cols - c0)
                    gt = gtile[col % 2]
                    tr.dma("sp", lambda e, nm=nm, r0=r0, pr=pr, c0=c0, pc=pc, gt=gt:
                           e.dma_start(out=gt[0:pr, 0:pc],
                                       in_=ein[nm][r0:r0 + pr, c0:c0 + pc]),
                           w=[bb(gt)])
                    tr.dve(lambda e, gt=gt, pr=pr, pc=pc, col=col: e.tensor_reduce(
                        gacc[0:pr, col:col + 1], gt[0:pr, 0:pc],
                        mybir.AxisListType.X, ALU.add, apply_absolute_value=True),
                        r=[bb(gt)], w=[bb(gacc)])
                    col += 1
                    c0 += pc
                r0 += pr
            spans.append((start_col, col))
        for j, (a, bcol) in enumerate(spans):
            tr.dve(lambda e, a=a, bcol=bcol: e.tensor_reduce(
                gcol[:, 0:1], gacc[:, a:bcol], mybir.AxisListType.X, ALU.add),
                r=[bb(gacc)], w=[bb(gcol)])
            tr.pe(lambda e: e.matmul(psg[:], g["ones_f"][:], gcol[:, 0:1],
                                     start=True, stop=True),
                  r=[bb(gcol), bb(g["ones_f"])], w=[bb(psg)])
            tr.dve(lambda e, j=j: e.tensor_copy(g["gpart"][:, j:j + 1], psg[:]),
                   r=[bb(psg)], w=[bb(g["gpart"])])


def _stats_pipeline(nc, tr, g, n):
    bb = tr.buf
    m, r, t = g["row_m"], g["row_r"], g["row_t"]
    tr.dve(lambda e: e.tensor_scalar_mul(m[:, 0:n], m[:, 0:n], 1.0 / D),
           r=[], w=[bb(m)])
    tr.dve(lambda e: e.tensor_tensor(t[:, 0:n], m[:, 0:n], m[:, 0:n], ALU.mult),
           r=[bb(m)], w=[bb(t)])
    tr.dve(lambda e: e.scalar_tensor_tensor(
        r[:, 0:n], r[:, 0:n], 1.0 / D, t[:, 0:n], ALU.mult, ALU.subtract),
        r=[bb(t)], w=[bb(r)])
    tr.dve(lambda e: e.tensor_scalar_add(r[:, 0:n], r[:, 0:n], LN_EPS),
           r=[], w=[bb(r)])
    tr.act(lambda e: e.activation(t[:, 0:n], r[:, 0:n], AF.Ln),
           r=[bb(r)], w=[bb(t)])
    tr.dve(lambda e: e.tensor_scalar_mul(t[:, 0:n], t[:, 0:n], -0.5),
           r=[], w=[bb(t)])
    tr.act(lambda e: e.activation(r[:, 0:n], t[:, 0:n], AF.Exp),
           r=[bb(t)], w=[bb(r)])


def _p1(nc, tr, g, ein):
    bb = tr.buf
    xb, ln = g["xb"], g["ln"]
    with ExitStack() as px:
        E = px.enter_context
        inpj = E(nc.sbuf_tensor("p1inpj", [128, NDC * 128], BF16))
        memg = E(nc.sbuf_tensor("p1memg", [128, NDC * 128], BF16))
        shf = [E(nc.sbuf_tensor(f"p1s{i}", [128, 512], BF16)) for i in range(2)]
        acc = E(nc.sbuf_tensor("p1acc", [128, 512], F32))
        g_f = E(nc.sbuf_tensor("p1gf", [GM, 512], F32))
        sel = E(nc.sbuf_tensor("p1sel", [GM, 512], F32))
        tau = E(nc.sbuf_tensor("p1tau", [GM, 512], F32))
        aa = E(nc.sbuf_tensor("p1aa", [GM, 512], F32))
        mm_ = E(nc.sbuf_tensor("p1mm", [GM, 512], F32))
        kpre = E(nc.sbuf_tensor("p1kpre", [GM, 512], F32))
        k2 = E(nc.sbuf_tensor("p1k2", [GM, 512], F32))
        lns = aa
        itg = mm_
        kq = k2
        nr = E(nc.sbuf_tensor("p1nr", [1, 512], F32))
        pr_ = E(nc.psum_tensor("p1pr", [GM, 512], F32))
        pk = E(nc.psum_tensor("p1pk", [GM, 512], F32))
        pse = E(nc.psum_tensor("p1pse", [GM, 512], F32))
        pta = E(nc.psum_tensor("p1pta", [GM, 512], F32))
        pn2 = E(nc.psum_tensor("p1pn2", [1, 512], F32))
        pnb = E(nc.psum_tensor("p1pnb", [GM, 512], F32))

        tr.dma("g", lambda e: e.dma_start(
            out=inpj[:], in_=ein["inprojT"].rearrange("(i p) c -> p i c", p=128)),
            w=[bb(inpj)])
        tr.dma("g", lambda e: e.dma_start(
            out=memg[:], in_=ein["memgT"].rearrange("(i p) c -> p i c", p=128)),
            w=[bb(memg)])

        for tg in range(NTG):
            t0 = tg * 512
            for dc in range(NDC):
                sh = shf[dc % 2]
                tr.dve(lambda e, dc=dc, t0=t0: e.tensor_scalar_mul(
                    acc[:], ln[dc][:, t0:t0 + 512],
                    g["convw_s"][:, dc * 4:dc * 4 + 1]),
                    r=[bb(ln[dc]), bb(g["convw_s"])], w=[bb(acc)])
                for i in range(1, 4):
                    tr.dve(lambda e, dc=dc, t0=t0, i=i: e.scalar_tensor_tensor(
                        acc[:], ln[dc][:, t0 + i:t0 + i + 512],
                        g["convw_s"][:, dc * 4 + i:dc * 4 + i + 1], acc[:],
                        ALU.mult, ALU.add),
                        r=[bb(ln[dc]), bb(g["convw_s"])], w=[bb(acc)])
                tr.act(lambda e, dc=dc, sh=sh: e.activation(
                    sh[:], acc[:], AF.Silu, bias=g["convb_s"][:, dc:dc + 1]),
                    r=[bb(acc), bb(g["convb_s"])], w=[bb(sh)])
                tr.dve(lambda e, dc=dc, t0=t0, sh=sh: e.scalar_tensor_tensor(
                    xb[dc][:, t0:t0 + 512], sh[:],
                    g["resw_s"][:, dc:dc + 1], xb[dc][:, t0:t0 + 512],
                    ALU.mult, ALU.add),
                    r=[bb(sh), bb(g["resw_s"])], w=[bb(xb[dc])])
                st, sp_ = (dc == 0), (dc == NDC - 1)
                tr.pe(lambda e, dc=dc, sh=sh, st=st, sp_=sp_: e.matmul(
                    pr_[:], inpj[:, dc * 128:dc * 128 + GM], sh[:],
                    start=st, stop=sp_), r=[bb(inpj), bb(sh)], w=[bb(pr_)])
                tr.pe(lambda e, dc=dc, sh=sh, st=st, sp_=sp_: e.matmul(
                    pk[:], inpj[:, dc * 128 + GM:dc * 128 + 128], sh[:],
                    start=st, stop=sp_), r=[bb(inpj), bb(sh)], w=[bb(pk)])
                tr.pe(lambda e, dc=dc, sh=sh, st=st, sp_=sp_: e.matmul(
                    pse[:], memg[:, dc * 128:dc * 128 + GM], sh[:],
                    start=st, stop=sp_), r=[bb(memg), bb(sh)], w=[bb(pse)])
                tr.pe(lambda e, dc=dc, sh=sh, st=st, sp_=sp_: e.matmul(
                    pta[:], memg[:, dc * 128 + GM:dc * 128 + 128], sh[:],
                    start=st, stop=sp_), r=[bb(memg), bb(sh)], w=[bb(pta)])
            tr.dve(lambda e, t0=t0: e.tensor_copy(
                g["r_bf"][:, t0:t0 + 512], pr_[:]),
                r=[bb(pr_)], w=[bb(g["r_bf"])])
            tr.dve(lambda e: e.tensor_copy(kpre[:], pk[:]),
                   r=[bb(pk)], w=[bb(kpre)])
            tr.act(lambda e: e.activation(tau[:], pta[:], AF.Sigmoid,
                                          bias=g["memgb_s"][:, 1:2]),
                   r=[bb(pta), bb(g["memgb_s"])], w=[bb(tau)])
            tr.dve(lambda e: e.tensor_scalar_add(mm_[:], pse[:],
                                                 g["memgb_s"][:, 0:1]),
                   r=[bb(pse), bb(g["memgb_s"])], w=[bb(mm_)])
            tr.act(lambda e: e.activation(aa[:], mm_[:], AF.Abs),
                   r=[bb(mm_)], w=[bb(aa)])
            tr.act(lambda e: e.activation(aa[:], aa[:], AF.Exp, scale=-1.0),
                   r=[], w=[bb(aa)])
            tr.act(lambda e: e.activation(aa[:], aa[:], AF.Ln, bias=1.0),
                   r=[], w=[bb(aa)])
            tr.dve(lambda e: e.tensor_scalar_max(mm_[:], mm_[:], 0.0),
                   r=[], w=[bb(mm_)])
            tr.dve(lambda e: e.tensor_add(sel[:], mm_[:], aa[:]),
                   r=[bb(mm_), bb(aa)], w=[bb(sel)])
            tr.dve(lambda e: e.tensor_scalar_max(sel[:], sel[:], 1e-38),
                   r=[], w=[bb(sel)])
            tr.act(lambda e: e.activation(lns[:], sel[:], AF.Ln),
                   r=[bb(sel)], w=[bb(lns)])
            tr.dve(lambda e: e.tensor_mul(lns[:], tau[:], lns[:]),
                   r=[bb(tau)], w=[bb(lns)])
            tr.act(lambda e: e.activation(itg[:], lns[:], AF.Exp),
                   r=[bb(lns)], w=[bb(itg)])
            tr.dve(lambda e: e.scalar_tensor_tensor(
                g_f[:], sel[:], -1.0, tau[:],
                ALU.mult, ALU.mult),
                r=[bb(sel), bb(tau)], w=[bb(g_f)])
            for cc_ in range(4):
                c = tg * 4 + cc_
                tr.dve(lambda e, c=c, cc_=cc_: e.tensor_tensor_scan(
                    g["d_f"][:, c * CH:(c + 1) * CH], g["ones64"][:, 0:CH],
                    g_f[:, cc_ * CH:(cc_ + 1) * CH], 0.0, ALU.mult, ALU.add),
                    r=[bb(g["ones64"]), bb(g_f)], w=[bb(g["d_f"])])
            tr.act(lambda e: e.activation(k2[:], kpre[:], AF.Square),
                   r=[bb(kpre)], w=[bb(k2)])
            tr.pe(lambda e: e.matmul(pn2[:], g["ones_f"][0:GM, 0:1], k2[:],
                                     start=True, stop=True),
                  r=[bb(k2), bb(g["ones_f"])], w=[bb(pn2)])
            tr.act(lambda e: e.activation(nr[:], pn2[:], AF.Sqrt),
                   r=[bb(pn2)], w=[bb(nr)])
            tr.dve(lambda e: e.tensor_scalar_max(nr[:], nr[:], 1e-12),
                   r=[], w=[bb(nr)])
            tr.dve(lambda e: e.reciprocal(nr[:], nr[:]), r=[], w=[bb(nr)])
            tr.pe(lambda e: e.matmul(pnb[:], g["onesr_f"][0:1, 0:GM], nr[:],
                                     start=True, stop=True),
                  r=[bb(nr), bb(g["onesr_f"])], w=[bb(pnb)])
            tr.dve(lambda e: e.tensor_tensor(kq[:], kpre[:], pnb[:], ALU.mult),
                   r=[bb(kpre), bb(pnb)], w=[bb(kq)])
            tr.dve(lambda e, t0=t0: e.tensor_tensor(
                g["kh_bf"][:, t0:t0 + 512], kq[:], itg[:], ALU.mult),
                r=[bb(kq), bb(itg)], w=[bb(g["kh_bf"])])



def _p2(nc, tr, g, ein, u_dram):
    bb = tr.buf
    ln = g["ln"]
    with ExitStack() as px:
        E = px.enter_context
        chw0_sb = E(nc.sbuf_tensor("p2w0", [128, NDC * 128], BF16))
        chw1_sb = E(nc.sbuf_tensor("p2w1", [128, D], BF16))
        chb1_sb = E(nc.sbuf_tensor("p2b1", [1, D], BF16))
        h0 = E(nc.sbuf_tensor("p2h0", [128, TPC], BF16))
        lntm = E(nc.sbuf_tensor("p2lntm", [128, D], BF16))
        chs = E(nc.sbuf_tensor("p2chs", [128, 512], BF16))
        ut = E(nc.sbuf_tensor("p2ut", [128, D], BF16))
        ph = [E(nc.psum_tensor(f"p2ph{i}", [128, 512], F32)) for i in range(2)]
        ptr = [E(nc.psum_tensor(f"p2ptr{i}", [128, 128], BF16)) for i in range(2)]
        pch = [E(nc.psum_tensor(f"p2pch{i}", [128, 512], F32)) for i in range(2)]

        tr.dma("g", lambda e: e.dma_start(
            out=chw0_sb[:], in_=ein["chw0T"].rearrange("(i p) c -> p i c", p=128)),
            w=[bb(chw0_sb)])
        tr.dma("g", lambda e: e.dma_start(out=chw1_sb[:], in_=ein["chw1T"]),
               w=[bb(chw1_sb)])
        tr.dma("g", lambda e: e.dma_start(out=chb1_sb[:], in_=ein["chb1"]),
               w=[bb(chb1_sb)])

        for tg in range(NTG):
            p = ph[tg % 2]
            for dc in range(NDC):
                tr.pe(lambda e, dc=dc, tg=tg, p=p: e.matmul(
                    p[:], chw0_sb[:, dc * 128:(dc + 1) * 128],
                    ln[dc][:, HALO + tg * 512:HALO + (tg + 1) * 512],
                    start=(dc == 0), stop=(dc == NDC - 1)),
                    r=[bb(chw0_sb), bb(ln[dc])], w=[bb(p)])
            tr.dve(lambda e, tg=tg, p=p: e.tensor_copy(
                h0[:, tg * 512:(tg + 1) * 512], p[:]), r=[bb(p)], w=[bb(h0)])

        for tc in range(NCH):
            c0 = HALO + tc * 128
            for dc in range(NDC):
                pt = ptr[dc % 2]
                tr.pe(lambda e, dc=dc, c0=c0, pt=pt: e.transpose(
                    pt[:], ln[dc][:, c0:c0 + 128], g["ident_bf"][:]),
                    r=[bb(ln[dc]), bb(g["ident_bf"])], w=[bb(pt)])
                tr.dve(lambda e, dc=dc, pt=pt: e.tensor_copy(
                    lntm[:, dc * 128:(dc + 1) * 128], pt[:]),
                    r=[bb(pt)], w=[bb(lntm)])
            for dg in range(4):
                p = pch[dg % 2]
                tr.pe(lambda e, tc=tc, dg=dg, p=p: e.matmul(
                    p[:], h0[:, tc * 128:(tc + 1) * 128],
                    chw1_sb[:, dg * 512:(dg + 1) * 512],
                    start=True, stop=False),
                    r=[bb(h0), bb(chw1_sb)], w=[bb(p)])
                tr.pe(lambda e, dg=dg, p=p: e.matmul(
                    p[:], g["onesr_bf"][:], chb1_sb[0:1, dg * 512:(dg + 1) * 512],
                    start=False, stop=True),
                    r=[bb(g["onesr_bf"]), bb(chb1_sb)], w=[bb(p)])
                tr.act(lambda e, p=p: e.activation(chs[:], p[:], AF.Sigmoid),
                       r=[bb(p)], w=[bb(chs)])
                tr.dve(lambda e, dg=dg: e.tensor_tensor(
                    ut[:, dg * 512:(dg + 1) * 512], chs[:],
                    lntm[:, dg * 512:(dg + 1) * 512], ALU.mult),
                    r=[bb(chs), bb(lntm)], w=[bb(ut)])
            tr.dma("sp", lambda e, tc=tc: e.dma_start(
                out=u_dram[tc * 128:(tc + 1) * 128, :], in_=ut[:]),
                r=[bb(ut)], w=[bb(u_dram)])



def _p3(nc, tr, g, ein, u_dram, s_stage, s_gath):
    bb = tr.buf
    xb = g["xb"]
    with ExitStack() as px:
        E = px.enter_context
        Uc = [E(nc.sbuf_tensor(f"p3u{i}", [128, D], BF16)) for i in range(2)]
        ndB = [E(nc.sbuf_tensor(f"p3nd{i}", [GM, CH], F32)) for i in range(2)]
        e0B = [E(nc.sbuf_tensor(f"p3e0{i}", [GM, CH], BF16)) for i in range(2)]
        q0B = [E(nc.sbuf_tensor(f"p3q0{i}", [GM, CH], BF16)) for i in range(2)]
        eqB = [E(nc.sbuf_tensor(f"p3eq{i}", [GM, 32], BF16)) for i in range(2)]
        ekB = [E(nc.sbuf_tensor(f"p3ek{i}", [GM, CH], BF16)) for i in range(2)]
        qjB = [E(nc.sbuf_tensor(f"p3qj{i}", [GM, 32], BF16)) for i in range(2)]
        kjB = [E(nc.sbuf_tensor(f"p3kj{i}", [GM, CH], BF16)) for i in range(2)]
        esB = [E(nc.sbuf_tensor(f"p3es{i}", [GM, CH], BF16)) for i in range(2)]
        kSB = [E(nc.sbuf_tensor(f"p3kS{i}", [GM, CH], BF16)) for i in range(2)]
        kTB = [E(nc.sbuf_tensor(f"p3kT{i}", [128, GM], BF16)) for i in range(2)]
        PtB = [E(nc.sbuf_tensor(f"p3Pt{i}", [128, CH], BF16)) for i in range(2)]
        pPB = [E(nc.psum_tensor(f"p3pP{i}", [128, CH], F32)) for i in range(2)]
        pT = E(nc.psum_tensor("p3pT", [128, GM], BF16))
        pI = E(nc.psum_tensor("p3pI", [GM, 512], F32))
        po = [E(nc.psum_tensor(f"p3po{i}", [128, CH], F32)) for i in range(4)]

        tr.gx(lambda e: e.memset(g["S_f"][:], 0.0), w=[bb(g["S_f"])])
        dbuf = bb(g["d_f"])

        for c in range(NCH):
            u = Uc[c % 2]
            nd, e0, q0 = ndB[c % 2], e0B[c % 2], q0B[c % 2]
            eq, ek, qj, kj = eqB[c % 2], ekB[c % 2], qjB[c % 2], kjB[c % 2]
            es, kS, kT, Pt = esB[c % 2], kSB[c % 2], kTB[c % 2], PtB[c % 2]
            pP = pPB[c % 2]
            tr.dma("sp" if c % 2 == 0 else "g", lambda e, c=c, u=u: e.dma_start(
                out=u[:], in_=u_dram[c * 128:(c + 1) * 128, :]),
                r=[bb(u_dram)], w=[bb(u)])
            dsl = g["d_f"][:, c * CH:(c + 1) * CH]
            tr.dve(lambda e, dsl=dsl, nd=nd: e.tensor_scalar_mul(nd[:], dsl, -1.0),
                   r=[dbuf], w=[bb(nd)])
            tr.act(lambda e, c=c, dsl=dsl: e.activation(
                g["eD_f"][:, c:c + 1], dsl[:, CH - 1:CH], AF.Exp),
                r=[dbuf], w=[bb(g["eD_f"])])
            tr.dve(lambda e: e.tensor_copy(g["S_bf"][:], g["S_f"][:]),
                   r=[bb(g["S_f"])], w=[bb(g["S_bf"])])
            tr.act(lambda e, dsl=dsl, e0=e0: e.activation(e0[:], dsl, AF.Exp),
                   r=[dbuf], w=[bb(e0)])
            tr.dve(lambda e, c=c, q0=q0, e0=e0: e.tensor_tensor(
                q0[:], g["r_bf"][:, c * CH:(c + 1) * CH], e0[:], ALU.mult),
                r=[bb(g["r_bf"]), bb(e0)], w=[bb(q0)])
            tr.dve(lambda e, pP=pP: e.memset(pP[:], 0.0), w=[bb(pP)])
            for J in range(4):
                w_ = 32 * J + 32
                mid = 32 * J + 16
                tr.act(lambda e, dsl=dsl, J=J, mid=mid, w_=w_, eq=eq, nd=nd: e.activation(
                    eq[:], dsl[:, 32 * J:w_], AF.Exp,
                    bias=nd[:, mid:mid + 1]),
                    r=[dbuf, bb(nd)], w=[bb(eq)])
                tr.dve(lambda e, c=c, J=J, w_=w_, qj=qj, eq=eq: e.tensor_tensor(
                    qj[:], g["r_bf"][:, c * CH + 32 * J:c * CH + w_],
                    eq[:], ALU.mult), r=[bb(g["r_bf"]), bb(eq)], w=[bb(qj)])
                tr.act(lambda e, dsl=dsl, w_=w_, mid=mid, ek=ek, nd=nd: e.activation(
                    ek[:, 0:w_], nd[:, 0:w_], AF.Exp,
                    bias=dsl[:, mid:mid + 1]),
                    r=[dbuf, bb(nd)], w=[bb(ek)])
                tr.dve(lambda e, c=c, w_=w_, kj=kj, ek=ek: e.tensor_tensor(
                    kj[:, 0:w_], g["kh_bf"][:, c * CH:c * CH + w_],
                    ek[:, 0:w_], ALU.mult),
                    r=[bb(g["kh_bf"]), bb(ek)], w=[bb(kj)])
                tr.pe(lambda e, J=J, w_=w_, pP=pP, kj=kj, qj=qj: e.matmul(
                    pP[0:w_, 32 * J:w_], kj[0:GM, 0:w_], qj[:],
                    start=True, stop=True),
                    r=[bb(kj), bb(qj)], w=[bb(pP)])
            tr.dve(lambda e, Pt=Pt, pP=pP: e.tensor_tensor(Pt[:], pP[:], g["glam_f"][:],
                                             ALU.mult),
                   r=[bb(pP), bb(g["glam_f"])], w=[bb(Pt)])
            for w0 in range(0, NDC, 4):
                dcs = range(w0, min(w0 + 4, NDC))
                for i, dc in enumerate(dcs):
                    tr.pe(lambda e, dc=dc, i=i, u=u, Pt=Pt: e.matmul(
                        po[i][:], u[:, dc * 128:(dc + 1) * 128], Pt[:],
                        start=True, stop=False),
                        r=[bb(u), bb(Pt)], w=[bb(po[i])])
                    tr.pe(lambda e, dc=dc, i=i, q0=q0: e.matmul(
                        po[i][:], g["S_bf"][:, dc * 128:(dc + 1) * 128], q0[:],
                        start=False, stop=True),
                        r=[bb(g["S_bf"]), bb(q0)], w=[bb(po[i])])
                for i, dc in enumerate(dcs):
                    tr.dve(lambda e, dc=dc, i=i, c=c: e.scalar_tensor_tensor(
                        xb[dc][:, c * CH:(c + 1) * CH], po[i][:], SCALE_GLA,
                        xb[dc][:, c * CH:(c + 1) * CH], ALU.mult, ALU.add),
                        r=[bb(po[i])], w=[bb(xb[dc])])
            tr.act(lambda e, dsl=dsl, es=es, nd=nd: e.activation(
                es[:], nd[:], AF.Exp, bias=dsl[:, CH - 1:CH]),
                r=[dbuf, bb(nd)], w=[bb(es)])
            tr.dve(lambda e, c=c, kS=kS, es=es: e.tensor_tensor(
                kS[:], g["kh_bf"][:, c * CH:(c + 1) * CH], es[:], ALU.mult),
                r=[bb(g["kh_bf"]), bb(es)], w=[bb(kS)])
            tr.pe(lambda e, kS=kS: e.transpose(pT[:], kS[:], g["ident_bf"][0:GM, 0:GM]),
                  r=[bb(kS), bb(g["ident_bf"])], w=[bb(pT)])
            tr.dve(lambda e, kT=kT: e.tensor_copy(kT[:], pT[:]), r=[bb(pT)], w=[bb(kT)])
            for dg in range(4):
                tr.pe(lambda e, dg=dg, u=u, kT=kT: e.matmul(
                    pI[:], kT[:, 0:GM], u[:, dg * 512:(dg + 1) * 512],
                    start=True, stop=True),
                    r=[bb(kT), bb(u)], w=[bb(pI)])
                tr.dve(lambda e, c=c, dg=dg: e.scalar_tensor_tensor(
                    g["S_f"][:, dg * 512:(dg + 1) * 512],
                    g["S_f"][:, dg * 512:(dg + 1) * 512],
                    g["eD_f"][:, c:c + 1], pI[:], ALU.mult, ALU.add),
                    r=[bb(g["eD_f"]), bb(pI)], w=[bb(g["S_f"])])

        tr.dma("sp", lambda e: e.dma_start(out=s_stage[0:GM, :], in_=g["S_f"][:]),
               r=[bb(g["S_f"])], w=[bb(s_stage)])
        tr.dma("sp", lambda e: e.dma_start(
            out=s_stage[GM:GM + 1, 0:16], in_=g["gpart"][:]),
            r=[bb(g["gpart"])], w=[bb(s_stage)])
        tr.cc(lambda e: e.collective_compute(
            "AllGather", ALU.bypass,
            replica_groups=[[0, 1, 2, 3, 4, 5, 6, 7]],
            ins=[s_stage.opt()], outs=[s_gath.opt()]),
            r=[bb(s_stage)], w=[bb(s_gath)])


def _p4(nc, tr, g, ein, s_gath, w12qc, w3qc):
    bb = tr.buf
    xb = g["xb"]
    with ExitStack() as px:
        E = px.enter_context
        sg_t = E(nc.sbuf_tensor("p4sg", [GM, D], F32))
        e0 = E(nc.sbuf_tensor("p4e0", [GM, CH], BF16))
        q0 = E(nc.sbuf_tensor("p4q0", [GM, CH], BF16))
        q0a = E(nc.sbuf_tensor("p4q0a", [GM, CH], BF16))
        Acum = E(nc.sbuf_tensor("p4A", [GM, NCH + 1], F32))
        Acs = E(nc.sbuf_tensor("p4Ac", [GM, NCH + 1], F32))
        grows = E(nc.sbuf_tensor("p4grows", [1, 24], F32))
        gsum = E(nc.sbuf_tensor("p4gsum", [1, 12], F32))
        g["_dbg_grows"] = grows
        g["_dbg_gsum"] = gsum
        g["_dbg_sgath"] = s_gath
        po = [E(nc.psum_tensor(f"p4po{i}", [128, CH], F32)) for i in range(6)]
        pb = E(nc.psum_tensor("p4pb", [128, 1], F32))

        tr.gx(lambda e: e.memset(g["S_f"][:], 0.0), w=[bb(g["S_f"])])
        for r in range(8):
            tr.dma("sp", lambda e, r=r: e.dma_start(
                out=sg_t[:], in_=s_gath[r * (GM + 1):r * (GM + 1) + GM, :]),
                r=[bb(s_gath)], w=[bb(sg_t)])
            tr.dve(lambda e, r=r: e.scalar_tensor_tensor(
                g["S_f"][:], sg_t[:], g["smask_sb"][:, r:r + 1], g["S_f"][:],
                ALU.mult, ALU.add),
                r=[bb(sg_t), bb(g["smask_sb"])], w=[bb(g["S_f"])])
        tr.dve(lambda e: e.tensor_copy(g["S_bf"][:], g["S_f"][:]),
               r=[bb(g["S_f"])], w=[bb(g["S_bf"])])
        tr.gx(lambda e: e.memset(Acum[:, 0:NCH + 1], 1.0), w=[bb(Acum)])
        tr.gx(lambda e: e.memset(Acs[:, 0:NCH + 1], 1.0), w=[bb(Acs)])
        for c in range(NCH):
            tr.dve(lambda e, c=c: e.tensor_tensor(
                Acum[:, c + 1:c + 2], Acs[:, c:c + 1], g["eD_f"][:, c:c + 1],
                ALU.mult), r=[bb(g["eD_f"]), bb(Acs)], w=[bb(Acum)])
            tr.act(lambda e, c=c: e.copy(Acs[:, c + 1:c + 2],
                                         Acum[:, c + 1:c + 2]),
                   r=[bb(Acum)], w=[bb(Acs)])

        dbuf = bb(g["d_f"])
        for c in range(NCH):
            dsl = g["d_f"][:, c * CH:(c + 1) * CH]
            tr.act(lambda e, dsl=dsl: e.activation(e0[:], dsl, AF.Exp),
                   r=[dbuf], w=[bb(e0)])
            tr.dve(lambda e, c=c: e.tensor_tensor(
                q0[:], g["r_bf"][:, c * CH:(c + 1) * CH], e0[:], ALU.mult),
                r=[bb(g["r_bf"]), bb(e0)], w=[bb(q0)])
            tr.dve(lambda e, c=c: e.tensor_scalar_mul(
                q0a[:], q0[:], Acs[:, c:c + 1]),
                r=[bb(q0), bb(Acs)], w=[bb(q0a)])
            for w0 in range(0, NDC, 6):
                dcs = range(w0, min(w0 + 6, NDC))
                for i, dc in enumerate(dcs):
                    tr.pe(lambda e, dc=dc, i=i: e.matmul(
                        po[i][:], g["S_bf"][:, dc * 128:(dc + 1) * 128], q0a[:],
                        start=True, stop=True),
                        r=[bb(g["S_bf"]), bb(q0a)], w=[bb(po[i])])
                for i, dc in enumerate(dcs):
                    tr.dve(lambda e, dc=dc, i=i, c=c: e.scalar_tensor_tensor(
                        xb[dc][:, c * CH:(c + 1) * CH], po[i][:], SCALE_GLA,
                        xb[dc][:, c * CH:(c + 1) * CH], ALU.mult, ALU.add),
                        r=[bb(po[i])], w=[bb(xb[dc])])

        for r in range(8):
            tr.dma("sp", lambda e, r=r: e.dma_start(
                out=grows[0:1, r * 3:(r + 1) * 3],
                in_=s_gath[r * (GM + 1) + GM:r * (GM + 1) + GM + 1, 0:3]),
                r=[bb(s_gath)], w=[bb(grows)])
        RW = dict(r=[bb(grows), bb(gsum)], w=[bb(grows), bb(gsum)])
        for p_ in range(4):
            tr.dve(lambda e, p_=p_: e.tensor_tensor(
                grows[0:1, p_ * 3:p_ * 3 + 3], grows[0:1, 2 * p_ * 3:2 * p_ * 3 + 3],
                grows[0:1, (2 * p_ + 1) * 3:(2 * p_ + 1) * 3 + 3], ALU.add),
                **RW)
        tr.act(lambda e: e.copy(gsum[:, 0:12], grows[0:1, 0:12]), **RW)
        for p_ in range(2):
            tr.dve(lambda e, p_=p_: e.tensor_tensor(
                grows[0:1, p_ * 3:p_ * 3 + 3], gsum[0:1, 2 * p_ * 3:2 * p_ * 3 + 3],
                gsum[0:1, (2 * p_ + 1) * 3:(2 * p_ + 1) * 3 + 3], ALU.add),
                **RW)
        tr.act(lambda e: e.copy(gsum[:, 0:6], grows[0:1, 0:6]), **RW)
        tr.dve(lambda e: e.tensor_tensor(
            grows[0:1, 0:3], gsum[0:1, 0:3], gsum[0:1, 3:6], ALU.add), **RW)
        tr.act(lambda e: e.copy(gsum[:, 0:3], grows[0:1, 0:3]), **RW)
        for j in range(3):
            tr.dve(lambda e, j=j: e.tensor_scalar(
                gsum[:, j:j + 1], gsum[:, j:j + 1], 0.5 / (FH * D), 0.5e-8,
                ALU.mult, ALU.add), **RW)
        tr.act(lambda e: e.copy(gsum[:, 4:7], gsum[:, 0:3]), **RW)
        for j in range(3):
            tr.pe(lambda e, j=j: e.matmul(
                pb[:], g["onesr_f"][:], gsum[0:1, 4 + j:5 + j], start=True,
                stop=True), r=[bb(gsum), bb(g["onesr_f"])], w=[bb(pb)])
            tr.dve(lambda e, j=j: e.tensor_copy(g["thr_s"][:, j:j + 1], pb[:]),
                   r=[bb(pb)], w=[bb(g["thr_s"])])
            tr.act(lambda e, j=j: e.mul(
                g["nthr_s"][:, j:j + 1], g["thr_s"][:, j:j + 1], -1.0),
                r=[bb(g["thr_s"])], w=[bb(g["nthr_s"])])



def _p4q(nc, tr, g, ein, w12qc, w3qc):
    bb = tr.buf
    with ExitStack() as px:
        E = px.enter_context
        wf_t = [E(nc.sbuf_tensor(f"p4wf{i}", [128, 2048], F32)) for i in range(3)]
        bl_t = [E(nc.sbuf_tensor(f"p4bl{i}", [128, 2048], BF16)) for i in range(2)]
        wq_t = [E(nc.sbuf_tensor(f"p4wq{i}", [128, 2048], BF16)) for i in range(3)]

        qjobs = []
        for hc in range(NHC):
            qjobs.append((ein["w14"][hc * D:(hc + 1) * D, :],
                          w12qc[hc * 2 * D:hc * 2 * D + D, :], w12qc, 0, 2048))
            qjobs.append((ein["w24"][hc * D:(hc + 1) * D, :],
                          w12qc[hc * 2 * D + D:(hc + 1) * 2 * D, :], w12qc, 1, 2048))
        for dc in range(NDC):
            for (o_, n_) in ((0, 2048), (2048, 2048), (4096, 1536)):
                r0 = dc * FH + o_
                qjobs.append((ein["w34"][r0:r0 + n_, :],
                              w3qc[r0:r0 + n_, :], w3qc, 2, n_))
        for i, (src, dst, track, j, rows) in enumerate(qjobs):
            rows = rows
            wf = wf_t[i % 3]
            bl = bl_t[i % 2]
            wq = wq_t[i % 3]
            nc_ = rows
            em = tr.dve
            tr.dma("sp" if i % 2 == 0 else "g",
                   lambda e, src=src, wf=wf, nc_=nc_: e.dma_start(
                       out=wf[:, 0:nc_],
                       in_=src.rearrange("(i p) c -> p i c", p=128)),
                   w=[bb(wf)])
            em(lambda e, wf=wf, bl=bl, nc_=nc_, j=j: e.tensor_scalar(
                bl[:, 0:nc_], wf[:, 0:nc_], g["nthr_s"][:, j:j + 1], None,
                ALU.is_lt),
                r=[bb(wf), bb(g["nthr_s"])], w=[bb(bl)])
            em(lambda e, wf=wf, bl=bl, wq=wq, nc_=nc_, j=j: e.scalar_tensor_tensor(
                wq[:, 0:nc_], wf[:, 0:nc_], g["thr_s"][:, j:j + 1], bl[:, 0:nc_],
                ALU.is_gt, ALU.subtract),
                r=[bb(wf), bb(bl), bb(g["thr_s"])], w=[bb(wq)])
            tr.dma("g" if i % 2 == 0 else "sp",
                   lambda e, dst=dst, wq=wq, nc_=nc_: e.dma_start(
                       out=dst.rearrange("(i p) c -> p i c", p=128),
                       in_=wq[:, 0:nc_]),
                   r=[bb(wq)], w=[bb(track)])



_LN_N = [0]


def _ln_x(nc, tr, g):
    bb = tr.buf
    xb, ln = g["xb"], g["ln"]
    _LN_N[0] += 1
    u_ = f"_{_LN_N[0]}"
    with ExitStack() as px:
        E = px.enter_context
        sq = [E(nc.sbuf_tensor(f"lnsq{i}{u_}", [128, 512], BF16)) for i in range(2)]
        scr = E(nc.sbuf_tensor("lnscr" + u_, [128, 512], F32))
        g["row_t"] = E(nc.sbuf_tensor("lnrowt" + u_, [1, TPC], F32))
        psx = E(nc.psum_tensor("lnpsx" + u_, [1, 512], F32))
        psq = E(nc.psum_tensor("lnpsq" + u_, [1, 512], F32))
        psb = [E(nc.psum_tensor(f"lnpsb{i}{u_}", [128, 512], F32)) for i in range(2)]

        for tg in range(NTG):
            t0 = tg * 512
            for dc in range(NDC):
                s = sq[dc % 2]
                tr.act(lambda e, dc=dc, t0=t0, s=s: e.activation(
                    s[:], xb[dc][:, t0:t0 + 512], AF.Square),
                    r=[bb(xb[dc])], w=[bb(s)])
                tr.pe(lambda e, dc=dc, t0=t0: e.matmul(
                    psx[:], g["ones_bf"][:], xb[dc][:, t0:t0 + 512],
                    start=(dc == 0), stop=(dc == NDC - 1)),
                    r=[bb(xb[dc]), bb(g["ones_bf"])], w=[bb(psx)])
                tr.pe(lambda e, s=s, dc=dc: e.matmul(
                    psq[:], g["ones_bf"][:], s[:],
                    start=(dc == 0), stop=(dc == NDC - 1)),
                    r=[bb(s), bb(g["ones_bf"])], w=[bb(psq)])
            tr.dve(lambda e, t0=t0: e.tensor_copy(
                g["row_m"][:, t0:t0 + 512], psx[:]),
                r=[bb(psx)], w=[bb(g["row_m"])])
            tr.dve(lambda e, t0=t0: e.tensor_copy(
                g["row_r"][:, t0:t0 + 512], psq[:]),
                r=[bb(psq)], w=[bb(g["row_r"])])
        _stats_pipeline(nc, tr, g, n=TPC)
        for tg in range(NTG):
            t0 = tg * 512
            tr.pe(lambda e, t0=t0: e.matmul(
                psb[0][:], g["onesr_f"][:], g["row_m"][0:1, t0:t0 + 512],
                start=True, stop=True),
                r=[bb(g["row_m"]), bb(g["onesr_f"])], w=[bb(psb[0])])
            tr.pe(lambda e, t0=t0: e.matmul(
                psb[1][:], g["onesr_f"][:], g["row_r"][0:1, t0:t0 + 512],
                start=True, stop=True),
                r=[bb(g["row_r"]), bb(g["onesr_f"])], w=[bb(psb[1])])
            for dc in range(NDC):
                tr.dve(lambda e, dc=dc, t0=t0: e.tensor_tensor(
                    scr[:], xb[dc][:, t0:t0 + 512], psb[0][:], ALU.subtract),
                    r=[bb(xb[dc]), bb(psb[0])], w=[bb(scr)])
                tr.dve(lambda e, dc=dc, t0=t0: e.tensor_tensor(
                    ln[dc][:, t0:t0 + 512], scr[:], psb[1][:], ALU.mult),
                    r=[bb(scr), bb(psb[1])], w=[bb(ln[dc])])



_FFN_N = [0]


def _p_ffn(nc, tr, g, w12qc, w3qc, y):
    bb = tr.buf
    xb, ln = g["xb"], g["ln"]
    _FFN_N[0] += 1
    u_ = f"_{_FFN_N[0]}"
    with ExitStack() as px:
        E = px.enter_context
        hid = [E(nc.sbuf_tensor(f"fh{i}{u_}", [128, 512], BF16)) for i in range(HHC)]
        wq12 = [E(nc.sbuf_tensor(f"fwq12{i}{u_}", [128, 4096], BF16)) for i in range(2)]
        wq3 = [E(nc.sbuf_tensor(f"fwq3{i}{u_}", [128, HHC * 128], BF16))
               for i in range(3)]
        sg = E(nc.sbuf_tensor("fsg" + u_, [128, 512], BF16))
        outf = E(nc.sbuf_tensor("foutf" + u_, [128, 512], F32))
        psA = [E(nc.psum_tensor(f"fpsA{i}{u_}", [128, 512], F32)) for i in range(2)]
        psB = [E(nc.psum_tensor(f"fpsB{i}{u_}", [128, 512], F32)) for i in range(2)]
        ps3 = [E(nc.psum_tensor(f"fps3{i}{u_}", [128, 512], F32)) for i in range(2)]

        for tt in range(NTG):
            t0 = tt * 512
            for hh in range(2):
                for i in range(HHC):
                    hc = hh * HHC + i
                    b12 = wq12[i % 2]
                    b1 = b12[:, 0:2048]
                    b2 = b12[:, 2048:4096]
                    tr.dma("sp" if i % 2 == 0 else "g",
                           lambda e, hc=hc, b12=b12: e.dma_start(
                               out=b12[:], in_=w12qc[hc * 2 * D:(hc + 1) * 2 * D, :]
                               .rearrange("(i p) c -> p i c", p=128)),
                           r=[bb(w12qc)], w=[bb(b12)])
                    pA, pB = psA[i % 2], psB[i % 2]
                    b12b = bb(wq12[i % 2])
                    for ic in range(NDC):
                        tr.pe(lambda e, ic=ic, t0=t0, b1=b1, pA=pA: e.matmul(
                            pA[:], b1[:, ic * 128:(ic + 1) * 128],
                            ln[ic][:, t0:t0 + 512],
                            start=(ic == 0), stop=(ic == NDC - 1)),
                            r=[b12b, bb(ln[ic])], w=[bb(pA)])
                    for ic in range(NDC):
                        tr.pe(lambda e, ic=ic, t0=t0, b2=b2, pB=pB: e.matmul(
                            pB[:], b2[:, ic * 128:(ic + 1) * 128],
                            ln[ic][:, t0:t0 + 512],
                            start=(ic == 0), stop=(ic == NDC - 1)),
                            r=[b12b, bb(ln[ic])], w=[bb(pB)])
                    tr.act(lambda e, pA=pA: e.activation(sg[:], pA[:], AF.Sigmoid),
                           r=[bb(pA)], w=[bb(sg)])
                    tr.dve(lambda e, i=i, pB=pB: e.tensor_tensor(
                        hid[i][:], sg[:], pB[:], ALU.mult),
                        r=[bb(sg), bb(pB)], w=[bb(hid[i])])
                for dc in range(NDC):
                    b3 = wq3[dc % 3]
                    r0 = dc * FH + hh * (HHC * 128)
                    tr.dma("sp" if dc % 2 == 0 else "g",
                           lambda e, r0=r0, b3=b3: e.dma_start(
                               out=b3[:], in_=w3qc[r0:r0 + HHC * 128, :]
                               .rearrange("(i p) c -> p i c", p=128)),
                           r=[bb(w3qc)], w=[bb(b3)])
                    p3 = ps3[dc % 2]
                    for i in range(HHC):
                        tr.pe(lambda e, i=i, b3=b3, p3=p3: e.matmul(
                            p3[:], b3[:, i * 128:(i + 1) * 128], hid[i][:],
                            start=(i == 0), stop=(i == HHC - 1)),
                            r=[bb(b3), bb(hid[i])], w=[bb(p3)])
                    if y is None or hh == 0:
                        tr.dve(lambda e, dc=dc, t0=t0, p3=p3: e.scalar_tensor_tensor(
                            xb[dc][:, t0:t0 + 512], p3[:], 1.0,
                            xb[dc][:, t0:t0 + 512], ALU.mult, ALU.add),
                            r=[bb(p3)], w=[bb(xb[dc])])
                    else:
                        tr.dve(lambda e, dc=dc, t0=t0, p3=p3: e.scalar_tensor_tensor(
                            outf[:], p3[:], 1.0, xb[dc][:, t0:t0 + 512],
                            ALU.mult, ALU.add),
                            r=[bb(p3), bb(xb[dc])], w=[bb(outf)])
                        tr.dma("sp", lambda e, dc=dc, t0=t0: e.dma_start(
                            out=y[dc * 128:(dc + 1) * 128, t0:t0 + 512],
                            in_=outf[:]), r=[bb(outf)], w=[bb(y)])


def _p6(nc, tr, g, ein, q_dram, kv_stage, kv_gath):
    bb = tr.buf
    ln = g["ln"]
    with ExitStack() as px:
        E = px.enter_context
        wt = [E(nc.sbuf_tensor(f"p6w{i}", [128, NDC * 512], BF16)) for i in range(2)]
        cosc = E(nc.sbuf_tensor("p6cos", [128, NCH * 256], BF16))
        sinc = E(nc.sbuf_tensor("p6sin", [128, NCH * 256], BF16))
        brow = [E(nc.sbuf_tensor(f"p6brow{i}", [1, 512], BF16)) for i in range(2)]
        qk = E(nc.sbuf_tensor("p6qk", [128, 512], BF16))
        rq = E(nc.sbuf_tensor("p6rq", [128, 512], BF16))
        tA = E(nc.sbuf_tensor("p6tA", [128, 256], BF16))
        tB = E(nc.sbuf_tensor("p6tB", [128, 256], BF16))
        ksq = E(nc.sbuf_tensor("p6ksq", [128, 512], BF16))
        nscr = E(nc.sbuf_tensor("p6nscr", [128, 1], F32))
        nstore = E(nc.sbuf_tensor("p6nst", [128, 512], F32))
        nsum = E(nc.sbuf_tensor("p6nsum", [128, 32], F32))
        ntr = E(nc.sbuf_tensor("p6ntr", [32, 1], F32))
        nflat = E(nc.sbuf_tensor("p6nflat", [1, 32], F32))
        ex2 = E(nc.sbuf_tensor("p6ex2", [1, 2], F32))
        ps = [E(nc.psum_tensor(f"p6ps{i}", [128, 512], F32)) for i in range(2)]
        pnt = E(nc.psum_tensor("p6pnt", [32, 128], F32))

        tr.dma("g", lambda e: e.dma_start(
            out=cosc[:], in_=ein["cosP"].rearrange("(tc p) c -> p tc c", p=128)),
            w=[bb(cosc)])
        tr.dma("g", lambda e: e.dma_start(
            out=sinc[:], in_=ein["sinP"].rearrange("(tc p) c -> p tc c", p=128)),
            w=[bb(sinc)])
        tr.gx(lambda e: e.memset(nsum[:], 0.0), w=[bb(nsum)])
        tr.gx(lambda e: e.memset(nstore[:], 0.0), w=[bb(nstore)])

        for ng in range(12):
            w_ = wt[ng % 2]
            br = brow[ng % 2]
            if ng < 8:
                src = ein["wqkT"]
                co = ng * 512
                bsrc = ein["bqkrow"][0:1, ng * 512:(ng + 1) * 512]
            else:
                src = ein["wvT"]
                co = (ng - 8) * 512
                bsrc = ein["bvrow"][0:1, co:co + 512]
            tr.dma("g", lambda e, bsrc=bsrc, br=br: e.dma_start(
                out=br[:], in_=bsrc), w=[bb(br)])
            for ic in range(NDC):
                tr.dma("g", lambda e, ic=ic, co=co, src=src, w_=w_: e.dma_start(
                    out=w_[:, ic * 512:(ic + 1) * 512],
                    in_=src[ic * 128:(ic + 1) * 128, co:co + 512]), w=[bb(w_)])
            for tc in range(NCH):
                p = ps[tc % 2]
                for ic in range(NDC):
                    tr.pe(lambda e, ic=ic, tc=tc, p=p, w_=w_: e.matmul(
                        p[:], ln[ic][:, tc * 128:(tc + 1) * 128],
                        w_[:, ic * 512:(ic + 1) * 512],
                        start=(ic == 0), stop=False),
                        r=[bb(ln[ic]), bb(w_)], w=[bb(p)])
                tr.pe(lambda e, p=p, br=br: e.matmul(
                    p[:], g["onesr_bf"][:], br[0:1, :],
                    start=False, stop=True),
                    r=[bb(g["onesr_bf"]), bb(br)], w=[bb(p)])
                tr.dve(lambda e, p=p: e.tensor_copy(qk[:], p[:]),
                       r=[bb(p)], w=[bb(qk)])
                if ng >= 8:
                    tr.dma("sp", lambda e, tc=tc, co=co: e.dma_start(
                        out=kv_stage[TPC + tc * 128:TPC + (tc + 1) * 128,
                                     co:co + 512], in_=qk[:]),
                        r=[bb(qk)], w=[bb(kv_stage)])
                    continue
                cs4 = cosc[:, tc * 256:(tc + 1) * 256].rearrange(
                    "p (h c) -> p h c", c=GM)
                sn4 = sinc[:, tc * 256:(tc + 1) * 256].rearrange(
                    "p (h c) -> p h c", c=GM)
                tA4 = tA[:].rearrange("p (h c) -> p h c", c=GM)
                tB4 = tB[:].rearrange("p (h c) -> p h c", c=GM)
                qk4 = qk[:].rearrange("p (h c) -> p h c", c=128)
                rq4 = rq[:].rearrange("p (h c) -> p h c", c=128)
                qe, qo = qk4[:, :, 0:GM], qk4[:, :, GM:128]
                re_, ro = rq4[:, :, 0:GM], rq4[:, :, GM:128]
                csb, snb = bb(cosc), bb(sinc)
                tr.dve(lambda e, qe=qe, cs4=cs4, tA4=tA4: e.tensor_tensor(
                    tA4, qe, cs4, ALU.mult), r=[bb(qk), csb], w=[bb(tA)])
                tr.dve(lambda e, qo=qo, sn4=sn4, tB4=tB4: e.tensor_tensor(
                    tB4, qo, sn4, ALU.mult), r=[bb(qk), snb], w=[bb(tB)])
                tr.dve(lambda e, re_=re_, tA4=tA4, tB4=tB4: e.tensor_tensor(
                    re_, tA4, tB4, ALU.subtract),
                    r=[bb(tA), bb(tB)], w=[bb(rq)])
                tr.dve(lambda e, qe=qe, sn4=sn4, tA4=tA4: e.tensor_tensor(
                    tA4, qe, sn4, ALU.mult), r=[bb(qk), snb], w=[bb(tA)])
                tr.dve(lambda e, qo=qo, cs4=cs4, tB4=tB4: e.tensor_tensor(
                    tB4, qo, cs4, ALU.mult), r=[bb(qk), csb], w=[bb(tB)])
                tr.dve(lambda e, ro=ro, tA4=tA4, tB4=tB4: e.tensor_tensor(
                    ro, tA4, tB4, ALU.add),
                    r=[bb(tA), bb(tB)], w=[bb(rq)])
                if ng < 4:
                    tr.dma("sp", lambda e, tc=tc, ng=ng: e.dma_start(
                        out=q_dram[tc * 128:(tc + 1) * 128,
                                   ng * 512:(ng + 1) * 512], in_=rq[:]),
                        r=[bb(rq)], w=[bb(q_dram)])
                else:
                    tr.dma("sp", lambda e, tc=tc, ng=ng: e.dma_start(
                        out=kv_stage[tc * 128:(tc + 1) * 128,
                                     (ng - 4) * 512:(ng - 3) * 512], in_=rq[:]),
                        r=[bb(rq)], w=[bb(kv_stage)])
                tr.act(lambda e: e.activation(ksq[:], rq[:], AF.Square),
                       r=[bb(rq)], w=[bb(ksq)])
                for hh in range(4):
                    colid = ng * 4 + hh
                    tr.dve(lambda e, hh=hh, tc=tc, colid=colid: e.tensor_reduce(
                        nstore[:, tc * 32 + colid:tc * 32 + colid + 1],
                        ksq[:, hh * 128:(hh + 1) * 128],
                        mybir.AxisListType.X, ALU.add),
                        r=[bb(ksq)], w=[bb(nstore)])

        for colid in range(32):
            tr.dve(lambda e, colid=colid: e.tensor_reduce(
                nsum[:, colid:colid + 1],
                nstore[:, colid:512:32], mybir.AxisListType.X, ALU.max),
                r=[bb(nstore)], w=[bb(nsum)])
        ident_f = px.enter_context(nc.sbuf_tensor("p6idf", [128, 128], F32))
        ndram = nc.dram_tensor("p6ndram", [32, 1], F32).ap()
        tr.dma("sp", lambda e: e.dma_start(out=ident_f[:], in_=ein["ident"]),
               w=[bb(ident_f)])
        tr.pe(lambda e: e.transpose(pnt[:], nsum[:], ident_f[:]),
              r=[bb(nsum), bb(ident_f)], w=[bb(pnt)])
        tr.dve(lambda e: e.tensor_reduce(ntr[:, 0:1], pnt[:],
                                         mybir.AxisListType.X, ALU.max),
               r=[bb(pnt)], w=[bb(ntr)])
        tr.dma("g", lambda e: e.dma_start(out=ndram, in_=ntr[:]),
               r=[bb(ntr)], w=[bb(ndram)])
        tr.dma("g", lambda e: e.dma_start(
            out=nflat[:], in_=ndram.rearrange("r c -> c r")),
            r=[bb(ndram)], w=[bb(nflat)])
        tr.dve(lambda e: e.tensor_reduce(ex2[:, 0:1], nflat[0:1, 0:16],
                                         mybir.AxisListType.X, ALU.max),
               r=[bb(nflat)], w=[bb(ex2)])
        tr.dve(lambda e: e.tensor_reduce(ex2[:, 1:2], nflat[0:1, 16:32],
                                         mybir.AxisListType.X, ALU.max),
               r=[bb(nflat)], w=[bb(ex2)])
        tr.act(lambda e: e.mul(ex2[:, 0:2], ex2[:, 0:2], 1.05),
               r=[bb(ex2)], w=[bb(ex2)])
        tr.dma("g", lambda e: e.dma_start(
            out=kv_stage[2 * TPC:2 * TPC + 1, 0:2], in_=ex2[:]),
            r=[bb(ex2)], w=[bb(kv_stage)])
        tr.cc(lambda e: e.collective_compute(
            "AllGather", ALU.bypass,
            replica_groups=[[0, 1], [2, 3], [4, 5], [6, 7]],
            ins=[kv_stage.opt()], outs=[kv_gath.opt()]),
            r=[bb(kv_stage)], w=[bb(kv_gath)])



def _p7(nc, tr, g, ein, kv_stage, kv_gath, kbuf1, vbuf1, q_dram):
    bb = tr.buf
    xb = g["xb"]
    av = g["ln"]
    with ExitStack() as px:
        E = px.enter_context
        tri_bf = E(nc.sbuf_tensor("p7tri", [128, 4 * 512], BF16))
        mrow = E(nc.sbuf_tensor("p7mrow", [1, 4], BF16))
        mser = E(nc.sbuf_tensor("p7mser", [1, 4], F32))
        mser2 = E(nc.sbuf_tensor("p7mser2", [1, 4], F32))
        negM = E(nc.sbuf_tensor("p7negM", [128, 1], F32))
        bias_all = E(nc.sbuf_tensor("p7bias", [128, NSLOT], F32))
        KsbB = [E(nc.sbuf_tensor(f"p7K{i}", [128, 2 * TPC], BF16)) for i in range(2)]
        VsbB = [E(nc.sbuf_tensor(f"p7V{i}", [128, 2 * TPC], BF16)) for i in range(2)]
        QsbB = [E(nc.sbuf_tensor(f"p7Q{i}", [128, TPC], BF16)) for i in range(2)]
        Pt = [E(nc.sbuf_tensor(f"p7P{i}", [128, 512], BF16)) for i in range(4)]
        srow = E(nc.sbuf_tensor("p7srow", [1, 512], F32))
        rbsb = E(nc.sbuf_tensor("p7rb", [128, 512], F32))
        wo = [E(nc.sbuf_tensor(f"p7wo{i}", [128, 2048], BF16)) for i in range(2)]
        psS = [E(nc.psum_tensor(f"p7psS{i}", [128, 512], F32)) for i in range(2)]
        psAVB = [E(nc.psum_tensor(f"p7psAV{i}", [128, 512], F32)) for i in range(2)]
        psSumB = [E(nc.psum_tensor(f"p7psSum{i}", [1, 512], F32)) for i in range(2)]
        psR = E(nc.psum_tensor("p7psR", [128, 512], F32))
        pso = [E(nc.psum_tensor("p7pso0", [128, 512], F32))]

        for i in range(4):
            tr.dma("g", lambda e, i=i: e.dma_start(
                out=tri_bf[:, i * 512:(i + 1) * 512],
                in_=ein["triband"][i * 128:(i + 1) * 128, :]), w=[bb(tri_bf)])

        bnc = KsbB[1]
        for grp in range(16):
            lo = (grp % 2) * D
            tr.dma("g", lambda e, grp=grp, lo=lo: e.indirect_dma_start(
                bnc[:, lo:lo + D], None, kv_gath, IndirectOffsetOnAxis(
                    g["idxk_sb"][:, grp:grp + 1], 0)),
                r=[bb(kv_gath), bb(g["idxk_sb"])], w=[bb(bnc)])
            tr.dma("g", lambda e, grp=grp, lo=lo: e.dma_start(
                out=kbuf1[grp * 128:(grp + 1) * 128, :], in_=bnc[:, lo:lo + D]),
                r=[bb(bnc)], w=[bb(kbuf1)])
        for grp in range(16):
            lo = (grp % 2) * D
            tr.dma("g", lambda e, grp=grp, lo=lo: e.indirect_dma_start(
                bnc[:, lo:lo + D], None, kv_gath, IndirectOffsetOnAxis(
                    g["idxv_sb"][:, grp:grp + 1], 0)),
                r=[bb(kv_gath), bb(g["idxv_sb"])], w=[bb(bnc)])
            tr.dma("g", lambda e, grp=grp, lo=lo: e.dma_start(
                out=vbuf1[grp * 128:(grp + 1) * 128, :], in_=bnc[:, lo:lo + D]),
                r=[bb(bnc)], w=[bb(vbuf1)])

        for r in range(2):
            tr.dma("sp", lambda e, r=r: e.dma_start(
                out=mrow[0:1, r * 2:(r + 1) * 2],
                in_=kv_gath[r * KV_ROWS + 2 * TPC:r * KV_ROWS + 2 * TPC + 1, 0:2]),
                r=[bb(kv_gath)], w=[bb(mrow)])
        tr.dve(lambda e: e.tensor_reduce(mser[:, 0:1], mrow[0:1, 0:4:2],
                                         mybir.AxisListType.X, ALU.max),
               r=[bb(mrow)], w=[bb(mser)])
        tr.dve(lambda e: e.tensor_reduce(mser[:, 1:2], mrow[0:1, 1:4:2],
                                         mybir.AxisListType.X, ALU.max),
               r=[bb(mrow)], w=[bb(mser)])
        tr.act(lambda e: e.copy(mser2[:, 0:2], mser[:, 0:2]),
               r=[bb(mser)], w=[bb(mser2)])
        tr.dve(lambda e: e.tensor_tensor(mser[:, 2:3], mser2[:, 0:1],
                                         mser2[:, 1:2], ALU.mult),
               r=[bb(mser2)], w=[bb(mser)])
        tr.act(lambda e: e.activation(mser[:, 3:4], mser[:, 2:3], AF.Sqrt),
               r=[], w=[bb(mser)])
        tr.dve(lambda e: e.tensor_scalar_mul(mser[:, 3:4], mser[:, 3:4], -1.0),
               r=[], w=[bb(mser)])
        tr.pe(lambda e: e.matmul(psR[:, 0:1], g["onesr_f"][:], mser[0:1, 3:4],
                                 start=True, stop=True),
              r=[bb(mser), bb(g["onesr_f"])], w=[bb(psR)])
        tr.dve(lambda e: e.tensor_copy(negM[:], psR[:, 0:1]),
               r=[bb(psR)], w=[bb(negM)])
        tr.act(lambda e: e.activation(bias_all[:], g["moff_sb"][:],
                                      AF.Identity, bias=negM[:, 0:1]),
               r=[bb(g["moff_sb"]), bb(negM)], w=[bb(bias_all)])

        pend = [None]

        def finalize(fin):
            h, gq, psAV, psSum = fin
            tr.dve(lambda e, psSum=psSum: e.reciprocal(srow[:], psSum[:]),
                   r=[bb(psSum)], w=[bb(srow)])
            tr.pe(lambda e: e.matmul(psR[:], g["onesr_f"][:], srow[:],
                                     start=True, stop=True),
                  r=[bb(srow), bb(g["onesr_f"])], w=[bb(psR)])
            tr.act(lambda e: e.copy(rbsb[:], psR[:]),
                   r=[bb(psR)], w=[bb(rbsb)])
            tr.dve(lambda e, h=h, gq=gq, psAV=psAV: e.tensor_tensor(
                av[h][:, gq * 512:(gq + 1) * 512], psAV[:], rbsb[:],
                ALU.mult), r=[bb(psAV), bb(rbsb)], w=[bb(av[h])])

        for h in range(NH):
            Ksb, Vsb, Qsb = KsbB[h % 2], VsbB[h % 2], QsbB[h % 2]
            q1, q2 = "sp", "g"
            for q4 in range(4):
                tr.dma(q1, lambda e, h=h, q4=q4, Ksb=Ksb: e.dma_start(
                    out=Ksb[:, TPC + q4 * 512:TPC + (q4 + 1) * 512],
                    in_=kv_stage[q4 * 512:(q4 + 1) * 512, h * 128:(h + 1) * 128],
                    transpose=True), r=[bb(kv_stage)], w=[bb(Ksb)])
                tr.dma(q1, lambda e, h=h, q4=q4, Qsb=Qsb: e.dma_start(
                    out=Qsb[:, q4 * 512:(q4 + 1) * 512],
                    in_=q_dram[q4 * 512:(q4 + 1) * 512, h * 128:(h + 1) * 128],
                    transpose=True), r=[bb(q_dram)], w=[bb(Qsb)])
                tr.dma(q2, lambda e, h=h, q4=q4, Vsb=Vsb: e.dma_start(
                    out=Vsb[:, TPC + q4 * 512:TPC + (q4 + 1) * 512],
                    in_=kv_stage[TPC + q4 * 512:TPC + (q4 + 1) * 512,
                                 h * 128:(h + 1) * 128]
                    .rearrange("(kt p) c -> p kt c", p=128)),
                    r=[bb(kv_stage)], w=[bb(Vsb)])
            for q4 in range(4):
                tr.dma(q1, lambda e, h=h, q4=q4, Ksb=Ksb: e.dma_start(
                    out=Ksb[:, q4 * 512:(q4 + 1) * 512],
                    in_=kbuf1[q4 * 512:(q4 + 1) * 512, h * 128:(h + 1) * 128],
                    transpose=True), r=[bb(kbuf1)], w=[bb(Ksb)])
                tr.dma(q2, lambda e, h=h, q4=q4, Vsb=Vsb: e.dma_start(
                    out=Vsb[:, q4 * 512:(q4 + 1) * 512],
                    in_=vbuf1[q4 * 512:(q4 + 1) * 512, h * 128:(h + 1) * 128]
                    .rearrange("(kt p) c -> p kt c", p=128)),
                    r=[bb(vbuf1)], w=[bb(Vsb)])
            slot_base = 0
            for gq in range(4):
                nkt = 20 + 4 * gq
                psAV = psAVB[gq % 2]
                psSum = psSumB[gq % 2]
                order = list(range(16, nkt)) + list(range(16))
                stage = [None]

                def emit_av(pp):
                    kt, P, first, last, psAV, psSum, Vsb = pp
                    tr.pe(lambda e, kt=kt, P=P, first=first, last=last,
                          psAV=psAV, Vsb=Vsb: e.matmul(
                        psAV[:], Vsb[:, kt * 128:(kt + 1) * 128], P[:],
                        start=first, stop=last),
                        r=[bb(Vsb), bb(P)], w=[bb(psAV)])
                    tr.pe(lambda e, P=P, first=first, last=last,
                          psSum=psSum: e.matmul(
                        psSum[:], g["ones_bf"][:], P[:],
                        start=first, stop=last),
                        r=[bb(g["ones_bf"]), bb(P)], w=[bb(psSum)])

                for oi, kt in enumerate(order):
                    slot = slot_base + kt
                    first = (oi == 0)
                    last = (oi == nkt - 1)
                    pS = psS[oi % 2]
                    P = Pt[oi % 4]
                    tr.pe(lambda e, kt=kt, gq=gq, pS=pS, Ksb=Ksb, Qsb=Qsb: e.matmul(
                        pS[:], Ksb[:, kt * 128:(kt + 1) * 128],
                        Qsb[:, gq * 512:(gq + 1) * 512],
                        start=True, stop=True),
                        r=[bb(Ksb), bb(Qsb)], w=[bb(pS)])
                    tr.act(lambda e, pS=pS, P=P, slot=slot: e.activation(
                        P[:], pS[:], AF.Exp, bias=bias_all[:, slot:slot + 1]),
                        r=[bb(pS), bb(bias_all)], w=[bb(P)])
                    if kt >= 16 + 4 * gq:
                        ti = kt - 16 - 4 * gq
                        tr.dve(lambda e, P=P, ti=ti: e.tensor_tensor(
                            P[:], P[:], tri_bf[:, ti * 512:(ti + 1) * 512],
                            ALU.mult), r=[bb(tri_bf)], w=[bb(P)])
                    if stage[0] is not None:
                        emit_av(stage[0])
                    stage[0] = (kt, P, first, last, psAV, psSum, Vsb)
                    if oi == 4 and pend[0] is not None:
                        finalize(pend[0])
                        pend[0] = None
                emit_av(stage[0])
                slot_base += nkt
                pend[0] = (h, gq, psAV, psSum)
        if pend[0] is not None:
            finalize(pend[0])
            pend[0] = None

        for oc in range(NDC):
            w_ = wo[oc % 2]
            tr.dma("g", lambda e, oc=oc, w_=w_: e.dma_start(
                out=w_[:], in_=ein["wout4"][oc * D:(oc + 1) * D, :]
                .rearrange("(i p) c -> p i c", p=128)), w=[bb(w_)])
            for tg in range(NTG):
                p = pso[0]
                for ic in range(NDC):
                    tr.pe(lambda e, ic=ic, tg=tg, w_=w_, p=p: e.matmul(
                        p[:], w_[:, ic * 128:(ic + 1) * 128],
                        av[ic][:, tg * 512:(tg + 1) * 512],
                        start=(ic == 0), stop=(ic == NDC - 1)),
                        r=[bb(w_), bb(av[ic])], w=[bb(p)])
                tr.dve(lambda e, oc=oc, tg=tg, p=p: e.scalar_tensor_tensor(
                    xb[oc][:, tg * 512:(tg + 1) * 512], p[:],
                    g["bout_s"][:, oc:oc + 1],
                    xb[oc][:, tg * 512:(tg + 1) * 512], ALU.add, ALU.add),
                    r=[bb(p), bb(g["bout_s"])], w=[bb(xb[oc])])



def kernel(**inputs):
    in_maps = _host_inputs(inputs)
    nc = build()
    res = run_bass_kernel_spmd(nc, in_maps, core_ids=list(range(8)))
    out = np.empty((B, T, D), np.float32)
    for c in range(8):
        b, hlf = c // 2, c % 2
        out[b, hlf * TPC:(hlf + 1) * TPC, :] = np.asarray(res.results[c]["y"]).T
    return out
